# version 07
# baseline (speedup 1.0000x reference)
import numpy as np
from contextlib import ExitStack

import concourse.bass as bass
import concourse.mybir as mybir
from concourse.bass_utils import run_bass_kernel_spmd
from concourse.alu_op_type import AluOpType as OP

F32 = mybir.dt.float32
BF16 = mybir.dt.bfloat16
F8 = mybir.dt.float8e4
AF = mybir.ActivationFunctionType
AX = mybir.AxisListType

MAGIC = float(np.float32(1.5 * 2 ** 23))
HALF_PI = float(np.float32(np.pi / 2))
INV_2PI = float(np.float32(1.0 / (2 * np.pi)))
C1 = 6.28125
C2 = float(np.float32(2 * np.pi - 6.28125))
PI_CLAMP = float(np.nextafter(np.float32(np.pi), np.float32(0)))
EPS = 1e-5

T = 2048
D = 1024
HID = 4096
NH = 16
HD = 64
NSC = 4
SC = 512
FT = 256
NFT = 8
FTE = FT + 2
GROUPS = [[0, 1, 2, 3], [4, 5, 6, 7]]

DEBUG = False
DEBUG_SET = ()


def build_nc():
    nc = bass.Bass("TRN2", target_bir_lowering=False)

    x_h = nc.dram_tensor("x", [D, T], F32, kind="ExternalInput").ap()
    wq_h = nc.dram_tensor("wq", [D, D], F32, kind="ExternalInput").ap()
    wk_h = nc.dram_tensor("wk", [D, D], F32, kind="ExternalInput").ap()
    wv_h = nc.dram_tensor("wv", [D, D], F32, kind="ExternalInput").ap()
    wo_h = nc.dram_tensor("wo", [D, D], F32, kind="ExternalInput").ap()
    g1_h = nc.dram_tensor("g1w", [D, HD], F32, kind="ExternalInput").ap()
    g2_h = nc.dram_tensor("g2w", [HD, D], F32, kind="ExternalInput").ap()
    w1t_h = nc.dram_tensor("w1t", [D, HID], F32, kind="ExternalInput").ap()
    w2t_h = nc.dram_tensor("w2t", [HID, D], F32, kind="ExternalInput").ap()
    theta_h = nc.dram_tensor("theta_pc", [128, 8], F32, kind="ExternalInput").ap()
    gamr_h = nc.dram_tensor("gamma_row", [1, D], F32, kind="ExternalInput").ap()
    betr_h = nc.dram_tensor("beta_row", [1, D], F32, kind="ExternalInput").ap()
    cw_h = [nc.dram_tensor(f"cw{j}_pc", [128, 32], F32, kind="ExternalInput").ap()
            for j in range(3)]
    cb_h = nc.dram_tensor("cb_pc", [128, 32], F32, kind="ExternalInput").ap()
    t0c_h = nc.dram_tensor("t0col", [128, 1], F32, kind="ExternalInput").ap()
    trimask_h = nc.dram_tensor("trimask", [128, 128], F32, kind="ExternalInput").ap()
    ident_h = nc.dram_tensor("ident", [128, 128], F32, kind="ExternalInput").ap()
    onescol_h = nc.dram_tensor("ones_col", [128, 1], F32, kind="ExternalInput").ap()
    onesrow_h = nc.dram_tensor("ones_row", [1, 128], F32, kind="ExternalInput").ap()
    ones512_h = nc.dram_tensor("ones512", [1, 512], F32, kind="ExternalInput").ap()
    pmask_h = nc.dram_tensor("pmask", [128, 4], F32, kind="ExternalInput").ap()
    hmL_h = nc.dram_tensor("hmaskL", [128, 4], F32, kind="ExternalInput").ap()
    hmR_h = nc.dram_tensor("hmaskR", [128, 4], F32, kind="ExternalInput").ap()
    out_h = nc.dram_tensor("out", [D, T], F32, kind="ExternalOutput").ap()
    dbg = {}
    if DEBUG:
        want = []
        if "x2" in DEBUG_SET:
            want += [("x2", [D, T])]
        if "tail" in DEBUG_SET or "oall" in DEBUG_SET:
            want += [("Oall", [128, 8 * T]), ("rows", [1, 2 * T])]
        if "tail" in DEBUG_SET or "sbf" in DEBUG_SET:
            want += [("sbf", [128, NH * HD])]
        if "ffn" in DEBUG_SET:
            want += [("xq0", [128, 8 * FTE]), ("h0", [128, 32 * FTE]),
                     ("y0", [128, 32 * FTE]), ("xq2_0", [128, 32 * FTE])]
        for nm, shape in want:
            dbg[nm] = nc.dram_tensor("dbg_" + nm, shape, F32,
                                     kind="ExternalOutput").ap()

    qprime_h = nc.dram_tensor("qprime", [NSC * NH * 128, SC], BF16).ap()
    x2_h = nc.dram_tensor("x2scr", [D, T], F32).ap()
    cc1_in = nc.dram_tensor("cc1_in", [NH * 128, HD], F32).ap()
    cc1_out = nc.dram_tensor("cc1_out", [4 * NH * 128, HD], F32).ap()
    cc2_in = nc.dram_tensor("cc2_in", [128, 20], F32).ap()
    cc2_out = nc.dram_tensor("cc2_out", [4 * 128, 20], F32).ap()

    st = ExitStack()
    with st:
        def sb(name, shape, dt):
            return st.enter_context(nc.sbuf_tensor("g_" + name, shape, dt))

        theta_pc = sb("theta_pc", [128, 8], F32)
        gamma_row_bf = sb("gamma_row_bf", [1, D], BF16)
        beta_row_bf = sb("beta_row_bf", [1, D], BF16)
        cw_pc = [sb(f"cw{j}_pc", [128, 32], F32) for j in range(3)]
        cb_pc = sb("cb_pc", [128, 32], F32)
        t0col = sb("t0col", [128, 1], F32)
        trimask_bf = sb("trimask_bf", [128, 128], BF16)
        ident_bf = sb("ident_bf", [128, 128], BF16)
        ident_f = sb("ident_f", [128, 128], F32)
        onescol_bf = sb("onescol_bf", [128, 1], BF16)
        onescol_f = sb("onescol_f", [128, 1], F32)
        onesrow_f = sb("onesrow_f", [1, 128], F32)
        onesrow_bf = sb("onesrow_bf", [1, 128], BF16)
        ones512_bf = sb("ones512_bf", [1, 512], BF16)
        pmask = sb("pmask", [128, 4], F32)
        hmL = sb("hmL", [128, 4], F32)
        hmR = sb("hmR", [128, 4], F32)
        wacc = sb("wacc", [128, 8], F32)
        eps_col = sb("eps_col", [128, 1], F32)
        halfpi_col = sb("halfpi_col", [128, 1], F32)
        magic_col = sb("magic_col", [128, 1], F32)
        mean1s = sb("mean1s", [128, 1], F32)
        mean2s = sb("mean2s", [128, 1], F32)
        sw1s = sb("sw1s", [128, 1], F32)
        sw2s = sb("sw2s", [128, 1], F32)
        halo_xqL = sb("halo_xqL", [128, 8], F32)
        halo_xqR = sb("halo_xqR", [128, 8], F32)
        halo_rcL = sb("halo_rcL", [128, 1], F32)
        halo_rcR = sb("halo_rcR", [128, 1], F32)
        stage2 = sb("stage2", [128, 20], F32)
        stage4 = sb("stage4", [128, 80], F32)
        rowa = sb("rowa", [1, 512], F32)
        rowb = sb("rowb", [1, 512], F32)
        rowc = sb("rowc", [1, 1024], F32)

        pb = st.enter_context(nc.psum_tensor("pb", [128, 4096], F32))

        def bank(i, lo=0, hi=512):
            return pb[:, i * 512 + lo: i * 512 + hi]

        semnames = ["CL", "GC", "GW", "GW2", "IOT", "WS0", "WS1", "XLA", "XLB", "QS0", "QS1", "AGD", "DBG", "QL0", "QL1", "X2S", "XF0", "XF1", "OS0", "OS1", "DBGG", "TP", "VP", "SP", "CC", "PC"]
        S = {n: st.enter_context(nc.semaphore(n)) for n in semnames}
        C = {n: 0 for n in semnames}
        block = st.enter_context(nc.Block())

        class _Px:
            def __init__(self, e):
                self._e = e
                self._last = None

            def __getattr__(self, n):
                f = getattr(self._e, n)
                if callable(f):
                    def g(*a, **k):
                        r = f(*a, **k)
                        if hasattr(r, "then_inc"):
                            self._last = r
                        return r
                    return g
                return f

            def fin(self, sem, n=1):
                self._last.then_inc(sem, n)

        def TE(f): block.tensor(lambda e: f(_Px(e)))
        def VE(f): block.vector(lambda e: f(_Px(e)))
        def SE(f): block.scalar(lambda e: f(_Px(e)))
        def SY(f): block.sync(lambda e: f(_Px(e)))
        def GP(f): block.gpsimd(lambda e: f(_Px(e)))

        def inc(name, n=1):
            C[name] += n
            return C[name]

        def body(sy):
            for dst, src in [
                (theta_pc[:, :], theta_h), (cb_pc[:, :], cb_h),
                (t0col[:, :], t0c_h), (pmask[:, :], pmask_h),
                (hmL[:, :], hmL_h), (hmR[:, :], hmR_h),
                (onesrow_f[:, :], onesrow_h), (onescol_f[:, :], onescol_h),
                (ident_f[:, :], ident_h),
                (cw_pc[0][:, :], cw_h[0]), (cw_pc[1][:, :], cw_h[1]),
                (cw_pc[2][:, :], cw_h[2]),
            ]:
                sy.dma_start(out=dst, in_=src).then_inc(S["CL"], 16)
                inc("CL", 16)
        SY(body)

        def body(g):
            for dst, src in [
                (trimask_bf[:, :], trimask_h), (ident_bf[:, :], ident_h),
                (onescol_bf[:, :], onescol_h), (onesrow_bf[:, :], onesrow_h),
                (gamma_row_bf[:, :], gamr_h), (beta_row_bf[:, :], betr_h),
                (ones512_bf[:, :], ones512_h),
            ]:
                g.dma_start(out=dst, in_=src).then_inc(S["GC"], 16)
                inc("GC", 16)
        GP(body)
        consts_CL, consts_GC = C["CL"], C["GC"]

        def body(v):
            v.wait_ge(S["GC"], consts_GC)
            v.wait_ge(S["CL"], consts_CL)
            v.memset(stage2[:, :], 0.0)
            v.memset(eps_col[:, :], EPS)
            v.memset(halfpi_col[:, :], HALF_PI)
            v.memset(magic_col[:, :], MAGIC)
            v.fin(S["VP"], 1)
        VE(body)
        constcol_VP = inc("VP", 1)

        st1 = ExitStack()
        with st1:
            def sb1(name, shape, dt):
                return st1.enter_context(nc.sbuf_tensor("p1_" + name, shape, dt))

            O_all = sb1("O_all", [128, 8 * T], BF16)
            m_rowF = sb1("m_rowF", [1, T], BF16)
            rstd_rowF = sb1("rstd_rowF", [1, T], BF16)
            Sf0 = sb1("Sf0", [128, NH * HD], F32)
            Sbf = sb1("Sbf", [128, NH * HD], BF16)
            Spref = sb1("Spref", [128, NH * HD], BF16)
            x_tile = sb1("x_tile", [128, 8 * SC], F32)
            xbf = sb1("xbf", [128, 8 * SC], BF16)
            scr = sb1("scr", [128, SC], F32)
            scr2 = sb1("scr2", [128, SC], F32)
            rstdb = sb1("rstdb", [128, SC], BF16)
            nmrow_bf = sb1("nmrow_bf", [1, SC], BF16)
            rrow_bf = sb1("rrow_bf", [1, SC], BF16)

            st1a = ExitStack()
            with st1a:
                def sba(name, shape, dt):
                    return st1a.enter_context(nc.sbuf_tensor("a_" + name, shape, dt))

                wq_sb = sba("wq_sb", [128, 8 * D], BF16)
                wk_sb = sba("wk_sb", [128, 8 * D], BF16)
                wv_sb = sba("wv_sb", [128, 8 * D], BF16)
                g1w_sb = sba("g1w_sb", [128, 8 * HD], BF16)
                colsum_q = sba("colsum_q", [1, D], BF16)
                colsum_k = sba("colsum_k", [1, D], BF16)
                colsum_v = sba("colsum_v", [1, D], BF16)
                xsq = sba("xsq", [128, 2 * SC], BF16)
                vsb = sba("vsb", [128, 4 * D], BF16)
                qsb = sba("qsb", [128, SC], BF16)
                ksb = sba("ksb", [128, SC], BF16)
                QPa = sba("QPa", [128, 2 * SC], BF16)
                QPb = sba("QPb", [128, 2 * SC], BF16)
                KPa = sba("KPa", [128, 2 * SC], BF16)
                KPb = sba("KPb", [128, 2 * SC], BF16)
                KtmT = sba("KtmT", [128, SC], BF16)
                Et_sb = sba("Et_sb", [128, 4 * SC], BF16)
                cos_t = sba("cos_t", [128, 8 * SC], BF16)
                sin_t = sba("sin_t", [128, 8 * SC], BF16)
                tb0 = sba("tb0", [128, SC], F32)
                tb1 = sba("tb1", [128, SC], F32)
                tb2 = sba("tb2", [128, SC], F32)
                posb = sba("posb", [128, SC], F32)
                iota_t = sba("iota_t", [128, SC], F32)
                rstd_colT = sba("rstd_colT", [128, 4], F32)
                wstage = sba("wstage", [128, 2 * 512], F32)
                wred = sba("wred", [128, 2], F32)

                def body(g):
                    for dst, src, cdim in ((wq_sb, wq_h, 8), (wk_sb, wk_h, 8),
                                           (wv_sb, wv_h, 8), (g1w_sb, g1_h, 8)):
                        g.dma_start(
                            out=dst[:, :].rearrange("p (c n) -> p c n", c=cdim),
                            in_=src.rearrange("(c p) n -> p c n", p=128)
                        ).then_inc(S["GW"], 16)
                        inc("GW", 16)
                GP(body)
                w_GD = C["GW"]

                def body(g):
                    g.iota(iota_t[:, :], [[1, SC]], base=0, channel_multiplier=0,
                           allow_small_or_imprecise_dtypes=True).then_inc(S["IOT"], 1)
                GP(body)
                iota_GD = inc("IOT", 1)

                def body(v):
                    v.memset(Sf0[:, :], 0.0)
                    v.memset(Sbf[:, :], 0.0)
                    v.fin(S["VP"], 1)
                    inc("VP")
                VE(body)
                sbf_VP = C["VP"]

                slab_VP = []
                for wi, wh in ((0, w1t_h), (1, w2t_h)):
                    src = wh.rearrange("(c p) n -> p c n", p=128)
                    wcols = wh.shape[1]
                    nslab = (wh.shape[0] // 128) * wcols // 512
                    for sl in range(nslab):
                        half = sl % 2
                        blk = (sl * 512) // wcols
                        off = (sl * 512) % wcols
                        need_vp = slab_VP[-2] if len(slab_VP) >= 2 else 0

                        def body(sy, half=half, blk=blk, off=off, src=src,
                                 need_vp=need_vp):
                            if need_vp:
                                sy.wait_ge(S["VP"], need_vp)
                            sy.dma_start(
                                out=wstage[:, half * 512:(half + 1) * 512],
                                in_=src[:, blk, off: off + 512]
                            ).then_inc(S["WS%d" % half], 16)
                        SY(body)
                        ld = inc("WS%d" % half, 16)

                        def body(v, half=half, wi=wi, ld=ld, first=(sl == 0),
                                 pacc=(slab_VP[-1] if slab_VP else 0)):
                            v.wait_ge(S["WS%d" % half], ld)
                            if pacc:
                                v.wait_ge(S["VP"], pacc)
                            v.tensor_reduce(wred[:, wi:wi + 1],
                                            wstage[:, half * 512:(half + 1) * 512],
                                            AX.X, OP.add, apply_absolute_value=True
                                            ).then_inc(S["VP"], 1)
                        VE(body)
                        vp_r = inc("VP", 1)

                        def body(v, wi=wi, vp_r=vp_r, first=(sl == 0)):
                            v.wait_ge(S["VP"], vp_r)
                            if first:
                                v.tensor_copy(wacc[:, wi:wi + 1], wred[:, wi:wi + 1]
                                              ).then_inc(S["VP"], 1)
                            else:
                                v.tensor_tensor(wacc[:, wi:wi + 1], wacc[:, wi:wi + 1],
                                                wred[:, wi:wi + 1], OP.add
                                                ).then_inc(S["VP"], 1)
                        VE(body)
                        slab_VP.append(inc("VP", 1))
                wstat_VP = C["VP"]

                def body(t):
                    t.wait_ge(S["VP"], wstat_VP)
                    t.wait_ge(S["CL"], consts_CL)
                    t.matmul(bank(7, 0, 2)[0:1, :], onescol_f[:, :], wacc[:, 0:2],
                             start=True, stop=True).then_inc(S["TP"], 1)
                TE(body)
                wsum_TP = inc("TP", 1)

                def body(v):
                    v.wait_ge(S["TP"], wsum_TP)
                    v.tensor_scalar(rowa[0:1, 0:2], bank(7, 0, 2)[0:1, :],
                                    1.0 / (D * HID), 1e-5, OP.mult, OP.max
                                    ).then_inc(S["VP"], 1)
                VE(body)
                vmean = inc("VP", 1)

                def body(v):
                    v.wait_ge(S["VP"], vmean)
                    v.reciprocal(rowa[0:1, 2:4], rowa[0:1, 0:2]).then_inc(S["VP"], 1)
                VE(body)
                vrec = inc("VP", 1)

                def body(t):
                    t.wait_ge(S["VP"], vrec)
                    t.matmul(bank(7, 16, 17), onesrow_f[:, :], rowa[0:1, 0:1],
                             start=True, stop=True)
                    t.matmul(bank(7, 17, 18), onesrow_f[:, :], rowa[0:1, 1:2],
                             start=True, stop=True, skip_group_check=True)
                    t.matmul(bank(7, 18, 19), onesrow_f[:, :], rowa[0:1, 2:3],
                             start=True, stop=True, skip_group_check=True)
                    t.matmul(bank(7, 19, 20), onesrow_f[:, :], rowa[0:1, 3:4],
                             start=True, stop=True, skip_group_check=True
                             ).then_inc(S["TP"], 1)
                TE(body)
                wbc_TP = inc("TP", 1)

                def body(v):
                    v.wait_ge(S["TP"], wbc_TP)
                    v.tensor_copy(mean1s[:, :], bank(7, 16, 17))
                    v.tensor_copy(mean2s[:, :], bank(7, 17, 18))
                    v.tensor_copy(sw1s[:, :], bank(7, 18, 19))
                    v.tensor_copy(sw2s[:, :], bank(7, 19, 20))
                    v.fin(S["VP"], 1)
                VE(body)
                wsc_VP = inc("VP", 1)

                prev_cs_VP = 0
                for wsb, width, dst in ((wq_sb, D, colsum_q), (wk_sb, D, colsum_k),
                                        (wv_sb, D, colsum_v)):
                    for hf in range(2):
                        w0, w1 = hf * 512, hf * 512 + 512

                        def body(t, wsb=wsb, width=width, w0=w0, w1=w1,
                                 pcs=prev_cs_VP):
                            t.wait_ge(S["GW"], w_GD)
                            t.wait_ge(S["GC"], consts_GC)
                            t.wait_ge(S["VP"], max(wsc_VP, pcs))
                            for k in range(8):
                                t.matmul(bank(6)[0:1, :], onescol_bf[:, :],
                                         wsb[:, k * width + w0: k * width + w1],
                                         start=(k == 0), stop=(k == 7))
                            t.fin(S["TP"], 1)
                        TE(body)
                        tp = inc("TP", 1)

                        def body(v, dst=dst, w0=w0, w1=w1, tp=tp):
                            v.wait_ge(S["TP"], tp)
                            v.tensor_copy(dst[0:1, w0:w1], bank(6)[0:1, :]
                                          ).then_inc(S["VP"], 1)
                        VE(body)
                        prev_cs_VP = inc("VP", 1)
                colsum_VP = C["VP"]

                qp_ST = [0, 0]
                prev_pair = None
                last_oevac_SP = 0
                prev_stat_TP = 0
                prev_tab_SP = 0
                prev_sd_parity = [0, 0]
                for s in range(NSC):

                    def body(sy, s=s, pst=prev_stat_TP):
                        if pst:
                            sy.wait_ge(S["TP"], pst)
                        sy.dma_start(
                            out=x_tile[:, :].rearrange("p (c n) -> p c n", c=8),
                            in_=x_h[:, s * SC:(s + 1) * SC].rearrange(
                                "(c p) n -> p c n", p=128)
                        ).then_inc(S["XLA"], 16)
                    SY(body)
                    x_LD = inc("XLA", 16)

                    def body(v, x_LD=x_LD):
                        v.wait_ge(S["XLA"], x_LD)
                        for c in range(8):
                            v.tensor_copy(xbf[:, c * SC:(c + 1) * SC],
                                          x_tile[:, c * SC:(c + 1) * SC])
                        v.fin(S["VP"], 1)
                    VE(body)
                    xbf_VP = inc("VP", 1)

                    sq_SPs = []
                    sq_TPs = []
                    for c in range(8):
                        def body(g, c=c, x_LD=x_LD,
                                 ptp=(sq_TPs[-2] if len(sq_TPs) >= 2 else 0)):
                            g.wait_ge(S["XLA"], x_LD)
                            if ptp:
                                g.wait_ge(S["TP"], ptp)
                            g.tensor_tensor(xsq[:, (c % 2) * SC:(c % 2 + 1) * SC],
                                            x_tile[:, c * SC:(c + 1) * SC],
                                            x_tile[:, c * SC:(c + 1) * SC],
                                            OP.mult).then_inc(S["PC"], 1)
                        GP(body)
                        sq_SPs.append(inc("PC", 1))

                        def body(t, c=c, xbf_VP=xbf_VP, sp=sq_SPs[-1]):
                            t.wait_ge(S["VP"], max(xbf_VP, colsum_VP))
                            t.wait_ge(S["PC"], sp)
                            t.matmul(bank(0)[0:1, :], onescol_bf[:, :],
                                     xbf[:, c * SC:(c + 1) * SC],
                                     start=(c == 0), stop=(c == 7))
                            t.matmul(bank(1)[0:1, :], onescol_bf[:, :],
                                     xsq[:, (c % 2) * SC:(c % 2 + 1) * SC],
                                     start=(c == 0), stop=(c == 7))
                            t.fin(S["TP"], 1)
                        TE(body)
                        sq_TPs.append(inc("TP", 1))
                    stat_TP = sq_TPs[-1]
                    prev_stat_TP = stat_TP

                    def body(v, stat_TP=stat_TP, s=s):
                        v.wait_ge(S["TP"], stat_TP)
                        v.tensor_scalar(m_rowF[0:1, s * SC:(s + 1) * SC],
                                        bank(0)[0:1, :], 1.0 / D, None, OP.mult)
                        v.tensor_scalar(rowa[0:1, 0:SC], bank(1)[0:1, :],
                                        1.0 / D, None, OP.mult)
                        v.fin(S["VP"], 1)
                    VE(body)
                    v1 = inc("VP", 1)

                    def body(v, v1=v1, s=s):
                        v.wait_ge(S["VP"], v1)
                        m_sl = m_rowF[0:1, s * SC:(s + 1) * SC]
                        v.tensor_tensor(rowb[0:1, 0:SC], m_sl, m_sl, OP.mult
                                        ).then_inc(S["VP"], 1)
                    VE(body)
                    v2 = inc("VP", 1)

                    def body(v, v2=v2):
                        v.wait_ge(S["VP"], v2)
                        v.tensor_tensor(rowb[0:1, 0:SC], rowa[0:1, 0:SC],
                                        rowb[0:1, 0:SC], OP.subtract
                                        ).then_inc(S["VP"], 1)
                    VE(body)
                    v3 = inc("VP", 1)

                    def body(sca, v3=v3, s=s, sA=C["SP"] + 1):
                        sca.wait_ge(S["VP"], v3)
                        sca.activation(rowc[0:1, 0:SC], rowb[0:1, 0:SC], AF.Ln,
                                       bias=eps_col[0:1, :]).then_inc(S["SP"], 1)
                        sca.wait_ge(S["SP"], sA)
                        sca.activation(rstd_rowF[0:1, s * SC:(s + 1) * SC],
                                       rowc[0:1, 0:SC], AF.Exp, scale=-0.5
                                       ).then_inc(S["SP"], 1)
                    SE(body)
                    sA = inc("SP", 1)
                    rstd_SP = inc("SP", 1)

                    def body(v, rstd_SP=rstd_SP, s=s):
                        v.wait_ge(S["SP"], rstd_SP)
                        m_sl = m_rowF[0:1, s * SC:(s + 1) * SC]
                        r_sl = rstd_rowF[0:1, s * SC:(s + 1) * SC]
                        v.scalar_tensor_tensor(rowc[0:1, 0:SC], m_sl, -1.0, r_sl,
                                               OP.mult, OP.mult).then_inc(S["VP"], 1)
                    VE(body)
                    v4 = inc("VP", 1)

                    def body(v, v4=v4, s=s):
                        v.wait_ge(S["VP"], v4)
                        v.tensor_copy(nmrow_bf[0:1, :], rowc[0:1, 0:SC])
                        v.tensor_copy(rrow_bf[0:1, :],
                                      rstd_rowF[0:1, s * SC:(s + 1) * SC])
                        v.fin(S["VP"], 1)
                    VE(body)
                    rows_VP = inc("VP", 1)

                    def body(t, rows_VP=rows_VP, s=s):
                        t.wait_ge(S["VP"], rows_VP)
                        t.matmul(bank(1), onesrow_bf[:, :], rrow_bf[0:1, :],
                                 start=True, stop=True)
                        for tc in range(4):
                            t.matmul(bank(7, 32 + tc, 33 + tc),
                                     rrow_bf[0:1, tc * 128:(tc + 1) * 128],
                                     onesrow_bf[0:1, 0:1], start=True, stop=True,
                                     skip_group_check=True)
                        t.fin(S["TP"], 1)
                    TE(body)
                    bc_TP = inc("TP", 1)

                    def body(v, bc_TP=bc_TP, s=s, iota_GD=iota_GD):
                        v.wait_ge(S["TP"], bc_TP)
                        v.wait_ge(S["IOT"], iota_GD)
                        v.tensor_scalar(posb[:, :], iota_t[:, :], t0col[:, :],
                                        float(s * SC), OP.add, OP.add)
                        v.tensor_copy(rstdb[:, :], bank(1))
                        v.tensor_copy(rstd_colT[:, 0:4], bank(7, 32, 36))
                        v.fin(S["VP"], 1)
                    VE(body)
                    bc_VP = inc("VP", 1)

                    prev_vev_SP = 0
                    for tc in range(4):
                        def body(t, tc=tc, bc_VP=bc_VP,
                                 last_oevac_SP=last_oevac_SP, pvev=prev_vev_SP):
                            t.wait_ge(S["GW"], w_GD)
                            t.wait_ge(S["VP"], bc_VP)
                            if last_oevac_SP:
                                t.wait_ge(S["VP"], last_oevac_SP)
                            if pvev:
                                t.wait_ge(S["VP"], pvev)
                            for half in range(2):
                                bnk = bank(4) if half == 0 else bank(5)
                                for k in range(8):
                                    t.matmul(
                                        bnk,
                                        xbf[:, k * SC + tc * 128: k * SC + (tc + 1) * 128],
                                        wv_sb[:, k * D + half * 512: k * D + half * 512 + 512],
                                        start=(k == 0), stop=False)
                                t.matmul(bnk, nmrow_bf[0:1, tc * 128:(tc + 1) * 128],
                                         colsum_v[0:1, half * 512: half * 512 + 512],
                                         start=False, stop=True)
                            t.fin(S["TP"], 1)
                        TE(body)
                        tp = inc("TP", 1)

                        def body(v, tc=tc, tp=tp):
                            v.wait_ge(S["TP"], tp)
                            v.tensor_scalar(vsb[:, tc * D: tc * D + 512], bank(4),
                                            rstd_colT[:, tc:tc + 1], None, OP.mult)
                            v.tensor_scalar(vsb[:, tc * D + 512: (tc + 1) * D],
                                            bank(5), rstd_colT[:, tc:tc + 1],
                                            None, OP.mult).then_inc(S["VP"], 1)
                        VE(body)
                        prev_vev_SP = inc("VP", 1)
                    vsb_SP = C["VP"]

                    for c8t in range(8):
                        def body(v, c8=c8t, bc_VP=bc_VP, pts=prev_tab_SP):
                            v.wait_ge(S["VP"], bc_VP)
                            if pts:
                                v.wait_ge(S["SP"], pts)
                            v.tensor_scalar(tb0[:, :], posb[:, :],
                                            theta_pc[:, c8:c8 + 1], None, OP.mult
                                            ).then_inc(S["VP"], 1)
                        VE(body)
                        tw1 = inc("VP", 1)

                        def body(v, tw1=tw1):
                            v.wait_ge(S["VP"], tw1)
                            v.tensor_scalar(tb1[:, :], tb0[:, :], INV_2PI, MAGIC,
                                            OP.mult, OP.add).then_inc(S["VP"], 1)
                        VE(body)
                        tw2 = inc("VP", 1)

                        def body(v, tw2=tw2):
                            v.wait_ge(S["VP"], tw2)
                            v.tensor_scalar(tb2[:, :], tb1[:, :], MAGIC, -C1,
                                            OP.subtract, OP.mult
                                            ).then_inc(S["VP"], 1)
                        VE(body)
                        tw3a = inc("VP", 1)

                        def body(v, tw3a=tw3a):
                            v.wait_ge(S["VP"], tw3a)
                            v.tensor_scalar(tb1[:, :], tb1[:, :], MAGIC, -C2,
                                            OP.subtract, OP.mult
                                            ).then_inc(S["VP"], 1)
                        VE(body)
                        tw3 = inc("VP", 1)

                        def body(v, tw3=tw3):
                            v.wait_ge(S["VP"], tw3)
                            v.tensor_tensor(tb0[:, :], tb0[:, :], tb2[:, :], OP.add
                                            ).then_inc(S["VP"], 1)
                        VE(body)
                        tw4 = inc("VP", 1)

                        def body(v, tw4=tw4):
                            v.wait_ge(S["VP"], tw4)
                            v.tensor_tensor(tb0[:, :], tb0[:, :], tb1[:, :], OP.add
                                            ).then_inc(S["VP"], 1)
                        VE(body)
                        tw5 = inc("VP", 1)

                        def body(v, tw5=tw5):
                            v.wait_ge(S["VP"], tw5)
                            v.tensor_scalar(tb0[:, :], tb0[:, :], -PI_CLAMP,
                                            PI_CLAMP, OP.max, OP.min
                                            ).then_inc(S["VP"], 1)
                        VE(body)
                        rr_VP = inc("VP", 1)

                        def body(sca, c8=c8t, rr_VP=rr_VP, sab=C["SP"] + 1):
                            sca.wait_ge(S["VP"], rr_VP)
                            sca.activation(sin_t[:, c8 * SC:(c8 + 1) * SC],
                                           tb0[:, :], AF.Sin)
                            sca.activation(tb2[:, :], tb0[:, :], AF.Abs
                                           ).then_inc(S["SP"], 1)
                            sca.wait_ge(S["SP"], sab)
                            sca.activation(cos_t[:, c8 * SC:(c8 + 1) * SC],
                                           tb2[:, :], AF.Sin, scale=-1.0,
                                           bias=halfpi_col[:, :]
                                           ).then_inc(S["SP"], 1)
                        SE(body)
                        sab = inc("SP", 1)
                        prev_tab_SP = inc("SP", 1)
                    tabs_SP = prev_tab_SP

                    for c8 in range(8):
                        QP = QPa if c8 % 2 == 0 else QPb
                        KP = KPa if c8 % 2 == 0 else KPb
                        pv = prev_pair or {}

                        def body(t, c8=c8, bc_VP=bc_VP, pv=pv):
                            t.wait_ge(S["VP"], bc_VP)
                            if "lrpe_VP" in pv:
                                t.wait_ge(S["VP"], pv["lrpe_VP"])
                            for k in range(8):
                                t.matmul(bank(2),
                                         wq_sb[:, k * D + c8 * 128: k * D + (c8 + 1) * 128],
                                         xbf[:, k * SC:(k + 1) * SC],
                                         start=(k == 0), stop=False)
                            t.matmul(bank(2), colsum_q[0:1, c8 * 128:(c8 + 1) * 128],
                                     nmrow_bf[0:1, :], start=False, stop=True)
                            t.fin(S["TP"], 1)
                        TE(body)
                        q_TP = inc("TP", 1)

                        def body(v, q_TP=q_TP, pv=pv):
                            v.wait_ge(S["TP"], q_TP)
                            if "ks_SP" in pv:
                                v.wait_ge(S["SP"], pv["ks_SP"])
                            v.tensor_tensor(scr[:, :], bank(2), rstdb[:, :], OP.mult
                                            ).then_inc(S["VP"], 1)
                        VE(body)
                        qe_VP = inc("VP", 1)

                        def body(sca, qe_VP=qe_VP, pv=pv):
                            sca.wait_ge(S["VP"], qe_VP)
                            if "lrpe_VP" in pv:
                                sca.wait_ge(S["VP"], pv["lrpe_VP"])
                            sca.activation(qsb[:, :], scr[:, :], AF.Silu)
                            sca.fin(S["SP"], 1)
                        SE(body)
                        qs_SP = inc("SP", 1)

                        def body(t, c8=c8, qe_VP=qe_VP):
                            t.wait_ge(S["VP"], qe_VP)
                            for k in range(8):
                                t.matmul(bank(3),
                                         wk_sb[:, k * D + c8 * 128: k * D + (c8 + 1) * 128],
                                         xbf[:, k * SC:(k + 1) * SC],
                                         start=(k == 0), stop=False)
                            t.matmul(bank(3), colsum_k[0:1, c8 * 128:(c8 + 1) * 128],
                                     nmrow_bf[0:1, :], start=False, stop=True)
                            t.fin(S["TP"], 1)
                        TE(body)
                        k_TP = inc("TP", 1)

                        def body(v, k_TP=k_TP):
                            v.wait_ge(S["TP"], k_TP)
                            v.tensor_tensor(scr2[:, :], bank(3), rstdb[:, :], OP.mult
                                            ).then_inc(S["VP"], 1)
                        VE(body)
                        ke_VP = inc("VP", 1)

                        def body(sca, ke_VP=ke_VP):
                            sca.wait_ge(S["VP"], ke_VP)
                            sca.activation(ksb[:, :], scr2[:, :], AF.Silu)
                            sca.fin(S["SP"], 1)
                        SE(body)
                        ks_SP = inc("SP", 1)


                        def body(v, QP=QP, KP=KP, tab_SP=tabs_SP,
                                 pv=pv, c8=c8, ks_SP=ks_SP,
                                 qpst=qp_ST[c8 % 2],
                                 psd=prev_sd_parity[c8 % 2]):
                            v.wait_ge(S["SP"], max(tab_SP, ks_SP))
                            if qpst:
                                v.wait_ge(S["QS%d" % (c8 % 2)], qpst)
                            if psd:
                                v.wait_ge(S["TP"], psd)
                            for hh in range(2):
                                q_ = qsb[hh * 64:(hh + 1) * 64, :]
                                k_ = ksb[hh * 64:(hh + 1) * 64, :]
                                c_ = cos_t[hh * 64:(hh + 1) * 64,
                                           c8 * SC:(c8 + 1) * SC]
                                s_ = sin_t[hh * 64:(hh + 1) * 64,
                                           c8 * SC:(c8 + 1) * SC]
                                v.tensor_tensor(QP[0:64, hh * SC:(hh + 1) * SC],
                                                q_, c_, OP.mult)
                                v.tensor_tensor(QP[64:128, hh * SC:(hh + 1) * SC],
                                                q_, s_, OP.mult)
                                v.tensor_tensor(KP[0:64, hh * SC:(hh + 1) * SC],
                                                k_, c_, OP.mult)
                                v.tensor_tensor(KP[64:128, hh * SC:(hh + 1) * SC],
                                                k_, s_, OP.mult)
                            v.fin(S["VP"], 1)
                        VE(body)
                        lrpe_VP = inc("VP", 1)

                        def body(sy, QP=QP, s=s, c8=c8, lrpe_VP=lrpe_VP):
                            sy.wait_ge(S["VP"], lrpe_VP)
                            for hh in range(2):
                                h = 2 * c8 + hh
                                sy.dma_start(
                                    out=qprime_h[(s * NH + h) * 128:(s * NH + h + 1) * 128, :],
                                    in_=QP[:, hh * SC:(hh + 1) * SC]
                                ).then_inc(S["QS%d" % (c8 % 2)], 16)
                        SY(body)
                        qp_ST[c8 % 2] = inc("QS%d" % (c8 % 2), 32)

                        et_final = {}
                        sd_TP = None
                        o_evac_SP = None
                        for hh in range(2):
                            h = 2 * c8 + hh
                            obank = 4 if hh == 0 else 5

                            def body(t, hh=hh, lrpe_VP=lrpe_VP, KP=KP):
                                t.wait_ge(S["VP"], lrpe_VP)
                                for cc in range(4):
                                    t.matmul(bank(0, cc * 128, (cc + 1) * 128),
                                             KP[:, hh * SC + cc * 128: hh * SC + (cc + 1) * 128],
                                             ident_bf[:, :],
                                             start=(cc == 0), stop=(cc == 3),
                                             skip_group_check=True)
                                t.fin(S["TP"], 1)
                            TE(body)
                            ktm_TP = inc("TP", 1)

                            def body(v, ktm_TP=ktm_TP):
                                v.wait_ge(S["TP"], ktm_TP)
                                v.tensor_copy(KtmT[:, :], bank(0)).then_inc(S["VP"], 1)
                            VE(body)
                            ktm_VP = inc("VP", 1)

                            et_evs = []
                            for i in range(4):
                                W = 512 - i * 128
                                ebank = 1 if i % 2 == 0 else 6

                                def body(t, i=i, hh=hh, W=W, ebank=ebank, QP=QP,
                                         KP=KP, lrpe_VP=lrpe_VP,
                                         prev=(et_evs[-2] if len(et_evs) >= 2 else None)):
                                    t.wait_ge(S["VP"], lrpe_VP)
                                    if prev is not None:
                                        t.wait_ge(S["VP"], max(prev[0], prev[1]))
                                    t.matmul(bank(ebank, 0, W),
                                             KP[:, hh * SC + i * 128: hh * SC + (i + 1) * 128],
                                             QP[:, hh * SC + i * 128: hh * SC + 512],
                                             start=True, stop=True)
                                    t.fin(S["TP"], 1)
                                TE(body)
                                et_TP = inc("TP", 1)

                                def body(v, i=i, ebank=ebank, W=W, et_TP=et_TP):
                                    v.wait_ge(S["TP"], et_TP)
                                    v.tensor_tensor(Et_sb[:, i * SC: i * SC + 128],
                                                    bank(ebank, 0, 128),
                                                    trimask_bf[:, :], OP.mult
                                                    ).then_inc(S["VP"], 1)
                                    if W > 128:
                                        v.tensor_copy(
                                            Et_sb[:, i * SC + 128: i * SC + W],
                                            bank(ebank, 128, W)
                                            ).then_inc(S["VP"], 1)
                                VE(body)
                                et_VP = inc("VP", 1)
                                et_OD = inc("VP", 1) if W > 128 else et_VP
                                et_evs.append((et_VP, et_OD))
                            et_final = {"et_VP": et_evs[-1][0], "et_SP": et_evs[-1][1]}

                            def body(t, hh=hh, h=h, obank=obank, QP=QP,
                                     etv=et_evs[-1], sbf_VP=sbf_VP,
                                     vsb_SP=vsb_SP):
                                t.wait_ge(S["VP"], max(etv[0], etv[1]))
                                t.wait_ge(S["VP"], vsb_SP)
                                t.wait_ge(S["VP"], sbf_VP)
                                t.matmul(bank(obank)[0:64, :],
                                         Sbf[:, h * HD:(h + 1) * HD],
                                         QP[:, hh * SC: hh * SC + 512],
                                         start=True, stop=False)
                                for i in range(4):
                                    W = 512 - i * 128
                                    t.matmul(bank(obank, i * 128, 512)[0:64, :],
                                             vsb[:, i * D + h * HD: i * D + (h + 1) * HD],
                                             Et_sb[:, i * SC: i * SC + W],
                                             start=False, stop=(i == 3))
                                t.fin(S["TP"], 1)
                            TE(body)
                            o_TP = inc("TP", 1)

                            def body(t, hh=hh, h=h, ktm_VP=ktm_VP, pv=pv):
                                t.wait_ge(S["VP"], ktm_VP)
                                if hh == 0 and "pair_s_VP" in pv:
                                    t.wait_ge(S["VP"], pv["pair_s_VP"])
                                for cc in range(4):
                                    t.matmul(bank(7, hh * 64, (hh + 1) * 64),
                                             KtmT[:, cc * 128:(cc + 1) * 128],
                                             vsb[:, cc * D + h * HD: cc * D + (h + 1) * HD],
                                             start=(cc == 0 and hh == 0), stop=False,
                                             skip_group_check=True)
                                t.fin(S["TP"], 1)
                            TE(body)
                            sd_TP = inc("TP", 1)

                            def body(v, hh=hh, c8=c8, s=s, obank=obank, o_TP=o_TP):
                                v.wait_ge(S["TP"], o_TP)
                                dst = O_all[hh * 64:(hh + 1) * 64,
                                            c8 * T + s * SC: c8 * T + (s + 1) * SC]
                                v.tensor_copy(dst, bank(obank)[0:64, :]
                                              ).then_inc(S["VP"], 1)
                            VE(body)
                            o_evac_SP = inc("VP", 1)

                        def body(v, c8=c8, sd_TP=sd_TP):
                            v.wait_ge(S["TP"], sd_TP)
                            v.tensor_tensor(Sf0[:, c8 * 128:(c8 + 1) * 128],
                                            Sf0[:, c8 * 128:(c8 + 1) * 128],
                                            bank(7, 0, 128), OP.add
                                            ).then_inc(S["VP"], 1)
                        VE(body)
                        pair_s_VP = inc("VP", 1)
                        last_oevac_SP = o_evac_SP
                        prev_sd_parity[c8 % 2] = sd_TP
                        prev_pair = dict(et_final)
                        prev_pair.update({"ks_SP": ks_SP, "lrpe_VP": lrpe_VP,
                                          "sd_TP": sd_TP,
                                          "pair_s_VP": pair_s_VP})

                    def body(v, last=C["VP"]):
                        v.wait_ge(S["VP"], last)
                        v.tensor_copy(Sbf[:, :], Sf0[:, :]).then_inc(S["VP"], 1)
                    VE(body)
                    sbf_VP = inc("VP", 1)

                Sfin = Sf0

                def body(g, sbf_VP=sbf_VP, Sfin=Sfin):
                    g.wait_ge(S["VP"], sbf_VP)
                    g.dma_start(out=cc1_in.rearrange("(h p) d -> p h d", p=128),
                                in_=Sfin[:, :].rearrange("p (h d) -> p h d", h=NH)
                                ).then_inc(S["AGD"], 16)
                GP(body)
                ag1_GD = inc("AGD", 16)

                def body(g, ag1_GD=ag1_GD):
                    g.wait_ge(S["AGD"], ag1_GD)
                    g.collective_compute(
                        "AllGather", OP.bypass, replica_groups=GROUPS,
                        ins=[cc1_in.opt()], outs=[cc1_out.opt()],
                    ).then_inc(S["CC"], 1)
                GP(body)
                ag1_CC = inc("CC", 1)

                def body(g, ag1_CC=ag1_CC):
                    g.wait_ge(S["CC"], ag1_CC)
                    g.dma_start(
                        out=x_tile[:, 0:4096].rearrange("p (r h d) -> p r h d",
                                                        r=4, h=NH),
                        in_=cc1_out.rearrange("(r h p) d -> p r h d", p=128, h=NH)
                    ).then_inc(S["AGD"], 16)
                GP(body)
                ag1r_GD = inc("AGD", 16)

                last = 0
                for h in range(NH):
                    for step in range(5):
                        def body(v, h=h, step=step, last=last,
                                 ag1r_GD=ag1r_GD):
                            if last:
                                v.wait_ge(S["VP"], last)
                            else:
                                v.wait_ge(S["AGD"], ag1r_GD)
                            def blk(r):
                                return x_tile[:, r * 1024 + h * HD:
                                              r * 1024 + (h + 1) * HD]
                            if step == 0:
                                v.tensor_scalar(scr[:, 0:HD], blk(0),
                                                pmask[:, 0:1], None, OP.mult
                                                ).then_inc(S["VP"], 1)
                            elif step <= 3:
                                v.scalar_tensor_tensor(scr[:, 0:HD], blk(step),
                                                       pmask[:, step:step + 1],
                                                       scr[:, 0:HD], OP.mult,
                                                       OP.add).then_inc(S["VP"], 1)
                            else:
                                v.tensor_copy(Spref[:, h * HD:(h + 1) * HD],
                                              scr[:, 0:HD]).then_inc(S["VP"], 1)
                        VE(body)
                        last = inc("VP", 1)
                spref_VP = C["VP"]
                if DEBUG and ("tail" in DEBUG_SET or "sbf" in DEBUG_SET):
                    def body(sy, spref_VP=spref_VP, Sfin=Sfin):
                        sy.wait_ge(S["VP"], spref_VP)
                        sy.dma_start(out=dbg["sbf"], in_=Sfin[:, :]
                                     ).then_inc(S["DBG"], 16)
                    SY(body)
                    inc("DBG", 16)

            st1b = ExitStack()
            with st1b:
                def sbb(name, shape, dt):
                    return st1b.enter_context(nc.sbuf_tensor("b_" + name, shape, dt))

                wo_sb = sbb("wo_sb", [128, 8 * D], BF16)
                g1w_sb2 = sbb("g1w_sb2", [128, 8 * HD], BF16)
                g2w_sb2 = sbb("g2w_sb2", [HD, D], BF16)
                colsum_g2 = sbb("colsum_g2", [1, HD], BF16)
                QPbuf = [sbb(f"QPbuf{i}", [128, SC], BF16) for i in range(2)]
                gate = sbb("gate", [128, 8 * SC], BF16)
                zg = sbb("zg", [128, 8 * SC], F32)
                zgbf = sbb("zgbf", [128, 8 * SC], BF16)
                zgsq = sbb("zgsq", [128, 8 * SC], BF16)
                x2t = sbb("x2t", [128, 8 * SC], F32)
                g1s = sbb("g1s", [64, SC], BF16)
                rzrow_bf = sbb("rzrow_bf", [1, SC], BF16)
                nmzrow_bf = sbb("nmzrow_bf", [1, SC], BF16)

                def body(g, spref_VP=spref_VP):
                    g.wait_ge(S["VP"], spref_VP)
                    g.dma_start(
                        out=wo_sb[:, :].rearrange("p (c n) -> p c n", c=8),
                        in_=wo_h.rearrange("(c p) n -> p c n", p=128)
                    ).then_inc(S["GW2"], 16)
                    g.dma_start(
                        out=g1w_sb2[:, :].rearrange("p (c n) -> p c n", c=8),
                        in_=g1_h.rearrange("(c p) n -> p c n", p=128)
                    ).then_inc(S["GW2"], 16)
                    g.dma_start(out=g2w_sb2[:, :], in_=g2_h).then_inc(S["GW2"], 16)
                GP(body)
                wo_GD = inc("GW2", 48)

                def body(t, wo_GD=wo_GD, spref_VP=spref_VP):
                    t.wait_ge(S["GW2"], wo_GD)
                    t.wait_ge(S["VP"], spref_VP)
                    for k in range(8):
                        t.matmul(bank(6, 0, HD)[0:1, :], onescol_bf[:, :],
                                 g1w_sb2[:, k * HD:(k + 1) * HD],
                                 start=(k == 0), stop=(k == 7))
                    t.fin(S["TP"], 1)
                TE(body)
                csg_TP = inc("TP", 1)

                def body(v, csg_TP=csg_TP):
                    v.wait_ge(S["TP"], csg_TP)
                    v.tensor_copy(colsum_g2[0:1, :], bank(6, 0, HD)[0:1, :]
                                  ).then_inc(S["VP"], 1)
                VE(body)
                csg_VP = inc("VP", 1)

                qpb_VP = [0, 0]
                x2st_ST = 0
                prev_gate_SP = 0
                prev_a2 = 0
                prev_x2_VP = spref_VP
                for s in range(NSC):
                    def body(sy, s=s, pxv=prev_x2_VP):
                        if pxv:
                            sy.wait_ge(S["VP"], pxv)
                        sy.dma_start(
                            out=x_tile[:, :].rearrange("p (c n) -> p c n", c=8),
                            in_=x_h[:, s * SC:(s + 1) * SC].rearrange(
                                "(c p) n -> p c n", p=128)
                        ).then_inc(S["XLB"], 16)
                    SY(body)
                    x_LD = inc("XLB", 16)

                    def body(v, x_LD=x_LD):
                        v.wait_ge(S["XLB"], x_LD)
                        for c in range(8):
                            v.tensor_copy(xbf[:, c * SC:(c + 1) * SC],
                                          x_tile[:, c * SC:(c + 1) * SC])
                        v.fin(S["VP"], 1)
                    VE(body)
                    xbf_VP = inc("VP", 1)

                    def body(v, s=s, xbf_VP=xbf_VP):
                        v.wait_ge(S["VP"], xbf_VP)
                        m_sl = m_rowF[0:1, s * SC:(s + 1) * SC]
                        r_sl = rstd_rowF[0:1, s * SC:(s + 1) * SC]
                        v.scalar_tensor_tensor(rowc[0:1, 0:SC], m_sl, -1.0, r_sl,
                                               OP.mult, OP.mult).then_inc(S["VP"], 1)
                    VE(body)
                    b1 = inc("VP", 1)

                    def body(v, b1=b1, s=s):
                        v.wait_ge(S["VP"], b1)
                        v.tensor_copy(nmrow_bf[0:1, :], rowc[0:1, 0:SC])
                        v.tensor_copy(rrow_bf[0:1, :],
                                      rstd_rowF[0:1, s * SC:(s + 1) * SC])
                        v.fin(S["VP"], 1)
                    VE(body)
                    rows_VP = inc("VP", 1)

                    def body(t, rows_VP=rows_VP):
                        t.wait_ge(S["VP"], rows_VP)
                        t.matmul(bank(1), onesrow_bf[:, :], rrow_bf[0:1, :],
                                 start=True, stop=True)
                        t.fin(S["TP"], 1)
                    TE(body)
                    rb_TP = inc("TP", 1)

                    def body(v, rb_TP=rb_TP):
                        v.wait_ge(S["TP"], rb_TP)
                        v.tensor_copy(rstdb[:, :], bank(1)).then_inc(S["VP"], 1)
                    VE(body)
                    rb_VP = inc("VP", 1)

                    def body(t, rows_VP=rows_VP, csg_VP=csg_VP, xbf_VP=xbf_VP,
                             prev_gate_SP=prev_gate_SP):
                        t.wait_ge(S["VP"], max(rows_VP, csg_VP, xbf_VP))
                        if prev_gate_SP:
                            t.wait_ge(S["SP"], prev_gate_SP)
                        for k in range(8):
                            t.matmul(bank(0)[0:64, :],
                                     g1w_sb2[:, k * HD:(k + 1) * HD],
                                     xbf[:, k * SC:(k + 1) * SC],
                                     start=(k == 0), stop=False)
                        t.matmul(bank(0)[0:64, :], colsum_g2[0:1, :],
                                 nmrow_bf[0:1, :], start=False, stop=True)
                        t.fin(S["TP"], 1)
                    TE(body)
                    g1_TP = inc("TP", 1)

                    def body(v, g1_TP=g1_TP, rb_VP=rb_VP):
                        v.wait_ge(S["TP"], g1_TP)
                        v.wait_ge(S["VP"], rb_VP)
                        v.tensor_tensor(g1s[:, :], bank(0)[0:64, :], rstdb[0:64, :],
                                        OP.mult).then_inc(S["VP"], 1)
                    VE(body)
                    g1_VP = inc("VP", 1)

                    gate_SPs = []
                    for c8 in range(8):
                        gb = 2 if c8 % 2 == 0 else 3

                        def body(t, c8=c8, gb=gb, g1_VP=g1_VP,
                                 prevsp=(gate_SPs[-2] if len(gate_SPs) >= 2 else 0)):
                            t.wait_ge(S["VP"], g1_VP)
                            if prevsp:
                                t.wait_ge(S["SP"], prevsp)
                            t.matmul(bank(gb), g2w_sb2[:, c8 * 128:(c8 + 1) * 128],
                                     g1s[:, :], start=True, stop=True)
                            t.fin(S["TP"], 1)
                        TE(body)
                        tp = inc("TP", 1)

                        def body(sca, c8=c8, gb=gb, tp=tp):
                            sca.wait_ge(S["TP"], tp)
                            sca.activation(gate[:, c8 * SC:(c8 + 1) * SC], bank(gb),
                                           AF.Sigmoid)
                            sca.fin(S["SP"], 1)
                        SE(body)
                        gate_SPs.append(inc("SP", 1))
                    gate_SP = gate_SPs[-1]
                    prev_gate_SP = gate_SP

                    for c8 in range(8):
                        for hh in range(2):
                            h = 2 * c8 + hh
                            buf = QPbuf[hh]
                            cbk = 4 if hh == 0 else 5

                            def body(sy, s=s, h=h, buf=buf, prev=qpb_VP[hh],
                                     qs0=qp_ST[0], qs1=qp_ST[1]):
                                if prev:
                                    sy.wait_ge(S["VP"], prev)
                                sy.wait_ge(S["QS0"], qs0)
                                sy.wait_ge(S["QS1"], qs1)
                                sy.dma_start(
                                    out=buf[:, :],
                                    in_=qprime_h[(s * NH + h) * 128:(s * NH + h + 1) * 128, :]
                                ).then_inc(S["QL%d" % hh], 16)
                            SY(body)
                            qp_LD = inc("QL%d" % hh, 16)

                            def body(t, h=h, hh=hh, cbk=cbk, buf=buf, qp_LD=qp_LD,
                                     spref_VP=spref_VP, prev=qpb_VP[hh]):
                                t.wait_ge(S["QL%d" % hh], qp_LD)
                                t.wait_ge(S["VP"], max(spref_VP, prev))
                                t.matmul(bank(cbk)[0:64, :],
                                         Spref[:, h * HD:(h + 1) * HD], buf[:, :],
                                         start=True, stop=True)
                                t.fin(S["TP"], 1)
                            TE(body)
                            corr_TP = inc("TP", 1)

                            def body(v, hh=hh, c8=c8, s=s, cbk=cbk, corr_TP=corr_TP):
                                v.wait_ge(S["TP"], corr_TP)
                                dst = O_all[hh * 64:(hh + 1) * 64,
                                            c8 * T + s * SC: c8 * T + (s + 1) * SC]
                                v.tensor_tensor(dst, bank(cbk)[0:64, :], dst, OP.add
                                                ).then_inc(S["VP"], 1)
                            VE(body)
                            qpb_VP[hh] = inc("VP", 1)
                    corr_VP = C["VP"]

                    zg_VPs = []
                    for c8 in range(8):
                        wb = 2 if c8 % 2 == 0 else 3

                        def body(t, c8=c8, wb=wb, corr_VP=corr_VP, wo_GD=wo_GD,
                                 gate_SP=gate_SP, s=s,
                                 prevzg=(zg_VPs[-2] if len(zg_VPs) >= 2 else 0)):
                            t.wait_ge(S["VP"], corr_VP)
                            t.wait_ge(S["SP"], gate_SP)
                            if prevzg:
                                t.wait_ge(S["VP"], prevzg)
                            for k in range(8):
                                t.matmul(bank(wb),
                                         wo_sb[:, k * D + c8 * 128: k * D + (c8 + 1) * 128],
                                         O_all[:, k * T + s * SC: k * T + (s + 1) * SC],
                                         start=(k == 0), stop=(k == 7))
                            t.fin(S["TP"], 1)
                        TE(body)
                        tp = inc("TP", 1)

                        def body(v, c8=c8, wb=wb, tp=tp):
                            v.wait_ge(S["TP"], tp)
                            v.tensor_tensor(zg[:, c8 * SC:(c8 + 1) * SC], bank(wb),
                                            gate[:, c8 * SC:(c8 + 1) * SC], OP.mult
                                            ).then_inc(S["VP"], 1)
                        VE(body)
                        zg_VPs.append(inc("VP", 1))
                    zg_VP = zg_VPs[-1]

                    def body(v, zg_VP=zg_VP):
                        v.wait_ge(S["VP"], zg_VP)
                        for c in range(8):
                            v.tensor_copy(zgbf[:, c * SC:(c + 1) * SC],
                                          zg[:, c * SC:(c + 1) * SC])
                        v.fin(S["VP"], 1)
                    VE(body)
                    zgbf_VP = inc("VP", 1)

                    def body(g, zg_VP=zg_VP):
                        g.wait_ge(S["VP"], zg_VP)
                        for c in range(8):
                            r = g.tensor_tensor(zgsq[:, c * SC:(c + 1) * SC],
                                                zg[:, c * SC:(c + 1) * SC],
                                                zg[:, c * SC:(c + 1) * SC],
                                                OP.mult)
                        r.then_inc(S["PC"], 1)
                    GP(body)
                    zgsq_SP = inc("PC", 1)

                    def body(t, zgbf_VP=zgbf_VP, zgsq_SP=zgsq_SP):
                        t.wait_ge(S["VP"], zgbf_VP)
                        t.wait_ge(S["PC"], zgsq_SP)
                        for c in range(8):
                            t.matmul(bank(0)[0:1, :], onescol_bf[:, :],
                                     zgbf[:, c * SC:(c + 1) * SC],
                                     start=(c == 0), stop=(c == 7))
                        for c in range(8):
                            t.matmul(bank(1)[0:1, :], onescol_bf[:, :],
                                     zgsq[:, c * SC:(c + 1) * SC],
                                     start=(c == 0), stop=(c == 7))
                        t.fin(S["TP"], 1)
                    TE(body)
                    zst_TP = inc("TP", 1)

                    def body(v, zst_TP=zst_TP):
                        v.wait_ge(S["TP"], zst_TP)
                        v.tensor_scalar(rowa[0:1, 0:SC], bank(0)[0:1, :], 1.0 / D,
                                        None, OP.mult)
                        v.tensor_scalar(rowb[0:1, 0:SC], bank(1)[0:1, :], 1.0 / D,
                                        None, OP.mult)
                        v.fin(S["VP"], 1)
                    VE(body)
                    z1 = inc("VP", 1)

                    def body(v, z1=z1):
                        v.wait_ge(S["VP"], z1)
                        v.tensor_tensor(rowc[0:1, 0:SC], rowa[0:1, 0:SC],
                                        rowa[0:1, 0:SC], OP.mult).then_inc(S["VP"], 1)
                    VE(body)
                    z2 = inc("VP", 1)

                    def body(v, z2=z2):
                        v.wait_ge(S["VP"], z2)
                        v.tensor_tensor(rowb[0:1, 0:SC], rowb[0:1, 0:SC],
                                        rowc[0:1, 0:SC], OP.subtract
                                        ).then_inc(S["VP"], 1)
                    VE(body)
                    z3 = inc("VP", 1)

                    def body(sca, z3=z3, z4=C["SP"] + 1):
                        sca.wait_ge(S["VP"], z3)
                        sca.activation(rowc[0:1, 0:SC], rowb[0:1, 0:SC], AF.Ln,
                                       bias=eps_col[0:1, :]).then_inc(S["SP"], 1)
                        sca.wait_ge(S["SP"], z4)
                        sca.activation(rowb[0:1, 0:SC], rowc[0:1, 0:SC], AF.Exp,
                                       scale=-0.5).then_inc(S["SP"], 1)
                    SE(body)
                    z4 = inc("SP", 1)
                    z5 = inc("SP", 1)

                    def body(v, z5=z5):
                        v.wait_ge(S["SP"], z5)
                        v.tensor_copy(rzrow_bf[0:1, :], rowb[0:1, 0:SC])
                        v.scalar_tensor_tensor(rowc[0:1, 0:SC], rowa[0:1, 0:SC],
                                               -1.0, rowb[0:1, 0:SC], OP.mult,
                                               OP.mult)
                        v.fin(S["VP"], 1)
                    VE(body)
                    z6 = inc("VP", 1)

                    def body(v, z6=z6):
                        v.wait_ge(S["VP"], z6)
                        v.tensor_copy(nmzrow_bf[0:1, :], rowc[0:1, 0:SC]
                                      ).then_inc(S["VP"], 1)
                    VE(body)
                    zrows_VP = inc("VP", 1)

                    for c8 in range(8):
                        def body(t, c8=c8, zrows_VP=zrows_VP, prev_a2=prev_a2):
                            t.wait_ge(S["VP"], zrows_VP)
                            if prev_a2:
                                t.wait_ge(S["VP"], prev_a2)
                            t.matmul(bank(4),
                                     gamma_row_bf[0:1, c8 * 128:(c8 + 1) * 128],
                                     rzrow_bf[0:1, :], start=True, stop=True)
                            t.matmul(bank(5),
                                     beta_row_bf[0:1, c8 * 128:(c8 + 1) * 128],
                                     ones512_bf[0:1, :], start=True, stop=False)
                            t.matmul(bank(5),
                                     gamma_row_bf[0:1, c8 * 128:(c8 + 1) * 128],
                                     nmzrow_bf[0:1, :], start=False, stop=True)
                            t.fin(S["TP"], 1)
                        TE(body)
                        mb_TP = inc("TP", 1)

                        def body(v, c8=c8, mb_TP=mb_TP):
                            v.wait_ge(S["TP"], mb_TP)
                            v.tensor_tensor(scr[:, :], zg[:, c8 * SC:(c8 + 1) * SC],
                                            bank(4), OP.mult).then_inc(S["VP"], 1)
                        VE(body)
                        a1 = inc("VP", 1)

                        def body(v, a1=a1):
                            v.wait_ge(S["VP"], a1)
                            v.tensor_tensor(scr2[:, :], scr[:, :], bank(5), OP.add
                                            ).then_inc(S["VP"], 1)
                        VE(body)
                        a2 = inc("VP", 1)

                        def body(v, c8=c8, a2=a2, pxs=x2st_ST, pdbg=C["DBG"],
                                 pdbgg=C["DBGG"]):
                            v.wait_ge(S["VP"], a2)
                            if c8 == 0 and pxs:
                                v.wait_ge(S["X2S"], pxs)
                            if c8 == 0 and DEBUG and pdbg:
                                v.wait_ge(S["DBG"], pdbg)
                            if c8 == 0 and DEBUG and pdbgg:
                                v.wait_ge(S["DBGG"], pdbgg)
                            v.tensor_tensor(x2t[:, c8 * SC:(c8 + 1) * SC],
                                            scr2[:, :],
                                            x_tile[:, c8 * SC:(c8 + 1) * SC], OP.add
                                            ).then_inc(S["VP"], 1)
                        VE(body)
                        inc("VP", 1)
                        prev_a2 = a2
                    x2_VP = C["VP"]
                    prev_x2_VP = x2_VP

                    def body(sy, s=s, x2_VP=x2_VP):
                        sy.wait_ge(S["VP"], x2_VP)
                        sy.dma_start(
                            out=x2_h[:, s * SC:(s + 1) * SC].rearrange(
                                "(c p) n -> p c n", p=128),
                            in_=x2t[:, :].rearrange("p (c n) -> p c n", c=8)
                        ).then_inc(S["X2S"], 16)
                    SY(body)
                    x2st_ST = inc("X2S", 16)
                    if DEBUG and "x2" in DEBUG_SET:
                        def body(sy, s=s, x2_VP=x2_VP):
                            sy.wait_ge(S["VP"], x2_VP)
                            sy.dma_start(
                                out=dbg["x2"][:, s * SC:(s + 1) * SC].rearrange(
                                    "(c p) n -> p c n", p=128),
                                in_=x2t[:, :].rearrange("p (c n) -> p c n", c=8)
                            ).then_inc(S["DBG"], 16)
                        SY(body)
                        inc("DBG", 16)

                def body(g, x2st_ST=x2st_ST):
                    g.wait_ge(S["X2S"], x2st_ST)
                GP(body)

                if DEBUG and ("tail" in DEBUG_SET or "oall" in DEBUG_SET):
                    def body(g, x2st_ST=x2st_ST):
                        g.wait_ge(S["X2S"], x2st_ST)
                        g.dma_start(out=dbg["Oall"], in_=O_all[:, :]
                                    ).then_inc(S["DBGG"], 16)
                        g.dma_start(out=dbg["rows"][0:1, 0:T], in_=m_rowF[0:1, :]
                                    ).then_inc(S["DBGG"], 16)
                        g.dma_start(out=dbg["rows"][0:1, T:2 * T],
                                    in_=rstd_rowF[0:1, :]).then_inc(S["DBGG"], 16)
                    GP(body)
                    inc("DBGG", 48)
            p1_ST = C["X2S"]

        st2 = ExitStack()
        with st2:
            def sb2(name, shape, dt):
                return st2.enter_context(nc.sbuf_tensor("f_" + name, shape, dt))

            tern1 = sb2("tern1", [128, 8 * HID], F8)
            tern2 = sb2("tern2", [128, 32 * D], F8)
            h_y = sb2("h_y", [128, 32 * FTE], F32)
            ca = sb2("ca", [128, 32 * FTE], BF16)
            cc2b = sb2("cc2b", [128, 32 * FTE], BF16)
            ce = sb2("ce", [128, 32 * FTE], BF16)
            x2b = [sb2(f"x2b{i}", [128, 8 * FTE], F32) for i in range(2)]
            xbf2 = sb2("xbf2", [128, 8 * FTE], BF16)
            xsq2 = sb2("xsq2", [128, 2 * FTE], BF16)
            xq = sb2("xq", [128, 8 * FTE], BF16)
            out_sb = sb2("out_sb", [128, 2 * FT], F32)
            tmax = sb2("tmax", [128, FTE], BF16)
            tmin = sb2("tmin", [128, FTE], BF16)
            alb = sb2("alb", [128, FTE], F32)
            beb = sb2("beb", [128, FTE], F32)
            r1b = sb2("r1b", [128, FTE], F32)
            s2b = sb2("s2b", [128, FT], F32)
            r2b = sb2("r2b", [128, FT], F32)
            scq = sb2("scq", [128, FTE], F32)
            scq2 = sb2("scq2", [128, FTE], F32)
            wst2 = sb2("wst2", [128, 2 * 512], F32)
            colred = sb2("colred", [128, 8], F32)

            def body(v, pvp=C["VP"], pst=p1_ST, pdbg=C["DBG"], pdbgg=C["DBGG"]):
                v.wait_ge(S["VP"], pvp)
                v.wait_ge(S["X2S"], pst)
                if pdbg:
                    v.wait_ge(S["DBG"], pdbg)
                if pdbgg:
                    v.wait_ge(S["DBGG"], pdbgg)
                v.memset(x2b[0][:, :], 0.0)
                v.memset(x2b[1][:, :], 0.0)
                v.fin(S["VP"], 1)
            VE(body)
            x2z_VP = inc("VP", 1)

            def emit_tern(wi, wh, tern, sws, first_wait_ST, init_prev=0):
                src = wh.rearrange("(c p) n -> p c n", p=128)
                wcols = wh.shape[1]
                nslab = (wh.shape[0] // 128) * wcols // 512
                prev_q = [init_prev, init_prev]
                for sl in range(nslab):
                    half = sl % 2
                    blk = (sl * 512) // wcols
                    off = (sl * 512) % wcols

                    def body(sy, half=half, blk=blk, off=off, src=src,
                             prev=prev_q[half], fw=(first_wait_ST if sl == 0 else 0),
                             pdbg=(C["DBG"] if sl == 0 else 0),
                             pdbgg=(C["DBGG"] if sl == 0 else 0)):
                        if fw:
                            sy.wait_ge(S["X2S"], fw)
                        if pdbg:
                            sy.wait_ge(S["DBG"], pdbg)
                        if pdbgg:
                            sy.wait_ge(S["DBGG"], pdbgg)
                        if prev:
                            sy.wait_ge(S["VP"], prev)
                        sy.dma_start(out=wst2[:, half * 512:(half + 1) * 512],
                                     in_=src[:, blk, off: off + 512]
                                     ).then_inc(S["WS%d" % half], 16)
                    SY(body)
                    ld = inc("WS%d" % half, 16)

                    def body(g, half=half, sws=sws, ld=ld):
                        g.wait_ge(S["WS%d" % half], ld)
                        g.tensor_scalar(wst2[:, half * 512:(half + 1) * 512],
                                        wst2[:, half * 512:(half + 1) * 512],
                                        sws[:, 0:1], MAGIC, OP.mult, OP.add
                                        ).then_inc(S["PC"], 1)
                    GP(body)
                    sp = inc("PC", 1)

                    def body(v, half=half, sp=sp):
                        v.wait_ge(S["PC"], sp)
                        v.tensor_scalar(wst2[:, half * 512:(half + 1) * 512],
                                        wst2[:, half * 512:(half + 1) * 512],
                                        MAGIC, -1.0, OP.subtract, OP.max
                                        ).then_inc(S["VP"], 1)
                    VE(body)
                    vq = inc("VP", 1)

                    def body(v, half=half, sl=sl, tern=tern, vq=vq):
                        v.wait_ge(S["VP"], vq)
                        v.tensor_scalar(tern[:, sl * 512:(sl + 1) * 512],
                                        wst2[:, half * 512:(half + 1) * 512],
                                        1.0, None, OP.min).then_inc(S["VP"], 1)
                    VE(body)
                    prev_q[half] = inc("VP", 1)
                return C["VP"]

            tern1_VP = emit_tern(0, w1t_h, tern1, sw1s, p1_ST)

            q1_stats_cell = [0]

            def emit_quant1(t_idx, xslot, edge_halo, halo_VP, payload_dst,
                            prev_x2read_VP, prev_stats_TP=0):
                lo = max(0, t_idx * FT - 1)
                hi = min(T, t_idx * FT + FT + 1)
                width = hi - lo
                dof = 1 if t_idx == 0 else 0
                x2tile = x2b[xslot]

                def body(sy, lo=lo, width=width, dof=dof, x2tile=x2tile,
                         pst=p1_ST, pxr=prev_x2read_VP, xslot=xslot,
                         pdbg=C["DBG"], pdbgg=C["DBGG"]):
                    sy.wait_ge(S["X2S"], pst)
                    if pdbg:
                        sy.wait_ge(S["DBG"], pdbg)
                    if pdbgg:
                        sy.wait_ge(S["DBGG"], pdbgg)
                    if pxr:
                        sy.wait_ge(S["VP"], pxr)
                    sy.dma_start(
                        out=x2tile[:, :].rearrange("p (c n) -> p c n", c=8
                                                   )[:, :, dof:dof + width],
                        in_=x2_h[:, lo:lo + width].rearrange("(c p) n -> p c n",
                                                             p=128)
                    ).then_inc(S["XF%d" % xslot], 16)
                SY(body)
                xl_LD = inc("XF%d" % xslot, 16)

                def body(v, x2tile=x2tile, xl_LD=xl_LD, xslot=xslot):
                    v.wait_ge(S["XF%d" % xslot], xl_LD)
                    for c in range(8):
                        v.tensor_copy(xbf2[:, c * FTE:(c + 1) * FTE],
                                      x2tile[:, c * FTE:(c + 1) * FTE])
                    v.fin(S["VP"], 1)
                VE(body)
                xb_VP = inc("VP", 1)

                sq_SPs2 = []
                sq_TPs2 = []
                for c in range(8):
                    def body(g, c=c, x2tile=x2tile, xl_LD=xl_LD, xslot=xslot,
                             ptp=(sq_TPs2[-2] if len(sq_TPs2) >= 2
                                  else prev_stats_TP)):
                        g.wait_ge(S["XF%d" % xslot], xl_LD)
                        if ptp:
                            g.wait_ge(S["TP"], ptp)
                        g.tensor_tensor(xsq2[:, (c % 2) * FTE:(c % 2 + 1) * FTE],
                                        x2tile[:, c * FTE:(c + 1) * FTE],
                                        x2tile[:, c * FTE:(c + 1) * FTE],
                                        OP.mult).then_inc(S["PC"], 1)
                    GP(body)
                    sq_SPs2.append(inc("PC", 1))

                    def body(t, c=c, xb_VP=xb_VP, sp=sq_SPs2[-1]):
                        t.wait_ge(S["VP"], xb_VP)
                        t.wait_ge(S["PC"], sp)
                        t.matmul(bank(5, 0, FTE)[0:1, :], onescol_bf[:, :],
                                 xbf2[:, c * FTE:(c + 1) * FTE],
                                 start=(c == 0), stop=(c == 7))
                        t.matmul(bank(6, 0, FTE)[0:1, :], onescol_bf[:, :],
                                 xsq2[:, (c % 2) * FTE:(c % 2 + 1) * FTE],
                                 start=(c == 0), stop=(c == 7))
                        t.fin(S["TP"], 1)
                    TE(body)
                    sq_TPs2.append(inc("TP", 1))
                st_TP = sq_TPs2[-1]
                q1_stats_cell[0] = st_TP

                def body(v, st_TP=st_TP):
                    v.wait_ge(S["TP"], st_TP)
                    v.tensor_scalar(rowa[0:1, 0:FTE], bank(5, 0, FTE)[0:1, :],
                                    1.0 / D, None, OP.mult)
                    v.tensor_scalar(rowb[0:1, 0:FTE], bank(6, 0, FTE)[0:1, :],
                                    1.0 / D, None, OP.mult)
                    v.fin(S["VP"], 1)
                VE(body)
                q1 = inc("VP", 1)

                def body(v, q1=q1):
                    v.wait_ge(S["VP"], q1)
                    v.tensor_tensor(rowc[0:1, 0:FTE], rowa[0:1, 0:FTE],
                                    rowa[0:1, 0:FTE], OP.mult).then_inc(S["VP"], 1)
                VE(body)
                q2_ = inc("VP", 1)

                def body(v, q2_=q2_):
                    v.wait_ge(S["VP"], q2_)
                    v.tensor_tensor(rowb[0:1, 0:FTE], rowb[0:1, 0:FTE],
                                    rowc[0:1, 0:FTE], OP.subtract
                                    ).then_inc(S["VP"], 1)
                VE(body)
                q3 = inc("VP", 1)

                def body(sca, q3=q3, q4=C["SP"] + 1):
                    sca.wait_ge(S["VP"], q3)
                    sca.activation(rowc[0:1, 0:FTE], rowb[0:1, 0:FTE], AF.Ln,
                                   bias=eps_col[0:1, :]).then_inc(S["SP"], 1)
                    sca.wait_ge(S["SP"], q4)
                    sca.activation(rowb[0:1, 0:FTE], rowc[0:1, 0:FTE], AF.Exp,
                                   scale=-0.5).then_inc(S["SP"], 1)
                SE(body)
                q4 = inc("SP", 1)
                rstd2_SP = inc("SP", 1)

                def body(v, xb_VP=xb_VP):
                    v.wait_ge(S["VP"], xb_VP)
                    for k in range(4):
                        v.tensor_tensor(ca.bitcast(BF16)[:, k * FTE:(k + 1) * FTE],
                                        xbf2[:, 2 * k * FTE:(2 * k + 1) * FTE],
                                        xbf2[:, (2 * k + 1) * FTE:(2 * k + 2) * FTE],
                                        OP.max)
                        v.tensor_tensor(ca.bitcast(BF16)[:, (4 + k) * FTE:(5 + k) * FTE],
                                        xbf2[:, 2 * k * FTE:(2 * k + 1) * FTE],
                                        xbf2[:, (2 * k + 1) * FTE:(2 * k + 2) * FTE],
                                        OP.min)
                    v.fin(S["VP"], 1)
                VE(body)
                tq1 = inc("VP", 1)

                def body(v, tq1=tq1):
                    v.wait_ge(S["VP"], tq1)
                    A = ca.bitcast(BF16)
                    for k in range(2):
                        v.tensor_tensor(A[:, (8 + k) * FTE:(9 + k) * FTE],
                                        A[:, 2 * k * FTE:(2 * k + 1) * FTE],
                                        A[:, (2 * k + 1) * FTE:(2 * k + 2) * FTE],
                                        OP.max)
                        v.tensor_tensor(A[:, (10 + k) * FTE:(11 + k) * FTE],
                                        A[:, (4 + 2 * k) * FTE:(5 + 2 * k) * FTE],
                                        A[:, (5 + 2 * k) * FTE:(6 + 2 * k) * FTE],
                                        OP.min)
                    v.fin(S["VP"], 1)
                VE(body)
                tq2 = inc("VP", 1)

                def body(v, tq2=tq2):
                    v.wait_ge(S["VP"], tq2)
                    A = ca.bitcast(BF16)
                    v.tensor_tensor(tmax[:, :], A[:, 8 * FTE:9 * FTE],
                                    A[:, 9 * FTE:10 * FTE], OP.max)
                    v.tensor_tensor(tmin[:, :], A[:, 10 * FTE:11 * FTE],
                                    A[:, 11 * FTE:12 * FTE], OP.min)
                    v.fin(S["VP"], 1)
                VE(body)
                tree_VP = inc("VP", 1)

                def body(t, tree_VP=tree_VP):
                    t.wait_ge(S["VP"], tree_VP)
                    t.matmul(bank(0, 0, 128), tmax[:, 0:128], ident_bf[:, :],
                             start=True, stop=True)
                    t.matmul(bank(1, 0, 128), tmax[:, 128:256], ident_bf[:, :],
                             start=True, stop=True)
                    t.matmul(bank(2, 0, 128)[0:2, :], tmax[:, 256:258],
                             ident_bf[:, :], start=True, stop=True)
                    t.matmul(bank(0, 128, 256), tmin[:, 0:128], ident_bf[:, :],
                             start=True, stop=True, skip_group_check=True)
                    t.matmul(bank(1, 128, 256), tmin[:, 128:256], ident_bf[:, :],
                             start=True, stop=True, skip_group_check=True)
                    t.matmul(bank(2, 128, 256)[0:2, :], tmin[:, 256:258],
                             ident_bf[:, :], start=True, stop=True,
                             skip_group_check=True)
                    t.fin(S["TP"], 1)
                TE(body)
                tr_TP = inc("TP", 1)

                def body(v, tr_TP=tr_TP):
                    v.wait_ge(S["TP"], tr_TP)
                    v.tensor_reduce(colred[:, 0:1], bank(0, 0, 128), AX.X, OP.max)
                    v.tensor_reduce(colred[:, 1:2], bank(1, 0, 128), AX.X, OP.max)
                    v.tensor_reduce(colred[0:2, 2:3], bank(2, 0, 128)[0:2, :], AX.X,
                                    OP.max)
                    v.tensor_reduce(colred[:, 3:4], bank(0, 128, 256), AX.X, OP.min)
                    v.tensor_reduce(colred[:, 4:5], bank(1, 128, 256), AX.X, OP.min)
                    v.tensor_reduce(colred[0:2, 5:6], bank(2, 128, 256)[0:2, :],
                                    AX.X, OP.min)
                    v.fin(S["VP"], 1)
                VE(body)
                red_VP = inc("VP", 1)

                def body(t, red_VP=red_VP):
                    t.wait_ge(S["VP"], red_VP)
                    t.matmul(bank(3, 0, 128)[0:1, :], colred[:, 0:1], ident_f[:, :],
                             start=True, stop=True)
                    t.matmul(bank(3, 128, 256)[0:1, :], colred[:, 1:2],
                             ident_f[:, :], start=True, stop=True,
                             skip_group_check=True)
                    t.matmul(bank(3, 256, 258)[0:1, :], colred[0:2, 2:3],
                             ident_f[0:2, 0:2], start=True, stop=True,
                             skip_group_check=True)
                    t.matmul(bank(4, 0, 128)[0:1, :], colred[:, 3:4], ident_f[:, :],
                             start=True, stop=True)
                    t.matmul(bank(4, 128, 256)[0:1, :], colred[:, 4:5],
                             ident_f[:, :], start=True, stop=True,
                             skip_group_check=True)
                    t.matmul(bank(4, 256, 258)[0:1, :], colred[0:2, 5:6],
                             ident_f[0:2, 0:2], start=True, stop=True,
                             skip_group_check=True)
                    t.fin(S["TP"], 1)
                TE(body)
                c2r_TP = inc("TP", 1)

                def body(v, c2r_TP=c2r_TP, rstd2_SP=rstd2_SP):
                    v.wait_ge(S["TP"], c2r_TP)
                    v.wait_ge(S["SP"], rstd2_SP)
                    v.tensor_copy(rowc[0:1, 0:FTE], bank(3, 0, FTE)[0:1, :])
                    v.tensor_copy(rowc[0:1, 512:512 + FTE], bank(4, 0, FTE)[0:1, :])
                    v.fin(S["VP"], 1)
                VE(body)
                asm_VP = inc("VP", 1)

                def body(v, asm_VP=asm_VP, rstd2_SP=rstd2_SP):
                    v.wait_ge(S["VP"], asm_VP)
                    v.wait_ge(S["SP"], rstd2_SP)
                    v.tensor_tensor(rowc[0:1, 0:FTE], rowc[0:1, 0:FTE],
                                    rowa[0:1, 0:FTE], OP.subtract)
                    v.fin(S["VP"], 1)
                VE(body)
                m1 = inc("VP", 1)

                def body(v, m1=m1):
                    v.wait_ge(S["VP"], m1)
                    v.tensor_tensor(rowc[0:1, 512:512 + FTE], rowa[0:1, 0:FTE],
                                    rowc[0:1, 512:512 + FTE], OP.subtract
                                    ).then_inc(S["VP"], 1)
                VE(body)
                m2_ = inc("VP", 1)

                def body(v, m2_=m2_):
                    v.wait_ge(S["VP"], m2_)
                    v.tensor_tensor(rowc[0:1, 0:FTE], rowc[0:1, 0:FTE],
                                    rowc[0:1, 512:512 + FTE], OP.max
                                    ).then_inc(S["VP"], 1)
                VE(body)
                m3 = inc("VP", 1)

                def body(v, m3=m3):
                    v.wait_ge(S["VP"], m3)
                    v.tensor_tensor(rowc[0:1, 0:FTE], rowc[0:1, 0:FTE],
                                    rowb[0:1, 0:FTE], OP.mult).then_inc(S["VP"], 1)
                VE(body)
                m4 = inc("VP", 1)

                def body(v, m4=m4):
                    v.wait_ge(S["VP"], m4)
                    v.tensor_scalar(rowc[0:1, 0:FTE], rowc[0:1, 0:FTE], 1e-5, None,
                                    OP.max).then_inc(S["VP"], 1)
                VE(body)
                m5 = inc("VP", 1)

                def body(v, m5=m5):
                    v.wait_ge(S["VP"], m5)
                    v.tensor_scalar(rowc[0:1, 512:512 + FTE], rowc[0:1, 0:FTE],
                                    1.0 / 127.0, None, OP.mult).then_inc(S["VP"], 1)
                VE(body)
                m5b = inc("VP", 1)

                def body(v, m5b=m5b):
                    v.wait_ge(S["VP"], m5b)
                    v.reciprocal(rowc[0:1, 0:FTE], rowc[0:1, 0:FTE]
                                 ).then_inc(S["VP"], 1)
                VE(body)
                m6 = inc("VP", 1)

                def body(v, m6=m6):
                    v.wait_ge(S["VP"], m6)
                    v.tensor_scalar(rowc[0:1, 0:FTE], rowc[0:1, 0:FTE], 127.0, None,
                                    OP.mult).then_inc(S["VP"], 1)
                VE(body)
                m7 = inc("VP", 1)

                def body(v, m7=m7):
                    v.wait_ge(S["VP"], m7)
                    v.tensor_tensor(rowb[0:1, 0:FTE], rowb[0:1, 0:FTE],
                                    rowc[0:1, 0:FTE], OP.mult).then_inc(S["VP"], 1)
                VE(body)
                m8 = inc("VP", 1)

                def body(v, m8=m8):
                    v.wait_ge(S["VP"], m8)
                    v.scalar_tensor_tensor(rowa[0:1, 0:FTE], rowa[0:1, 0:FTE], -1.0,
                                           rowb[0:1, 0:FTE], OP.mult, OP.mult
                                           ).then_inc(S["VP"], 1)
                VE(body)
                rows2_VP = inc("VP", 1)

                if edge_halo == "L":
                    def body(v, rows2_VP=rows2_VP, halo_VP=halo_VP):
                        v.wait_ge(S["VP"], max(rows2_VP, halo_VP))
                        v.tensor_copy(rowc[0:1, 512:513], halo_rcL[0:1, 0:1]
                                      ).then_inc(S["VP"], 1)
                    VE(body)
                    rows2_VP = inc("VP", 1)
                elif edge_halo == "R":
                    def body(v, rows2_VP=rows2_VP, halo_VP=halo_VP):
                        v.wait_ge(S["VP"], max(rows2_VP, halo_VP))
                        v.tensor_copy(rowc[0:1, 512 + 257:512 + 258],
                                      halo_rcR[0:1, 0:1]).then_inc(S["VP"], 1)
                    VE(body)
                    rows2_VP = inc("VP", 1)

                def body(t, rows2_VP=rows2_VP):
                    t.wait_ge(S["VP"], rows2_VP)
                    t.matmul(bank(5, 0, FTE), onesrow_f[:, :], rowb[0:1, 0:FTE],
                             start=True, stop=True)
                    t.matmul(bank(6, 0, FTE), onesrow_f[:, :], rowa[0:1, 0:FTE],
                             start=True, stop=True)
                    t.matmul(bank(7, 0, FTE), onesrow_f[:, :],
                             rowc[0:1, 512:512 + FTE], start=True, stop=True)
                    t.fin(S["TP"], 1)
                TE(body)
                qb_TP = inc("TP", 1)

                def body(v, qb_TP=qb_TP):
                    v.wait_ge(S["TP"], qb_TP)
                    v.tensor_copy(alb[:, :], bank(5, 0, FTE))
                    v.tensor_copy(beb[:, :], bank(6, 0, FTE))
                    v.tensor_copy(r1b[:, :], bank(7, 0, FTE))
                    v.fin(S["VP"], 1)
                VE(body)
                qbc_VP = inc("VP", 1)

                xq_wave = []
                for c in range(8):
                    def body(v, c=c, qbc_VP=qbc_VP, x2tile=x2tile):
                        v.wait_ge(S["VP"], qbc_VP)
                        v.tensor_tensor(scq[:, :], x2tile[:, c * FTE:(c + 1) * FTE],
                                        alb[:, :], OP.mult).then_inc(S["VP"], 1)
                    VE(body)
                    qq1 = inc("VP", 1)

                    def body(v, qq1=qq1):
                        v.wait_ge(S["VP"], qq1)
                        v.tensor_tensor(scq2[:, :], scq[:, :], beb[:, :], OP.add
                                        ).then_inc(S["VP"], 1)
                    VE(body)
                    qq2 = inc("VP", 1)

                    def body(v, c=c, qq2=qq2):
                        v.wait_ge(S["VP"], qq2)
                        v.tensor_scalar(xq[:, c * FTE:(c + 1) * FTE], scq2[:, :],
                                        MAGIC, -MAGIC, OP.add, OP.add
                                        ).then_inc(S["VP"], 1)
                    VE(body)
                    xq_wave.append(inc("VP", 1))
                xq_VP = xq_wave[-1]

                if payload_dst is not None:
                    col, dstc = payload_dst

                    def body(v, xq_VP=xq_VP, col=col, dstc=dstc):
                        v.wait_ge(S["VP"], xq_VP)
                        v.tensor_copy(
                            stage2[:, dstc:dstc + 8],
                            xq[:, :].rearrange("p (c n) -> p c n", c=8
                                               )[:, :, col:col + 1
                                                 ].squeeze(2))
                        v.tensor_copy(stage2[0:1, dstc + 8:dstc + 9],
                                      rowc[0:1, 512 + col:512 + col + 1])
                        v.fin(S["VP"], 1)
                    VE(body)
                    xq_VP = inc("VP", 1)

                if edge_halo == "L":
                    def body(v, xq_VP=xq_VP, halo_VP=halo_VP):
                        v.wait_ge(S["VP"], max(xq_VP, halo_VP))
                        v.tensor_copy(
                            xq[:, :].rearrange("p (c n) -> p c n", c=8)[:, :, 0:1],
                            halo_xqL[:, :].unsqueeze(2)
                        ).then_inc(S["VP"], 1)
                    VE(body)
                    xq_VP = inc("VP", 1)
                elif edge_halo == "R":
                    def body(v, xq_VP=xq_VP, halo_VP=halo_VP):
                        v.wait_ge(S["VP"], max(xq_VP, halo_VP))
                        v.tensor_copy(
                            xq[:, :].rearrange("p (c n) -> p c n", c=8
                                               )[:, :, 257:258],
                            halo_xqR[:, :].unsqueeze(2)
                        ).then_inc(S["VP"], 1)
                    VE(body)
                    xq_VP = inc("VP", 1)

                def body(v, xq_VP=xq_VP):
                    v.wait_ge(S["VP"], xq_VP)
                    for c in range(8):
                        v.tensor_tensor(xq[:, c * FTE:(c + 1) * FTE],
                                        xq[:, c * FTE:(c + 1) * FTE], r1b[:, :],
                                        OP.mult)
                    v.fin(S["VP"], 1)
                VE(body)
                return inc("VP", 1)

            pay0_VP = emit_quant1(0, 0, None, 0, (1, 0), x2z_VP)
            pay7_VP = emit_quant1(7, 1, None, 0, (256, 10), x2z_VP,
                                  prev_stats_TP=q1_stats_cell[0])

            def body(g, pay7_VP=pay7_VP):
                g.wait_ge(S["VP"], pay7_VP)
                g.dma_start(out=cc2_in, in_=stage2[:, :]).then_inc(S["AGD"], 16)
            GP(body)
            ag2_GD = inc("AGD", 16)

            def body(g, ag2_GD=ag2_GD):
                g.wait_ge(S["AGD"], ag2_GD)
                g.collective_compute(
                    "AllGather", OP.bypass, replica_groups=GROUPS,
                    ins=[cc2_in.opt()], outs=[cc2_out.opt()],
                ).then_inc(S["CC"], 1)
            GP(body)
            ag2_CC = inc("CC", 1)

            tern2_VP = emit_tern(1, w2t_h, tern2, sw2s, 0, init_prev=tern1_VP)

            def body(g, ag2_CC=ag2_CC):
                g.wait_ge(S["CC"], ag2_CC)
                g.dma_start(out=stage4[:, :].rearrange("p (r c) -> p r c", r=4),
                            in_=cc2_out.rearrange("(r p) c -> p r c", p=128)
                            ).then_inc(S["AGD"], 16)
            GP(body)
            ag2r_GD = inc("AGD", 16)

            halo_specs = [
                (halo_xqL, 10, 8, hmL), (halo_rcL, 18, 1, hmL),
                (halo_xqR, 0, 8, hmR), (halo_rcR, 8, 1, hmR),
            ]
            last = 0
            for dst, off, w, msk in halo_specs:
                for r in range(4):
                    def body(v, dst=dst, off=off, w=w, msk=msk, r=r, last=last,
                             ag2r_GD=ag2r_GD):
                        if last:
                            v.wait_ge(S["VP"], last)
                        else:
                            v.wait_ge(S["AGD"], ag2r_GD)
                        if r == 0:
                            v.tensor_scalar(dst[:, :],
                                            stage4[:, off:off + w],
                                            msk[:, 0:1], None, OP.mult
                                            ).then_inc(S["VP"], 1)
                        else:
                            v.scalar_tensor_tensor(dst[:, :],
                                                   stage4[:, r * 20 + off:
                                                          r * 20 + off + w],
                                                   msk[:, r:r + 1], dst[:, :],
                                                   OP.mult, OP.add
                                                   ).then_inc(S["VP"], 1)
                    VE(body)
                    last = inc("VP", 1)
            halo_VP = C["VP"]

            prev_x2read_VP = x2z_VP
            prev_out_ST = [0, 0]
            for t_idx in range(NFT):
                eh = "L" if t_idx == 0 else ("R" if t_idx == NFT - 1 else None)
                xqs_VP = emit_quant1(t_idx, t_idx % 2, eh, halo_VP, None,
                                     prev_x2read_VP,
                                     prev_stats_TP=q1_stats_cell[0])

                evac_SPs = []
                for grp in range(8):
                    def body(t, grp=grp, xqs_VP=xqs_VP, tern1_VP=tern1_VP,
                             prevsp=(evac_SPs[-1] if evac_SPs else 0),
                             wsc_VP=wsc_VP):
                        t.wait_ge(S["VP"], max(xqs_VP, tern1_VP))
                        if prevsp:
                            t.wait_ge(S["SP"], prevsp)
                        for j in range(4):
                            hc = grp * 4 + j
                            for k in range(8):
                                t.matmul(
                                    bank(j, 0, FTE),
                                    tern1[:, k * HID + hc * 128: k * HID + (hc + 1) * 128],
                                    xq[:, k * FTE:(k + 1) * FTE],
                                    start=(k == 0), stop=(k == 7))
                        t.fin(S["TP"], 1)
                    TE(body)
                    w1_TP = inc("TP", 1)

                    def body(sca, grp=grp, w1_TP=w1_TP):
                        sca.wait_ge(S["TP"], w1_TP)
                        src = pb[:, 0:2048].rearrange("p (b n) -> p b n", b=4
                                                      )[:, :, 0:FTE]
                        dst = h_y[:, grp * 4 * FTE:(grp + 1) * 4 * FTE
                                  ].rearrange("p (b n) -> p b n", b=4)
                        sca.activation(dst, src, AF.Identity, scale=mean1s[:, :])
                        sca.fin(S["SP"], 1)
                    SE(body)
                    evac_SPs.append(inc("SP", 1))
                h_SP = evac_SPs[-1]

                def body(g, h_SP=h_SP):
                    g.wait_ge(S["SP"], h_SP)
                    for hc in range(32):
                        r = g.tensor_scalar(ca[:, hc * FTE:(hc + 1) * FTE],
                                            h_y[:, hc * FTE:(hc + 1) * FTE],
                                            cw_pc[0][:, hc:hc + 1], None, OP.mult)
                    r.then_inc(S["PC"], 1)
                GP(body)
                tapA_VP = inc("PC", 1)

                def body(g, tapA_VP=tapA_VP):
                    g.wait_ge(S["PC"], tapA_VP)
                    for hc in range(32):
                        r = g.tensor_scalar(cc2b[:, hc * FTE:(hc + 1) * FTE],
                                            h_y[:, hc * FTE:(hc + 1) * FTE],
                                            cw_pc[2][:, hc:hc + 1], None, OP.mult)
                    r.then_inc(S["PC"], 1)
                GP(body)
                tapC_VP = inc("PC", 1)

                def body(g, tapC_VP=tapC_VP):
                    g.wait_ge(S["PC"], tapC_VP)
                    for hc in range(32):
                        r = g.tensor_scalar(ce[:, hc * FTE:(hc + 1) * FTE],
                                            h_y[:, hc * FTE:(hc + 1) * FTE],
                                            cw_pc[1][:, hc:hc + 1], None, OP.mult)
                    r.then_inc(S["PC"], 1)
                GP(body)
                tapE_VP = inc("PC", 1)

                def body(v, tapE_VP=tapE_VP):
                    v.wait_ge(S["PC"], tapE_VP)
                    A3 = ca[:, :].rearrange("p (c n) -> p c n", c=32)
                    C3 = cc2b[:, :].rearrange("p (c n) -> p c n", c=32)
                    v.tensor_tensor(A3[:, :, 0:FT], A3[:, :, 0:FT], C3[:, :, 2:FT + 2],
                                    OP.add).then_inc(S["VP"], 1)
                VE(body)
                t1_VP = inc("VP", 1)

                def body(v, t1_VP=t1_VP):
                    v.wait_ge(S["VP"], t1_VP)
                    A3 = ca[:, :].rearrange("p (c n) -> p c n", c=32)
                    C3 = cc2b[:, :].rearrange("p (c n) -> p c n", c=32)
                    E3 = ce[:, :].rearrange("p (c n) -> p c n", c=32)
                    v.tensor_tensor(C3[:, :, 0:FT], A3[:, :, 0:FT], E3[:, :, 1:FT + 1],
                                    OP.add).then_inc(S["VP"], 1)
                VE(body)
                yp_VP = inc("VP", 1)

                def body(sca, yp_VP=yp_VP):
                    sca.wait_ge(S["VP"], yp_VP)
                    for hc in range(32):
                        sca.activation(h_y[:, hc * FTE + 1: hc * FTE + 1 + FT],
                                       cc2b[:, hc * FTE: hc * FTE + FT], AF.Silu,
                                       bias=cb_pc[:, hc:hc + 1])
                    sca.fin(S["SP"], 1)
                SE(body)
                y_SP = inc("SP", 1)

                if DEBUG and "ffn" in DEBUG_SET and t_idx == 0:
                    def body(sy, y_SP=y_SP):
                        sy.wait_ge(S["SP"], y_SP)
                        sy.dma_start(out=dbg["y0"], in_=h_y[:, :]).then_inc(S["DBG"], 16)
                    SY(body)
                    inc("DBG", 16)

                def body(v, y_SP=y_SP):
                    v.wait_ge(S["SP"], y_SP)
                    A = ca.bitcast(BF16)
                    B = cc2b.bitcast(BF16)
                    for k in range(16):
                        a0 = h_y[:, 2 * k * FTE + 1: 2 * k * FTE + 1 + FT]
                        a1 = h_y[:, (2 * k + 1) * FTE + 1: (2 * k + 1) * FTE + 1 + FT]
                        v.tensor_tensor(A[:, k * FT:(k + 1) * FT], a0, a1, OP.max)
                        v.tensor_tensor(B[:, k * FT:(k + 1) * FT], a0, a1, OP.min)
                    v.fin(S["VP"], 1)
                VE(body)
                ty = inc("VP", 1)
                width = 16
                base = 0
                while width > 1:
                    nw = width // 2
                    def body(v, ty=ty, base=base, nw=nw, width=width):
                        v.wait_ge(S["VP"], ty)
                        A = ca.bitcast(BF16)
                        B = cc2b.bitcast(BF16)
                        nbase = base + width
                        for k in range(nw):
                            v.tensor_tensor(A[:, (nbase + k) * FT:(nbase + k + 1) * FT],
                                            A[:, (base + 2 * k) * FT:(base + 2 * k + 1) * FT],
                                            A[:, (base + 2 * k + 1) * FT:(base + 2 * k + 2) * FT],
                                            OP.max)
                            v.tensor_tensor(B[:, (nbase + k) * FT:(nbase + k + 1) * FT],
                                            B[:, (base + 2 * k) * FT:(base + 2 * k + 1) * FT],
                                            B[:, (base + 2 * k + 1) * FT:(base + 2 * k + 2) * FT],
                                            OP.min)
                        v.fin(S["VP"], 1)
                    VE(body)
                    ty = inc("VP", 1)
                    base += width
                    width = nw

                def body(v, ty=ty, base=base):
                    v.wait_ge(S["VP"], ty)
                    A = ca.bitcast(BF16)
                    B = cc2b.bitcast(BF16)
                    v.tensor_copy(tmax[:, 0:FT], A[:, base * FT:(base + 1) * FT])
                    v.tensor_copy(tmin[:, 0:FT], B[:, base * FT:(base + 1) * FT])
                    v.fin(S["VP"], 1)
                VE(body)
                ytree_VP = inc("VP", 1)

                def body(t, ytree_VP=ytree_VP):
                    t.wait_ge(S["VP"], ytree_VP)
                    t.matmul(bank(0, 0, 128), tmax[:, 0:128], ident_bf[:, :],
                             start=True, stop=True)
                    t.matmul(bank(1, 0, 128), tmax[:, 128:256], ident_bf[:, :],
                             start=True, stop=True)
                    t.matmul(bank(0, 128, 256), tmin[:, 0:128], ident_bf[:, :],
                             start=True, stop=True, skip_group_check=True)
                    t.matmul(bank(1, 128, 256), tmin[:, 128:256], ident_bf[:, :],
                             start=True, stop=True, skip_group_check=True)
                    t.fin(S["TP"], 1)
                TE(body)
                ytr_TP = inc("TP", 1)

                def body(v, ytr_TP=ytr_TP):
                    v.wait_ge(S["TP"], ytr_TP)
                    v.tensor_reduce(colred[:, 0:1], bank(0, 0, 128), AX.X, OP.max)
                    v.tensor_reduce(colred[:, 1:2], bank(1, 0, 128), AX.X, OP.max)
                    v.tensor_reduce(colred[:, 3:4], bank(0, 128, 256), AX.X, OP.min)
                    v.tensor_reduce(colred[:, 4:5], bank(1, 128, 256), AX.X, OP.min)
                    v.fin(S["VP"], 1)
                VE(body)
                yred_VP = inc("VP", 1)

                def body(t, yred_VP=yred_VP):
                    t.wait_ge(S["VP"], yred_VP)
                    t.matmul(bank(3, 0, 128)[0:1, :], colred[:, 0:1], ident_f[:, :],
                             start=True, stop=True)
                    t.matmul(bank(3, 128, 256)[0:1, :], colred[:, 1:2],
                             ident_f[:, :], start=True, stop=True,
                             skip_group_check=True)
                    t.matmul(bank(4, 0, 128)[0:1, :], colred[:, 3:4], ident_f[:, :],
                             start=True, stop=True)
                    t.matmul(bank(4, 128, 256)[0:1, :], colred[:, 4:5],
                             ident_f[:, :], start=True, stop=True,
                             skip_group_check=True)
                    t.fin(S["TP"], 1)
                TE(body)
                yc2r_TP = inc("TP", 1)

                def body(v, yc2r_TP=yc2r_TP):
                    v.wait_ge(S["TP"], yc2r_TP)
                    v.tensor_copy(rowa[0:1, 0:FT], bank(3, 0, FT)[0:1, :])
                    v.tensor_copy(rowb[0:1, 0:FT], bank(4, 0, FT)[0:1, :])
                    v.fin(S["VP"], 1)
                VE(body)
                y1 = inc("VP", 1)

                def body(v, y1=y1):
                    v.wait_ge(S["VP"], y1)
                    v.scalar_tensor_tensor(rowa[0:1, 0:FT], rowb[0:1, 0:FT], -1.0,
                                           rowa[0:1, 0:FT], OP.mult, OP.max
                                           ).then_inc(S["VP"], 1)
                VE(body)
                y2 = inc("VP", 1)

                def body(v, y2=y2):
                    v.wait_ge(S["VP"], y2)
                    v.tensor_scalar(rowa[0:1, 0:FT], rowa[0:1, 0:FT], 1e-5, None,
                                    OP.max).then_inc(S["VP"], 1)
                VE(body)
                y3 = inc("VP", 1)

                def body(v, y3=y3):
                    v.wait_ge(S["VP"], y3)
                    v.tensor_scalar(rowb[0:1, 0:FT], rowa[0:1, 0:FT], 1.0 / 127.0,
                                    None, OP.mult).then_inc(S["VP"], 1)
                VE(body)
                y3b = inc("VP", 1)

                def body(v, y3b=y3b):
                    v.wait_ge(S["VP"], y3b)
                    v.reciprocal(rowa[0:1, 0:FT], rowa[0:1, 0:FT]
                                 ).then_inc(S["VP"], 1)
                VE(body)
                y4 = inc("VP", 1)

                def body(v, y4=y4):
                    v.wait_ge(S["VP"], y4)
                    v.tensor_scalar(rowa[0:1, 0:FT], rowa[0:1, 0:FT], 127.0, None,
                                    OP.mult).then_inc(S["VP"], 1)
                VE(body)
                y5 = inc("VP", 1)

                def body(t, y5=y5):
                    t.wait_ge(S["VP"], y5)
                    t.matmul(bank(5, 0, FT), onesrow_f[:, :], rowa[0:1, 0:FT],
                             start=True, stop=True)
                    t.matmul(bank(6, 0, FT), onesrow_f[:, :], rowb[0:1, 0:FT],
                             start=True, stop=True)
                    t.fin(S["TP"], 1)
                TE(body)
                s2bc_TP = inc("TP", 1)

                def body(v, s2bc_TP=s2bc_TP):
                    v.wait_ge(S["TP"], s2bc_TP)
                    v.tensor_copy(s2b[:, :], bank(5, 0, FT))
                    v.tensor_copy(r2b[:, :], bank(6, 0, FT))
                    v.fin(S["VP"], 1)
                VE(body)
                s2bc_VP = inc("VP", 1)

                xq2_wave = []
                for hc in range(32):
                    def body(v, hc=hc, s2bc_VP=s2bc_VP,
                             prevq=(xq2_wave[-1] if xq2_wave else 0)):
                        v.wait_ge(S["VP"], s2bc_VP)
                        if prevq:
                            v.wait_ge(S["VP"], prevq)
                        v.tensor_tensor(scq[:, 0:FT],
                                        h_y[:, hc * FTE + 1: hc * FTE + 1 + FT],
                                        s2b[:, :], OP.mult).then_inc(S["VP"], 1)
                    VE(body)
                    u1 = inc("VP", 1)

                    def body(v, hc=hc, u1=u1):
                        v.wait_ge(S["VP"], u1)
                        v.tensor_scalar(ce[:, hc * FTE: hc * FTE + FT],
                                        scq[:, 0:FT], MAGIC, -MAGIC,
                                        OP.add, OP.add).then_inc(S["VP"], 1)
                    VE(body)
                    xq2_wave.append(inc("VP", 1))
                xq2_VP = xq2_wave[-1]

                if DEBUG and "ffn" in DEBUG_SET and t_idx == 0:
                    def body(g, xq2_VP=xq2_VP):
                        g.wait_ge(S["VP"], xq2_VP)
                        g.dma_start(out=dbg["h0"], in_=h_y[:, :]).then_inc(S["DBGG"], 16)
                        g.dma_start(out=dbg["xq2_0"], in_=ce[:, :]).then_inc(S["DBGG"], 16)
                        g.dma_start(out=dbg["xq0"], in_=xq[:, :]).then_inc(S["DBGG"], 16)
                    GP(body)
                    inc("DBGG", 48)

                w2ev_VPs = []
                for dc in range(8):
                    wbk = 0 if dc % 2 == 0 else 1

                    def body(t, dc=dc, wbk=wbk, xq2_VP=xq2_VP, tern2_VP=tern2_VP,
                             prevev=(w2ev_VPs[-2] if len(w2ev_VPs) >= 2 else 0)):
                        t.wait_ge(S["VP"], max(xq2_VP, tern2_VP))
                        if prevev:
                            t.wait_ge(S["VP"], prevev)
                        for k in range(32):
                            t.matmul(bank(wbk, 0, FT),
                                     tern2[:, k * D + dc * 128: k * D + (dc + 1) * 128],
                                     ce[:, k * FTE: k * FTE + FT],
                                     start=(k == 0), stop=(k == 31))
                        t.fin(S["TP"], 1)
                    TE(body)
                    w2_TP = inc("TP", 1)

                    def body(v, dc=dc, wbk=wbk, w2_TP=w2_TP, t_idx=t_idx,
                             pwe=(w2ev_VPs[-1] if w2ev_VPs else 0)):
                        v.wait_ge(S["TP"], w2_TP)
                        if pwe:
                            v.wait_ge(S["VP"], pwe)
                        v.scalar_tensor_tensor(scq[:, 0:FT], bank(wbk, 0, FT),
                                               mean2s[:, :], r2b[:, :],
                                               OP.mult, OP.mult).then_inc(S["VP"], 1)
                    VE(body)
                    e1 = inc("VP", 1)

                    oslot = dc % 2

                    def body(v, dc=dc, e1=e1, t_idx=t_idx, oslot=oslot,
                             post=(prev_out_ST if isinstance(prev_out_ST, list)
                                   else [0, 0])):
                        v.wait_ge(S["VP"], e1)
                        if post[oslot]:
                            v.wait_ge(S["OS%d" % oslot], post[oslot])
                        x2tile = x2b[t_idx % 2]
                        v.tensor_tensor(out_sb[:, oslot * FT:(oslot + 1) * FT],
                                        scq[:, 0:FT],
                                        x2tile[:, dc * FTE + 1: dc * FTE + 1 + FT],
                                        OP.add).then_inc(S["VP"], 1)
                    VE(body)
                    ev = inc("VP", 1)
                    w2ev_VPs.append(ev)

                    def body(sy, dc=dc, t_idx=t_idx, oslot=oslot, ev=ev):
                        sy.wait_ge(S["VP"], ev)
                        sy.dma_start(
                            out=out_h[dc * 128:(dc + 1) * 128,
                                      t_idx * FT:(t_idx + 1) * FT],
                            in_=out_sb[:, oslot * FT:(oslot + 1) * FT]
                        ).then_inc(S["OS%d" % oslot], 16)
                    SY(body)
                    stc = inc("OS%d" % oslot, 16)
                    if not isinstance(prev_out_ST, list):
                        prev_out_ST = [0, 0]
                    prev_out_ST[oslot] = stc
                w2_done_VP = w2ev_VPs[-1]
                prev_x2read_VP = w2_done_VP

            fo0, fo1, fdbg, fdbgg = C["OS0"], C["OS1"], C["DBG"], C["DBGG"]

            def body(sy, fo0=fo0, fo1=fo1, fdbg=fdbg, fdbgg=fdbgg):
                sy.wait_ge(S["OS0"], fo0)
                sy.wait_ge(S["OS1"], fo1)
                if fdbg:
                    sy.wait_ge(S["DBG"], fdbg)
                if fdbgg:
                    sy.wait_ge(S["DBGG"], fdbgg)
            SY(body)
    return nc


def _prep_core_inputs(inputs, core):
    b, sc = core // 4, core % 4
    t0 = sc * T
    x = np.asarray(inputs["L0_attn_x"], np.float32)
    xs = np.ascontiguousarray(x[b, t0:t0 + T, :].T)
    d = {
        "x": xs,
        "wq": np.ascontiguousarray(np.asarray(inputs["L0_attn_Wq"], np.float32)),
        "wk": np.ascontiguousarray(np.asarray(inputs["L0_attn_Wk"], np.float32)),
        "wv": np.ascontiguousarray(np.asarray(inputs["L0_attn_Wv"], np.float32)),
        "wo": np.ascontiguousarray(np.asarray(inputs["L0_attn_Wo"], np.float32)),
        "g1w": np.ascontiguousarray(np.asarray(inputs["L0_attn_G1"], np.float32)),
        "g2w": np.ascontiguousarray(np.asarray(inputs["L0_attn_G2"], np.float32)),
        "w1t": np.ascontiguousarray(np.asarray(inputs["L0_ffn_w1"], np.float32).T),
        "w2t": np.ascontiguousarray(np.asarray(inputs["L0_ffn_w2"], np.float32).T),
        "theta_pc": np.ascontiguousarray(
            np.asarray(inputs["L0_attn_theta"], np.float32).reshape(-1)
            .reshape(8, 128).T),
        "gamma_row": np.asarray(inputs["L0_attn_gamma"], np.float32).reshape(1, D),
        "beta_row": np.asarray(inputs["L0_attn_beta"], np.float32).reshape(1, D),
        "cb_pc": np.ascontiguousarray(
            np.asarray(inputs["L0_ffn_conv_b"], np.float32).reshape(32, 128).T),
        "t0col": np.full((128, 1), float(t0), np.float32),
        "trimask": np.triu(np.ones((128, 128), np.float32)),
        "ident": np.eye(128, dtype=np.float32),
        "ones_col": np.ones((128, 1), np.float32),
        "ones_row": np.ones((1, 128), np.float32),
        "ones512": np.ones((1, 512), np.float32),
    }
    cw = np.asarray(inputs["L0_ffn_conv_w"], np.float32)
    for j in range(3):
        d[f"cw{j}_pc"] = np.ascontiguousarray(cw[:, 0, j].reshape(32, 128).T)
    pm = np.zeros((128, 4), np.float32)
    pm[:, :sc] = 1.0
    d["pmask"] = pm
    hl = np.zeros((128, 4), np.float32)
    if sc > 0:
        hl[:, sc - 1] = 1.0
    d["hmaskL"] = hl
    hr = np.zeros((128, 4), np.float32)
    if sc < 3:
        hr[:, sc + 1] = 1.0
    d["hmaskR"] = hr
    return d


_NC_CACHE = {}


def kernel(**inputs):
    key = ("nc", DEBUG, tuple(DEBUG_SET))
    if key not in _NC_CACHE:
        _NC_CACHE[key] = build_nc()
    nc = _NC_CACHE[key]
    in_maps = [_prep_core_inputs(inputs, c) for c in range(8)]
    res = run_bass_kernel_spmd(nc, in_maps, core_ids=list(range(8)))
    out = np.empty((2, 8192, 1024), np.float32)
    for c in range(8):
        b, sc = c // 4, c % 4
        out[b, sc * T:(sc + 1) * T, :] = res.results[c]["out"].T
    if DEBUG:
        kernel._dbg = res.results
    return out


if __name__ == "__main__":
    nc = build_nc()
    print("built ok")


# version 08
# speedup vs baseline: 1.5985x; 1.5985x over previous
import numpy as np
from contextlib import ExitStack

import concourse.bass as bass
import concourse.mybir as mybir
from concourse.bass_utils import run_bass_kernel_spmd
from concourse.alu_op_type import AluOpType as OP

F32 = mybir.dt.float32
BF16 = mybir.dt.bfloat16
F8 = mybir.dt.float8e4
AF = mybir.ActivationFunctionType
AX = mybir.AxisListType

MAGIC = float(np.float32(1.5 * 2 ** 23))
HALF_PI = float(np.float32(np.pi / 2))
INV_2PI = float(np.float32(1.0 / (2 * np.pi)))
C1 = 6.28125
C2 = float(np.float32(2 * np.pi - 6.28125))
PI_CLAMP = float(np.nextafter(np.float32(np.pi), np.float32(0)))
EPS = 1e-5

T = 2048
D = 1024
HID = 4096
NH = 16
HD = 64
NSC = 4
SC = 512
FT = 256
NFT = 8
FTE = FT + 2
GROUPS = [[0, 1, 2, 3], [4, 5, 6, 7]]

DEBUG = False
DEBUG_SET = ()


def build_nc():
    nc = bass.Bass("TRN2", target_bir_lowering=False)

    x_h = nc.dram_tensor("x", [D, T], F32, kind="ExternalInput").ap()
    wq_h = nc.dram_tensor("wq", [D, D], F32, kind="ExternalInput").ap()
    wk_h = nc.dram_tensor("wk", [D, D], F32, kind="ExternalInput").ap()
    wv_h = nc.dram_tensor("wv", [D, D], F32, kind="ExternalInput").ap()
    wo_h = nc.dram_tensor("wo", [D, D], F32, kind="ExternalInput").ap()
    g1_h = nc.dram_tensor("g1w", [D, HD], F32, kind="ExternalInput").ap()
    g2_h = nc.dram_tensor("g2w", [HD, D], F32, kind="ExternalInput").ap()
    w1t_h = nc.dram_tensor("w1t", [D, HID], F32, kind="ExternalInput").ap()
    w2t_h = nc.dram_tensor("w2t", [HID, D], F32, kind="ExternalInput").ap()
    theta_h = nc.dram_tensor("theta_pc", [128, 8], F32, kind="ExternalInput").ap()
    gamr_h = nc.dram_tensor("gamma_row", [1, D], F32, kind="ExternalInput").ap()
    betr_h = nc.dram_tensor("beta_row", [1, D], F32, kind="ExternalInput").ap()
    cw_h = [nc.dram_tensor(f"cw{j}_pc", [128, 32], F32, kind="ExternalInput").ap()
            for j in range(3)]
    cb_h = nc.dram_tensor("cb_pc", [128, 32], F32, kind="ExternalInput").ap()
    t0c_h = nc.dram_tensor("t0col", [128, 1], F32, kind="ExternalInput").ap()
    trimask_h = nc.dram_tensor("trimask", [128, 128], F32, kind="ExternalInput").ap()
    ident_h = nc.dram_tensor("ident", [128, 128], F32, kind="ExternalInput").ap()
    onescol_h = nc.dram_tensor("ones_col", [128, 1], F32, kind="ExternalInput").ap()
    onesrow_h = nc.dram_tensor("ones_row", [1, 128], F32, kind="ExternalInput").ap()
    ones512_h = nc.dram_tensor("ones512", [1, 512], F32, kind="ExternalInput").ap()
    pmask_h = nc.dram_tensor("pmask", [128, 4], F32, kind="ExternalInput").ap()
    hmL_h = nc.dram_tensor("hmaskL", [128, 4], F32, kind="ExternalInput").ap()
    hmR_h = nc.dram_tensor("hmaskR", [128, 4], F32, kind="ExternalInput").ap()
    out_h = nc.dram_tensor("out", [D, T], F32, kind="ExternalOutput").ap()
    dbg = {}
    if DEBUG:
        want = []
        if "x2" in DEBUG_SET:
            want += [("x2", [D, T])]
        if "tail" in DEBUG_SET or "oall" in DEBUG_SET:
            want += [("Oall", [128, 8 * T]), ("rows", [1, 2 * T])]
        if "tail" in DEBUG_SET or "sbf" in DEBUG_SET:
            want += [("sbf", [128, NH * HD])]
        if "ffn" in DEBUG_SET:
            want += [("xq0", [128, 8 * FTE]), ("h0", [128, 32 * FTE]),
                     ("y0", [128, 32 * FTE]), ("xq2_0", [128, 32 * FTE])]
        for nm, shape in want:
            dbg[nm] = nc.dram_tensor("dbg_" + nm, shape, F32,
                                     kind="ExternalOutput").ap()

    qprime_h = nc.dram_tensor("qprime", [NSC * NH * 128, SC], BF16).ap()
    x2_h = nc.dram_tensor("x2scr", [D, T], F32).ap()
    cc1_in = nc.dram_tensor("cc1_in", [NH * 128, HD], F32).ap()
    cc1_out = nc.dram_tensor("cc1_out", [4 * NH * 128, HD], F32).ap()
    cc2_in = nc.dram_tensor("cc2_in", [128, 20], F32).ap()
    cc2_out = nc.dram_tensor("cc2_out", [4 * 128, 20], F32).ap()

    st = ExitStack()
    with st:
        def sb(name, shape, dt):
            return st.enter_context(nc.sbuf_tensor("g_" + name, shape, dt))

        theta_pc = sb("theta_pc", [128, 8], F32)
        gamma_row_bf = sb("gamma_row_bf", [1, D], BF16)
        beta_row_bf = sb("beta_row_bf", [1, D], BF16)
        cw_pc = [sb(f"cw{j}_pc", [128, 32], F32) for j in range(3)]
        cb_pc = sb("cb_pc", [128, 32], F32)
        t0col = sb("t0col", [128, 1], F32)
        trimask_bf = sb("trimask_bf", [128, 128], BF16)
        ident_bf = sb("ident_bf", [128, 128], BF16)
        ident_f = sb("ident_f", [128, 128], F32)
        onescol_bf = sb("onescol_bf", [128, 1], BF16)
        onescol_f = sb("onescol_f", [128, 1], F32)
        onesrow_f = sb("onesrow_f", [1, 128], F32)
        onesrow_bf = sb("onesrow_bf", [1, 128], BF16)
        ones512_bf = sb("ones512_bf", [1, 512], BF16)
        pmask = sb("pmask", [128, 4], F32)
        hmL = sb("hmL", [128, 4], F32)
        hmR = sb("hmR", [128, 4], F32)
        wacc = sb("wacc", [128, 8], F32)
        eps_col = sb("eps_col", [128, 1], F32)
        halfpi_col = sb("halfpi_col", [128, 1], F32)
        magic_col = sb("magic_col", [128, 1], F32)
        mean1s = sb("mean1s", [128, 1], F32)
        mean2s = sb("mean2s", [128, 1], F32)
        sw1s = sb("sw1s", [128, 1], F32)
        sw2s = sb("sw2s", [128, 1], F32)
        halo_xqL = sb("halo_xqL", [128, 8], F32)
        halo_xqR = sb("halo_xqR", [128, 8], F32)
        halo_rcL = sb("halo_rcL", [128, 1], F32)
        halo_rcR = sb("halo_rcR", [128, 1], F32)
        stage2 = sb("stage2", [128, 20], F32)
        stage4 = sb("stage4", [128, 80], F32)
        rowa = sb("rowa", [1, 512], F32)
        rowb = sb("rowb", [1, 512], F32)
        rowc = sb("rowc", [1, 1024], F32)

        pb = st.enter_context(nc.psum_tensor("pb", [128, 4096], F32))

        def bank(i, lo=0, hi=512):
            return pb[:, i * 512 + lo: i * 512 + hi]

        semnames = ["CL", "GC", "GW", "GW2", "IOT", "WS0", "WS1", "XLA", "XLB", "QS0", "QS1", "AGD", "DBG", "QL0", "QL1", "X2S", "XF0", "XF1", "OS0", "OS1", "DBGG", "TP", "VP", "SP", "CC", "PC"]
        S = {n: st.enter_context(nc.semaphore(n)) for n in semnames}
        C = {n: 0 for n in semnames}
        block = st.enter_context(nc.Block())

        class _Px:
            def __init__(self, e):
                self._e = e
                self._last = None

            def __getattr__(self, n):
                f = getattr(self._e, n)
                if callable(f):
                    def g(*a, **k):
                        r = f(*a, **k)
                        if hasattr(r, "then_inc"):
                            self._last = r
                        return r
                    return g
                return f

            def fin(self, sem, n=1):
                self._last.then_inc(sem, n)

        def TE(f): block.tensor(lambda e: f(_Px(e)))
        def VE(f): block.vector(lambda e: f(_Px(e)))
        def SE(f): block.scalar(lambda e: f(_Px(e)))
        def SY(f): block.sync(lambda e: f(_Px(e)))
        def GP(f): block.gpsimd(lambda e: f(_Px(e)))

        def inc(name, n=1):
            C[name] += n
            return C[name]

        def body(sy):
            for dst, src in [
                (theta_pc[:, :], theta_h), (cb_pc[:, :], cb_h),
                (t0col[:, :], t0c_h), (pmask[:, :], pmask_h),
                (hmL[:, :], hmL_h), (hmR[:, :], hmR_h),
                (onesrow_f[:, :], onesrow_h), (onescol_f[:, :], onescol_h),
                (ident_f[:, :], ident_h),
                (cw_pc[0][:, :], cw_h[0]), (cw_pc[1][:, :], cw_h[1]),
                (cw_pc[2][:, :], cw_h[2]),
            ]:
                sy.dma_start(out=dst, in_=src).then_inc(S["CL"], 16)
                inc("CL", 16)
        SY(body)

        def body(g):
            for dst, src in [
                (trimask_bf[:, :], trimask_h), (ident_bf[:, :], ident_h),
                (onescol_bf[:, :], onescol_h), (onesrow_bf[:, :], onesrow_h),
                (gamma_row_bf[:, :], gamr_h), (beta_row_bf[:, :], betr_h),
                (ones512_bf[:, :], ones512_h),
            ]:
                g.dma_start(out=dst, in_=src).then_inc(S["GC"], 16)
                inc("GC", 16)
        GP(body)
        consts_CL, consts_GC = C["CL"], C["GC"]

        def body(v):
            v.wait_ge(S["GC"], consts_GC)
            v.wait_ge(S["CL"], consts_CL)
            v.memset(stage2[:, :], 0.0)
            v.memset(eps_col[:, :], EPS)
            v.memset(halfpi_col[:, :], HALF_PI)
            v.memset(magic_col[:, :], MAGIC)
            v.fin(S["VP"], 1)
        VE(body)
        constcol_VP = inc("VP", 1)

        st1 = ExitStack()
        with st1:
            def sb1(name, shape, dt):
                return st1.enter_context(nc.sbuf_tensor("p1_" + name, shape, dt))

            O_all = sb1("O_all", [128, 8 * T], BF16)
            m_rowF = sb1("m_rowF", [1, T], BF16)
            rstd_rowF = sb1("rstd_rowF", [1, T], BF16)
            Sf0 = sb1("Sf0", [128, NH * HD], F32)
            Sbf = sb1("Sbf", [128, NH * HD], BF16)
            Spref = sb1("Spref", [128, NH * HD], BF16)
            x_tile = sb1("x_tile", [128, 8 * SC], F32)
            xbf = sb1("xbf", [128, 8 * SC], BF16)
            scr = sb1("scr", [128, SC], F32)
            scr2 = sb1("scr2", [128, SC], F32)
            rstdb = sb1("rstdb", [128, SC], BF16)
            nmrow_bf = sb1("nmrow_bf", [1, SC], BF16)
            rrow_bf = sb1("rrow_bf", [1, SC], BF16)

            st1a = ExitStack()
            with st1a:
                def sba(name, shape, dt):
                    return st1a.enter_context(nc.sbuf_tensor("a_" + name, shape, dt))

                wq_sb = sba("wq_sb", [128, 8 * D], BF16)
                wk_sb = sba("wk_sb", [128, 8 * D], BF16)
                wv_sb = sba("wv_sb", [128, 8 * D], BF16)
                g1w_sb = sba("g1w_sb", [128, 8 * HD], BF16)
                colsum_q = sba("colsum_q", [1, D], BF16)
                colsum_k = sba("colsum_k", [1, D], BF16)
                colsum_v = sba("colsum_v", [1, D], BF16)
                xsq = sba("xsq", [128, 2 * SC], BF16)
                vsb = sba("vsb", [128, 4 * D], BF16)
                qsb = sba("qsb", [128, SC], BF16)
                ksb = sba("ksb", [128, SC], BF16)
                QPa = sba("QPa", [128, 2 * SC], BF16)
                QPb = sba("QPb", [128, 2 * SC], BF16)
                KPa = sba("KPa", [128, 2 * SC], BF16)
                KPb = sba("KPb", [128, 2 * SC], BF16)
                KtmT = sba("KtmT", [128, SC], BF16)
                Et_sb = sba("Et_sb", [128, 4 * SC], BF16)
                cos_t = sba("cos_t", [128, 8 * SC], BF16)
                sin_t = sba("sin_t", [128, 8 * SC], BF16)
                tb0 = sba("tb0", [128, SC], F32)
                tb1 = sba("tb1", [128, SC], F32)
                tb2 = sba("tb2", [128, SC], F32)
                posb = sba("posb", [128, SC], F32)
                iota_t = sba("iota_t", [128, SC], F32)
                rstd_colT = sba("rstd_colT", [128, 4], F32)
                wstage = sba("wstage", [128, 2 * 512], F32)
                wred = sba("wred", [128, 2], F32)

                def body(g):
                    for dst, src, cdim in ((wq_sb, wq_h, 8), (wk_sb, wk_h, 8),
                                           (wv_sb, wv_h, 8), (g1w_sb, g1_h, 8)):
                        g.dma_start(
                            out=dst[:, :].rearrange("p (c n) -> p c n", c=cdim),
                            in_=src.rearrange("(c p) n -> p c n", p=128)
                        ).then_inc(S["GW"], 16)
                        inc("GW", 16)
                GP(body)
                w_GD = C["GW"]

                def body(g):
                    g.iota(iota_t[:, :], [[1, SC]], base=0, channel_multiplier=0,
                           allow_small_or_imprecise_dtypes=True).then_inc(S["IOT"], 1)
                GP(body)
                iota_GD = inc("IOT", 1)

                def body(v):
                    v.memset(Sf0[:, :], 0.0)
                    v.memset(Sbf[:, :], 0.0)
                    v.fin(S["VP"], 1)
                    inc("VP")
                VE(body)
                sbf_VP = C["VP"]

                slab_VP = []
                for wi, wh in ((0, w1t_h), (1, w2t_h)):
                    src = wh.rearrange("(c p) n -> p c n", p=128)
                    wcols = wh.shape[1]
                    nslab = (wh.shape[0] // 128) * wcols // 512
                    for sl in range(nslab):
                        half = sl % 2
                        blk = (sl * 512) // wcols
                        off = (sl * 512) % wcols
                        need_vp = slab_VP[-2] if len(slab_VP) >= 2 else 0

                        def body(sy, half=half, blk=blk, off=off, src=src,
                                 need_vp=need_vp):
                            if need_vp:
                                sy.wait_ge(S["VP"], need_vp)
                            sy.dma_start(
                                out=wstage[:, half * 512:(half + 1) * 512],
                                in_=src[:, blk, off: off + 512]
                            ).then_inc(S["WS%d" % half], 16)
                        SY(body)
                        ld = inc("WS%d" % half, 16)

                        def body(v, half=half, wi=wi, ld=ld, first=(sl == 0),
                                 pacc=(slab_VP[-1] if slab_VP else 0)):
                            v.wait_ge(S["WS%d" % half], ld)
                            if pacc:
                                v.wait_ge(S["VP"], pacc)
                            v.tensor_reduce(wred[:, wi:wi + 1],
                                            wstage[:, half * 512:(half + 1) * 512],
                                            AX.X, OP.add, apply_absolute_value=True
                                            ).then_inc(S["VP"], 1)
                        VE(body)
                        vp_r = inc("VP", 1)

                        def body(v, wi=wi, vp_r=vp_r, first=(sl == 0)):
                            v.wait_ge(S["VP"], vp_r)
                            if first:
                                v.tensor_copy(wacc[:, wi:wi + 1], wred[:, wi:wi + 1]
                                              ).then_inc(S["VP"], 1)
                            else:
                                v.tensor_tensor(wacc[:, wi:wi + 1], wacc[:, wi:wi + 1],
                                                wred[:, wi:wi + 1], OP.add
                                                ).then_inc(S["VP"], 1)
                        VE(body)
                        slab_VP.append(inc("VP", 1))
                wstat_VP = C["VP"]

                def body(t):
                    t.wait_ge(S["VP"], wstat_VP)
                    t.wait_ge(S["CL"], consts_CL)
                    t.matmul(bank(7, 0, 2)[0:1, :], onescol_f[:, :], wacc[:, 0:2],
                             start=True, stop=True).then_inc(S["TP"], 1)
                TE(body)
                wsum_TP = inc("TP", 1)

                def body(v):
                    v.wait_ge(S["TP"], wsum_TP)
                    v.tensor_scalar(rowa[0:1, 0:2], bank(7, 0, 2)[0:1, :],
                                    1.0 / (D * HID), 1e-5, OP.mult, OP.max
                                    ).then_inc(S["VP"], 1)
                VE(body)
                vmean = inc("VP", 1)

                def body(v):
                    v.wait_ge(S["VP"], vmean)
                    v.reciprocal(rowa[0:1, 2:4], rowa[0:1, 0:2]).then_inc(S["VP"], 1)
                VE(body)
                vrec = inc("VP", 1)

                def body(t):
                    t.wait_ge(S["VP"], vrec)
                    t.matmul(bank(7, 16, 17), onesrow_f[:, :], rowa[0:1, 0:1],
                             start=True, stop=True)
                    t.matmul(bank(7, 17, 18), onesrow_f[:, :], rowa[0:1, 1:2],
                             start=True, stop=True, skip_group_check=True)
                    t.matmul(bank(7, 18, 19), onesrow_f[:, :], rowa[0:1, 2:3],
                             start=True, stop=True, skip_group_check=True)
                    t.matmul(bank(7, 19, 20), onesrow_f[:, :], rowa[0:1, 3:4],
                             start=True, stop=True, skip_group_check=True
                             ).then_inc(S["TP"], 1)
                TE(body)
                wbc_TP = inc("TP", 1)

                def body(v):
                    v.wait_ge(S["TP"], wbc_TP)
                    v.tensor_copy(mean1s[:, :], bank(7, 16, 17))
                    v.tensor_copy(mean2s[:, :], bank(7, 17, 18))
                    v.tensor_copy(sw1s[:, :], bank(7, 18, 19))
                    v.tensor_copy(sw2s[:, :], bank(7, 19, 20))
                    v.fin(S["VP"], 1)
                VE(body)
                wsc_VP = inc("VP", 1)

                prev_cs_VP = 0
                for wsb, width, dst in ((wq_sb, D, colsum_q), (wk_sb, D, colsum_k),
                                        (wv_sb, D, colsum_v)):
                    for hf in range(2):
                        w0, w1 = hf * 512, hf * 512 + 512

                        def body(t, wsb=wsb, width=width, w0=w0, w1=w1,
                                 pcs=prev_cs_VP):
                            t.wait_ge(S["GW"], w_GD)
                            t.wait_ge(S["GC"], consts_GC)
                            t.wait_ge(S["VP"], max(wsc_VP, pcs))
                            for k in range(8):
                                t.matmul(bank(6)[0:1, :], onescol_bf[:, :],
                                         wsb[:, k * width + w0: k * width + w1],
                                         start=(k == 0), stop=(k == 7))
                            t.fin(S["TP"], 1)
                        TE(body)
                        tp = inc("TP", 1)

                        def body(v, dst=dst, w0=w0, w1=w1, tp=tp):
                            v.wait_ge(S["TP"], tp)
                            v.tensor_copy(dst[0:1, w0:w1], bank(6)[0:1, :]
                                          ).then_inc(S["VP"], 1)
                        VE(body)
                        prev_cs_VP = inc("VP", 1)
                colsum_VP = C["VP"]

                qp_ST = [0, 0]
                prev_pair = None
                last_oevac_SP = 0
                prev_stat_TP = 0
                prev_tab_SP = 0
                prev_sd_parity = [0, 0]
                for s in range(NSC):

                    def body(sy, s=s, pst=prev_stat_TP):
                        if pst:
                            sy.wait_ge(S["TP"], pst)
                        sy.dma_start(
                            out=x_tile[:, :].rearrange("p (c n) -> p c n", c=8),
                            in_=x_h[:, s * SC:(s + 1) * SC].rearrange(
                                "(c p) n -> p c n", p=128)
                        ).then_inc(S["XLA"], 16)
                    SY(body)
                    x_LD = inc("XLA", 16)

                    def body(v, x_LD=x_LD):
                        v.wait_ge(S["XLA"], x_LD)
                        for c in range(8):
                            v.tensor_copy(xbf[:, c * SC:(c + 1) * SC],
                                          x_tile[:, c * SC:(c + 1) * SC])
                        v.fin(S["VP"], 1)
                    VE(body)
                    xbf_VP = inc("VP", 1)

                    sq_SPs = []
                    sq_TPs = []
                    for c in range(8):
                        def body(g, c=c, x_LD=x_LD,
                                 ptp=(sq_TPs[-2] if len(sq_TPs) >= 2 else 0)):
                            g.wait_ge(S["XLA"], x_LD)
                            if ptp:
                                g.wait_ge(S["TP"], ptp)
                            g.tensor_tensor(xsq[:, (c % 2) * SC:(c % 2 + 1) * SC],
                                            x_tile[:, c * SC:(c + 1) * SC],
                                            x_tile[:, c * SC:(c + 1) * SC],
                                            OP.mult).then_inc(S["PC"], 1)
                        GP(body)
                        sq_SPs.append(inc("PC", 1))

                        def body(t, c=c, xbf_VP=xbf_VP, sp=sq_SPs[-1]):
                            t.wait_ge(S["VP"], max(xbf_VP, colsum_VP))
                            t.wait_ge(S["PC"], sp)
                            t.matmul(bank(0)[0:1, :], onescol_bf[:, :],
                                     xbf[:, c * SC:(c + 1) * SC],
                                     start=(c == 0), stop=(c == 7))
                            t.matmul(bank(1)[0:1, :], onescol_bf[:, :],
                                     xsq[:, (c % 2) * SC:(c % 2 + 1) * SC],
                                     start=(c == 0), stop=(c == 7))
                            t.fin(S["TP"], 1)
                        TE(body)
                        sq_TPs.append(inc("TP", 1))
                    stat_TP = sq_TPs[-1]
                    prev_stat_TP = stat_TP

                    def body(v, stat_TP=stat_TP, s=s):
                        v.wait_ge(S["TP"], stat_TP)
                        v.tensor_scalar(m_rowF[0:1, s * SC:(s + 1) * SC],
                                        bank(0)[0:1, :], 1.0 / D, None, OP.mult)
                        v.tensor_scalar(rowa[0:1, 0:SC], bank(1)[0:1, :],
                                        1.0 / D, None, OP.mult)
                        v.fin(S["VP"], 1)
                    VE(body)
                    v1 = inc("VP", 1)

                    def body(v, v1=v1, s=s):
                        v.wait_ge(S["VP"], v1)
                        m_sl = m_rowF[0:1, s * SC:(s + 1) * SC]
                        v.tensor_tensor(rowb[0:1, 0:SC], m_sl, m_sl, OP.mult
                                        ).then_inc(S["VP"], 1)
                    VE(body)
                    v2 = inc("VP", 1)

                    def body(v, v2=v2):
                        v.wait_ge(S["VP"], v2)
                        v.tensor_tensor(rowb[0:1, 0:SC], rowa[0:1, 0:SC],
                                        rowb[0:1, 0:SC], OP.subtract
                                        ).then_inc(S["VP"], 1)
                    VE(body)
                    v3 = inc("VP", 1)

                    def body(sca, v3=v3, s=s, sA=C["SP"] + 1):
                        sca.wait_ge(S["VP"], v3)
                        sca.activation(rowc[0:1, 0:SC], rowb[0:1, 0:SC], AF.Ln,
                                       bias=eps_col[0:1, :]).then_inc(S["SP"], 1)
                        sca.wait_ge(S["SP"], sA)
                        sca.activation(rstd_rowF[0:1, s * SC:(s + 1) * SC],
                                       rowc[0:1, 0:SC], AF.Exp, scale=-0.5
                                       ).then_inc(S["SP"], 1)
                    SE(body)
                    sA = inc("SP", 1)
                    rstd_SP = inc("SP", 1)

                    def body(v, rstd_SP=rstd_SP, s=s):
                        v.wait_ge(S["SP"], rstd_SP)
                        m_sl = m_rowF[0:1, s * SC:(s + 1) * SC]
                        r_sl = rstd_rowF[0:1, s * SC:(s + 1) * SC]
                        v.scalar_tensor_tensor(rowc[0:1, 0:SC], m_sl, -1.0, r_sl,
                                               OP.mult, OP.mult).then_inc(S["VP"], 1)
                    VE(body)
                    v4 = inc("VP", 1)

                    def body(v, v4=v4, s=s):
                        v.wait_ge(S["VP"], v4)
                        v.tensor_copy(nmrow_bf[0:1, :], rowc[0:1, 0:SC])
                        v.tensor_copy(rrow_bf[0:1, :],
                                      rstd_rowF[0:1, s * SC:(s + 1) * SC])
                        v.fin(S["VP"], 1)
                    VE(body)
                    rows_VP = inc("VP", 1)

                    def body(t, rows_VP=rows_VP, s=s):
                        t.wait_ge(S["VP"], rows_VP)
                        t.matmul(bank(1), onesrow_bf[:, :], rrow_bf[0:1, :],
                                 start=True, stop=True)
                        for tc in range(4):
                            t.matmul(bank(7, 32 + tc, 33 + tc),
                                     rrow_bf[0:1, tc * 128:(tc + 1) * 128],
                                     onesrow_bf[0:1, 0:1], start=True, stop=True,
                                     skip_group_check=True)
                        t.fin(S["TP"], 1)
                    TE(body)
                    bc_TP = inc("TP", 1)

                    def body(v, bc_TP=bc_TP, s=s, iota_GD=iota_GD):
                        v.wait_ge(S["TP"], bc_TP)
                        v.wait_ge(S["IOT"], iota_GD)
                        v.tensor_scalar(posb[:, :], iota_t[:, :], t0col[:, :],
                                        float(s * SC), OP.add, OP.add)
                        v.tensor_copy(rstdb[:, :], bank(1))
                        v.tensor_copy(rstd_colT[:, 0:4], bank(7, 32, 36))
                        v.fin(S["VP"], 1)
                    VE(body)
                    bc_VP = inc("VP", 1)

                    prev_vev_SP = 0
                    for tc in range(4):
                        def body(t, tc=tc, bc_VP=bc_VP,
                                 last_oevac_SP=last_oevac_SP, pvev=prev_vev_SP):
                            t.wait_ge(S["GW"], w_GD)
                            t.wait_ge(S["VP"], bc_VP)
                            if last_oevac_SP:
                                t.wait_ge(S["VP"], last_oevac_SP)
                            if pvev:
                                t.wait_ge(S["VP"], pvev)
                            for half in range(2):
                                bnk = bank(4) if half == 0 else bank(5)
                                for k in range(8):
                                    t.matmul(
                                        bnk,
                                        xbf[:, k * SC + tc * 128: k * SC + (tc + 1) * 128],
                                        wv_sb[:, k * D + half * 512: k * D + half * 512 + 512],
                                        start=(k == 0), stop=False)
                                t.matmul(bnk, nmrow_bf[0:1, tc * 128:(tc + 1) * 128],
                                         colsum_v[0:1, half * 512: half * 512 + 512],
                                         start=False, stop=True)
                            t.fin(S["TP"], 1)
                        TE(body)
                        tp = inc("TP", 1)

                        def body(v, tc=tc, tp=tp):
                            v.wait_ge(S["TP"], tp)
                            v.tensor_scalar(vsb[:, tc * D: tc * D + 512], bank(4),
                                            rstd_colT[:, tc:tc + 1], None, OP.mult)
                            v.tensor_scalar(vsb[:, tc * D + 512: (tc + 1) * D],
                                            bank(5), rstd_colT[:, tc:tc + 1],
                                            None, OP.mult).then_inc(S["VP"], 1)
                        VE(body)
                        prev_vev_SP = inc("VP", 1)
                    vsb_SP = C["VP"]

                    for c8t in range(8):
                        def body(v, c8=c8t, bc_VP=bc_VP, pts=prev_tab_SP):
                            v.wait_ge(S["VP"], bc_VP)
                            if pts:
                                v.wait_ge(S["SP"], pts)
                            v.tensor_scalar(tb0[:, :], posb[:, :],
                                            theta_pc[:, c8:c8 + 1], None, OP.mult
                                            ).then_inc(S["VP"], 1)
                        VE(body)
                        tw1 = inc("VP", 1)

                        def body(v, tw1=tw1):
                            v.wait_ge(S["VP"], tw1)
                            v.tensor_scalar(tb1[:, :], tb0[:, :], INV_2PI, MAGIC,
                                            OP.mult, OP.add).then_inc(S["VP"], 1)
                        VE(body)
                        tw2 = inc("VP", 1)

                        def body(v, tw2=tw2):
                            v.wait_ge(S["VP"], tw2)
                            v.tensor_scalar(tb2[:, :], tb1[:, :], MAGIC, -C1,
                                            OP.subtract, OP.mult
                                            ).then_inc(S["VP"], 1)
                        VE(body)
                        tw3a = inc("VP", 1)

                        def body(v, tw3a=tw3a):
                            v.wait_ge(S["VP"], tw3a)
                            v.tensor_scalar(tb1[:, :], tb1[:, :], MAGIC, -C2,
                                            OP.subtract, OP.mult
                                            ).then_inc(S["VP"], 1)
                        VE(body)
                        tw3 = inc("VP", 1)

                        def body(v, tw3=tw3):
                            v.wait_ge(S["VP"], tw3)
                            v.tensor_tensor(tb0[:, :], tb0[:, :], tb2[:, :], OP.add
                                            ).then_inc(S["VP"], 1)
                        VE(body)
                        tw4 = inc("VP", 1)

                        def body(v, tw4=tw4):
                            v.wait_ge(S["VP"], tw4)
                            v.tensor_tensor(tb0[:, :], tb0[:, :], tb1[:, :], OP.add
                                            ).then_inc(S["VP"], 1)
                        VE(body)
                        tw5 = inc("VP", 1)

                        def body(v, tw5=tw5):
                            v.wait_ge(S["VP"], tw5)
                            v.tensor_scalar(tb0[:, :], tb0[:, :], -PI_CLAMP,
                                            PI_CLAMP, OP.max, OP.min
                                            ).then_inc(S["VP"], 1)
                        VE(body)
                        rr_VP = inc("VP", 1)

                        def body(sca, c8=c8t, rr_VP=rr_VP, sab=C["SP"] + 1):
                            sca.wait_ge(S["VP"], rr_VP)
                            sca.activation(sin_t[:, c8 * SC:(c8 + 1) * SC],
                                           tb0[:, :], AF.Sin)
                            sca.activation(tb2[:, :], tb0[:, :], AF.Abs
                                           ).then_inc(S["SP"], 1)
                            sca.wait_ge(S["SP"], sab)
                            sca.activation(cos_t[:, c8 * SC:(c8 + 1) * SC],
                                           tb2[:, :], AF.Sin, scale=-1.0,
                                           bias=halfpi_col[:, :]
                                           ).then_inc(S["SP"], 1)
                        SE(body)
                        sab = inc("SP", 1)
                        prev_tab_SP = inc("SP", 1)
                    tabs_SP = prev_tab_SP

                    for c8 in range(8):
                        QP = QPa if c8 % 2 == 0 else QPb
                        KP = KPa if c8 % 2 == 0 else KPb
                        pv = prev_pair or {}

                        def body(t, c8=c8, bc_VP=bc_VP, pv=pv):
                            t.wait_ge(S["VP"], bc_VP)
                            if "lrpe_VP" in pv:
                                t.wait_ge(S["VP"], pv["lrpe_VP"])
                            for k in range(8):
                                t.matmul(bank(2),
                                         wq_sb[:, k * D + c8 * 128: k * D + (c8 + 1) * 128],
                                         xbf[:, k * SC:(k + 1) * SC],
                                         start=(k == 0), stop=False)
                            t.matmul(bank(2), colsum_q[0:1, c8 * 128:(c8 + 1) * 128],
                                     nmrow_bf[0:1, :], start=False, stop=True)
                            t.fin(S["TP"], 1)
                        TE(body)
                        q_TP = inc("TP", 1)

                        def body(v, q_TP=q_TP, pv=pv):
                            v.wait_ge(S["TP"], q_TP)
                            if "ks_SP" in pv:
                                v.wait_ge(S["SP"], pv["ks_SP"])
                            v.tensor_tensor(scr[:, :], bank(2), rstdb[:, :], OP.mult
                                            ).then_inc(S["VP"], 1)
                        VE(body)
                        qe_VP = inc("VP", 1)

                        def body(sca, qe_VP=qe_VP, pv=pv):
                            sca.wait_ge(S["VP"], qe_VP)
                            if "lrpe_VP" in pv:
                                sca.wait_ge(S["VP"], pv["lrpe_VP"])
                            sca.activation(qsb[:, :], scr[:, :], AF.Silu
                                           ).then_inc(S["SP"], 1)
                            sca.wait_ge(S["VP"], qe_VP + 1)
                            sca.activation(ksb[:, :], scr2[:, :], AF.Silu
                                           ).then_inc(S["SP"], 1)
                        SE(body)
                        qs_SP = inc("SP", 1)

                        def body(t, c8=c8, qe_VP=qe_VP):
                            t.wait_ge(S["VP"], qe_VP)
                            for k in range(8):
                                t.matmul(bank(3),
                                         wk_sb[:, k * D + c8 * 128: k * D + (c8 + 1) * 128],
                                         xbf[:, k * SC:(k + 1) * SC],
                                         start=(k == 0), stop=False)
                            t.matmul(bank(3), colsum_k[0:1, c8 * 128:(c8 + 1) * 128],
                                     nmrow_bf[0:1, :], start=False, stop=True)
                            t.fin(S["TP"], 1)
                        TE(body)
                        k_TP = inc("TP", 1)

                        def body(v, k_TP=k_TP):
                            v.wait_ge(S["TP"], k_TP)
                            v.tensor_tensor(scr2[:, :], bank(3), rstdb[:, :], OP.mult
                                            ).then_inc(S["VP"], 1)
                        VE(body)
                        ke_VP = inc("VP", 1)
                        assert ke_VP == qe_VP + 1
                        ks_SP = inc("SP", 1)


                        def body(v, QP=QP, KP=KP, tab_SP=tabs_SP,
                                 pv=pv, c8=c8, ks_SP=ks_SP,
                                 qpst=qp_ST[c8 % 2],
                                 psd=prev_sd_parity[c8 % 2]):
                            v.wait_ge(S["SP"], max(tab_SP, ks_SP))
                            if qpst:
                                v.wait_ge(S["QS%d" % (c8 % 2)], qpst)
                            if psd:
                                v.wait_ge(S["TP"], psd)
                            for hh in range(2):
                                q_ = qsb[hh * 64:(hh + 1) * 64, :]
                                k_ = ksb[hh * 64:(hh + 1) * 64, :]
                                c_ = cos_t[hh * 64:(hh + 1) * 64,
                                           c8 * SC:(c8 + 1) * SC]
                                s_ = sin_t[hh * 64:(hh + 1) * 64,
                                           c8 * SC:(c8 + 1) * SC]
                                v.tensor_tensor(QP[0:64, hh * SC:(hh + 1) * SC],
                                                q_, c_, OP.mult)
                                v.tensor_tensor(QP[64:128, hh * SC:(hh + 1) * SC],
                                                q_, s_, OP.mult)
                                v.tensor_tensor(KP[0:64, hh * SC:(hh + 1) * SC],
                                                k_, c_, OP.mult)
                                v.tensor_tensor(KP[64:128, hh * SC:(hh + 1) * SC],
                                                k_, s_, OP.mult)
                            v.fin(S["VP"], 1)
                        VE(body)
                        lrpe_VP = inc("VP", 1)

                        def body(sy, QP=QP, s=s, c8=c8, lrpe_VP=lrpe_VP):
                            sy.wait_ge(S["VP"], lrpe_VP)
                            for hh in range(2):
                                h = 2 * c8 + hh
                                sy.dma_start(
                                    out=qprime_h[(s * NH + h) * 128:(s * NH + h + 1) * 128, :],
                                    in_=QP[:, hh * SC:(hh + 1) * SC]
                                ).then_inc(S["QS%d" % (c8 % 2)], 16)
                        SY(body)
                        qp_ST[c8 % 2] = inc("QS%d" % (c8 % 2), 32)

                        et_final = {}
                        sd_TP = None
                        o_evac_SP = None
                        for hh in range(2):
                            h = 2 * c8 + hh
                            obank = 4 if hh == 0 else 5

                            def body(t, hh=hh, lrpe_VP=lrpe_VP, KP=KP):
                                t.wait_ge(S["VP"], lrpe_VP)
                                for cc in range(4):
                                    t.matmul(bank(0, cc * 128, (cc + 1) * 128),
                                             KP[:, hh * SC + cc * 128: hh * SC + (cc + 1) * 128],
                                             ident_bf[:, :],
                                             start=(cc == 0), stop=(cc == 3),
                                             skip_group_check=True)
                                t.fin(S["TP"], 1)
                            TE(body)
                            ktm_TP = inc("TP", 1)

                            def body(v, ktm_TP=ktm_TP):
                                v.wait_ge(S["TP"], ktm_TP)
                                v.tensor_copy(KtmT[:, :], bank(0)).then_inc(S["VP"], 1)
                            VE(body)
                            ktm_VP = inc("VP", 1)

                            et_evs = []
                            for i in range(4):
                                W = 512 - i * 128
                                ebank = 1 if i % 2 == 0 else 6

                                def body(t, i=i, hh=hh, W=W, ebank=ebank, QP=QP,
                                         KP=KP, lrpe_VP=lrpe_VP,
                                         prev=(et_evs[-2] if len(et_evs) >= 2 else None)):
                                    t.wait_ge(S["VP"], lrpe_VP)
                                    if prev is not None:
                                        t.wait_ge(S["VP"], max(prev[0], prev[1]))
                                    t.matmul(bank(ebank, 0, W),
                                             KP[:, hh * SC + i * 128: hh * SC + (i + 1) * 128],
                                             QP[:, hh * SC + i * 128: hh * SC + 512],
                                             start=True, stop=True)
                                    t.fin(S["TP"], 1)
                                TE(body)
                                et_TP = inc("TP", 1)

                                def body(v, i=i, ebank=ebank, W=W, et_TP=et_TP):
                                    v.wait_ge(S["TP"], et_TP)
                                    v.tensor_tensor(Et_sb[:, i * SC: i * SC + 128],
                                                    bank(ebank, 0, 128),
                                                    trimask_bf[:, :], OP.mult
                                                    ).then_inc(S["VP"], 1)
                                    if W > 128:
                                        v.tensor_copy(
                                            Et_sb[:, i * SC + 128: i * SC + W],
                                            bank(ebank, 128, W)
                                            ).then_inc(S["VP"], 1)
                                VE(body)
                                et_VP = inc("VP", 1)
                                et_OD = inc("VP", 1) if W > 128 else et_VP
                                et_evs.append((et_VP, et_OD))
                            et_final = {"et_VP": et_evs[-1][0], "et_SP": et_evs[-1][1]}

                            def body(t, hh=hh, h=h, obank=obank, QP=QP,
                                     etv=et_evs[-1], sbf_VP=sbf_VP,
                                     vsb_SP=vsb_SP):
                                t.wait_ge(S["VP"], max(etv[0], etv[1]))
                                t.wait_ge(S["VP"], vsb_SP)
                                t.wait_ge(S["VP"], sbf_VP)
                                t.matmul(bank(obank)[0:64, :],
                                         Sbf[:, h * HD:(h + 1) * HD],
                                         QP[:, hh * SC: hh * SC + 512],
                                         start=True, stop=False)
                                for i in range(4):
                                    W = 512 - i * 128
                                    t.matmul(bank(obank, i * 128, 512)[0:64, :],
                                             vsb[:, i * D + h * HD: i * D + (h + 1) * HD],
                                             Et_sb[:, i * SC: i * SC + W],
                                             start=False, stop=(i == 3))
                                t.fin(S["TP"], 1)
                            TE(body)
                            o_TP = inc("TP", 1)

                            def body(t, hh=hh, h=h, ktm_VP=ktm_VP, pv=pv):
                                t.wait_ge(S["VP"], ktm_VP)
                                if hh == 0 and "pair_s_VP" in pv:
                                    t.wait_ge(S["VP"], pv["pair_s_VP"])
                                for cc in range(4):
                                    t.matmul(bank(7, hh * 64, (hh + 1) * 64),
                                             KtmT[:, cc * 128:(cc + 1) * 128],
                                             vsb[:, cc * D + h * HD: cc * D + (h + 1) * HD],
                                             start=(cc == 0 and hh == 0), stop=False,
                                             skip_group_check=True)
                                t.fin(S["TP"], 1)
                            TE(body)
                            sd_TP = inc("TP", 1)

                            def body(v, hh=hh, c8=c8, s=s, obank=obank, o_TP=o_TP):
                                v.wait_ge(S["TP"], o_TP)
                                dst = O_all[hh * 64:(hh + 1) * 64,
                                            c8 * T + s * SC: c8 * T + (s + 1) * SC]
                                v.tensor_copy(dst, bank(obank)[0:64, :]
                                              ).then_inc(S["VP"], 1)
                            VE(body)
                            o_evac_SP = inc("VP", 1)

                        def body(v, c8=c8, sd_TP=sd_TP):
                            v.wait_ge(S["TP"], sd_TP)
                            v.tensor_tensor(Sf0[:, c8 * 128:(c8 + 1) * 128],
                                            Sf0[:, c8 * 128:(c8 + 1) * 128],
                                            bank(7, 0, 128), OP.add
                                            ).then_inc(S["VP"], 1)
                        VE(body)
                        pair_s_VP = inc("VP", 1)
                        last_oevac_SP = o_evac_SP
                        prev_sd_parity[c8 % 2] = sd_TP
                        prev_pair = dict(et_final)
                        prev_pair.update({"ks_SP": ks_SP, "lrpe_VP": lrpe_VP,
                                          "sd_TP": sd_TP,
                                          "pair_s_VP": pair_s_VP})

                    def body(v, last=C["VP"]):
                        v.wait_ge(S["VP"], last)
                        v.tensor_copy(Sbf[:, :], Sf0[:, :]).then_inc(S["VP"], 1)
                    VE(body)
                    sbf_VP = inc("VP", 1)

                Sfin = Sf0

                def body(g, sbf_VP=sbf_VP, Sfin=Sfin):
                    g.wait_ge(S["VP"], sbf_VP)
                    g.dma_start(out=cc1_in.rearrange("(h p) d -> p h d", p=128),
                                in_=Sfin[:, :].rearrange("p (h d) -> p h d", h=NH)
                                ).then_inc(S["AGD"], 16)
                GP(body)
                ag1_GD = inc("AGD", 16)

                def body(g, ag1_GD=ag1_GD):
                    g.wait_ge(S["AGD"], ag1_GD)
                    g.collective_compute(
                        "AllGather", OP.bypass, replica_groups=GROUPS,
                        ins=[cc1_in.opt()], outs=[cc1_out.opt()],
                    ).then_inc(S["CC"], 1)
                GP(body)
                ag1_CC = inc("CC", 1)

                def body(g, ag1_CC=ag1_CC):
                    g.wait_ge(S["CC"], ag1_CC)
                    g.dma_start(
                        out=x_tile[:, 0:4096].rearrange("p (r h d) -> p r h d",
                                                        r=4, h=NH),
                        in_=cc1_out.rearrange("(r h p) d -> p r h d", p=128, h=NH)
                    ).then_inc(S["AGD"], 16)
                GP(body)
                ag1r_GD = inc("AGD", 16)

                last = 0
                for h in range(NH):
                    for step in range(5):
                        def body(v, h=h, step=step, last=last,
                                 ag1r_GD=ag1r_GD):
                            if last:
                                v.wait_ge(S["VP"], last)
                            else:
                                v.wait_ge(S["AGD"], ag1r_GD)
                            def blk(r):
                                return x_tile[:, r * 1024 + h * HD:
                                              r * 1024 + (h + 1) * HD]
                            if step == 0:
                                v.tensor_scalar(scr[:, 0:HD], blk(0),
                                                pmask[:, 0:1], None, OP.mult
                                                ).then_inc(S["VP"], 1)
                            elif step <= 3:
                                v.scalar_tensor_tensor(scr[:, 0:HD], blk(step),
                                                       pmask[:, step:step + 1],
                                                       scr[:, 0:HD], OP.mult,
                                                       OP.add).then_inc(S["VP"], 1)
                            else:
                                v.tensor_copy(Spref[:, h * HD:(h + 1) * HD],
                                              scr[:, 0:HD]).then_inc(S["VP"], 1)
                        VE(body)
                        last = inc("VP", 1)
                spref_VP = C["VP"]
                if DEBUG and ("tail" in DEBUG_SET or "sbf" in DEBUG_SET):
                    def body(sy, spref_VP=spref_VP, Sfin=Sfin):
                        sy.wait_ge(S["VP"], spref_VP)
                        sy.dma_start(out=dbg["sbf"], in_=Sfin[:, :]
                                     ).then_inc(S["DBG"], 16)
                    SY(body)
                    inc("DBG", 16)

            st1b = ExitStack()
            with st1b:
                def sbb(name, shape, dt):
                    return st1b.enter_context(nc.sbuf_tensor("b_" + name, shape, dt))

                wo_sb = sbb("wo_sb", [128, 8 * D], BF16)
                g1w_sb2 = sbb("g1w_sb2", [128, 8 * HD], BF16)
                g2w_sb2 = sbb("g2w_sb2", [HD, D], BF16)
                colsum_g2 = sbb("colsum_g2", [1, HD], BF16)
                QPbuf = [sbb(f"QPbuf{i}", [128, SC], BF16) for i in range(2)]
                gate = sbb("gate", [128, 8 * SC], BF16)
                zg = sbb("zg", [128, 8 * SC], F32)
                zgbf = sbb("zgbf", [128, 8 * SC], BF16)
                zgsq = sbb("zgsq", [128, 8 * SC], BF16)
                x2t = sbb("x2t", [128, 8 * SC], F32)
                g1s = sbb("g1s", [64, SC], BF16)
                rzrow_bf = sbb("rzrow_bf", [1, SC], BF16)
                nmzrow_bf = sbb("nmzrow_bf", [1, SC], BF16)

                def body(g, spref_VP=spref_VP):
                    g.wait_ge(S["VP"], spref_VP)
                    g.dma_start(
                        out=wo_sb[:, :].rearrange("p (c n) -> p c n", c=8),
                        in_=wo_h.rearrange("(c p) n -> p c n", p=128)
                    ).then_inc(S["GW2"], 16)
                    g.dma_start(
                        out=g1w_sb2[:, :].rearrange("p (c n) -> p c n", c=8),
                        in_=g1_h.rearrange("(c p) n -> p c n", p=128)
                    ).then_inc(S["GW2"], 16)
                    g.dma_start(out=g2w_sb2[:, :], in_=g2_h).then_inc(S["GW2"], 16)
                GP(body)
                wo_GD = inc("GW2", 48)

                def body(t, wo_GD=wo_GD, spref_VP=spref_VP):
                    t.wait_ge(S["GW2"], wo_GD)
                    t.wait_ge(S["VP"], spref_VP)
                    for k in range(8):
                        t.matmul(bank(6, 0, HD)[0:1, :], onescol_bf[:, :],
                                 g1w_sb2[:, k * HD:(k + 1) * HD],
                                 start=(k == 0), stop=(k == 7))
                    t.fin(S["TP"], 1)
                TE(body)
                csg_TP = inc("TP", 1)

                def body(v, csg_TP=csg_TP):
                    v.wait_ge(S["TP"], csg_TP)
                    v.tensor_copy(colsum_g2[0:1, :], bank(6, 0, HD)[0:1, :]
                                  ).then_inc(S["VP"], 1)
                VE(body)
                csg_VP = inc("VP", 1)

                qpb_VP = [0, 0]
                x2st_ST = 0
                prev_gate_SP = 0
                prev_a2 = 0
                prev_x2_VP = spref_VP
                for s in range(NSC):
                    def body(sy, s=s, pxv=prev_x2_VP):
                        if pxv:
                            sy.wait_ge(S["VP"], pxv)
                        sy.dma_start(
                            out=x_tile[:, :].rearrange("p (c n) -> p c n", c=8),
                            in_=x_h[:, s * SC:(s + 1) * SC].rearrange(
                                "(c p) n -> p c n", p=128)
                        ).then_inc(S["XLB"], 16)
                    SY(body)
                    x_LD = inc("XLB", 16)

                    def body(v, x_LD=x_LD):
                        v.wait_ge(S["XLB"], x_LD)
                        for c in range(8):
                            v.tensor_copy(xbf[:, c * SC:(c + 1) * SC],
                                          x_tile[:, c * SC:(c + 1) * SC])
                        v.fin(S["VP"], 1)
                    VE(body)
                    xbf_VP = inc("VP", 1)

                    def body(v, s=s, xbf_VP=xbf_VP):
                        v.wait_ge(S["VP"], xbf_VP)
                        m_sl = m_rowF[0:1, s * SC:(s + 1) * SC]
                        r_sl = rstd_rowF[0:1, s * SC:(s + 1) * SC]
                        v.scalar_tensor_tensor(rowc[0:1, 0:SC], m_sl, -1.0, r_sl,
                                               OP.mult, OP.mult).then_inc(S["VP"], 1)
                    VE(body)
                    b1 = inc("VP", 1)

                    def body(v, b1=b1, s=s):
                        v.wait_ge(S["VP"], b1)
                        v.tensor_copy(nmrow_bf[0:1, :], rowc[0:1, 0:SC])
                        v.tensor_copy(rrow_bf[0:1, :],
                                      rstd_rowF[0:1, s * SC:(s + 1) * SC])
                        v.fin(S["VP"], 1)
                    VE(body)
                    rows_VP = inc("VP", 1)

                    def body(t, rows_VP=rows_VP):
                        t.wait_ge(S["VP"], rows_VP)
                        t.matmul(bank(1), onesrow_bf[:, :], rrow_bf[0:1, :],
                                 start=True, stop=True)
                        t.fin(S["TP"], 1)
                    TE(body)
                    rb_TP = inc("TP", 1)

                    def body(v, rb_TP=rb_TP):
                        v.wait_ge(S["TP"], rb_TP)
                        v.tensor_copy(rstdb[:, :], bank(1)).then_inc(S["VP"], 1)
                    VE(body)
                    rb_VP = inc("VP", 1)

                    def body(t, rows_VP=rows_VP, csg_VP=csg_VP, xbf_VP=xbf_VP,
                             prev_gate_SP=prev_gate_SP):
                        t.wait_ge(S["VP"], max(rows_VP, csg_VP, xbf_VP))
                        if prev_gate_SP:
                            t.wait_ge(S["SP"], prev_gate_SP)
                        for k in range(8):
                            t.matmul(bank(0)[0:64, :],
                                     g1w_sb2[:, k * HD:(k + 1) * HD],
                                     xbf[:, k * SC:(k + 1) * SC],
                                     start=(k == 0), stop=False)
                        t.matmul(bank(0)[0:64, :], colsum_g2[0:1, :],
                                 nmrow_bf[0:1, :], start=False, stop=True)
                        t.fin(S["TP"], 1)
                    TE(body)
                    g1_TP = inc("TP", 1)

                    def body(v, g1_TP=g1_TP, rb_VP=rb_VP):
                        v.wait_ge(S["TP"], g1_TP)
                        v.wait_ge(S["VP"], rb_VP)
                        v.tensor_tensor(g1s[:, :], bank(0)[0:64, :], rstdb[0:64, :],
                                        OP.mult).then_inc(S["VP"], 1)
                    VE(body)
                    g1_VP = inc("VP", 1)

                    gate_SPs = []
                    for c8 in range(8):
                        gb = 2 if c8 % 2 == 0 else 3

                        def body(t, c8=c8, gb=gb, g1_VP=g1_VP,
                                 prevsp=(gate_SPs[-2] if len(gate_SPs) >= 2 else 0)):
                            t.wait_ge(S["VP"], g1_VP)
                            if prevsp:
                                t.wait_ge(S["SP"], prevsp)
                            t.matmul(bank(gb), g2w_sb2[:, c8 * 128:(c8 + 1) * 128],
                                     g1s[:, :], start=True, stop=True)
                            t.fin(S["TP"], 1)
                        TE(body)
                        tp = inc("TP", 1)

                        def body(sca, c8=c8, gb=gb, tp=tp):
                            sca.wait_ge(S["TP"], tp)
                            sca.activation(gate[:, c8 * SC:(c8 + 1) * SC], bank(gb),
                                           AF.Sigmoid)
                            sca.fin(S["SP"], 1)
                        SE(body)
                        gate_SPs.append(inc("SP", 1))
                    gate_SP = gate_SPs[-1]
                    prev_gate_SP = gate_SP

                    for c8 in range(8):
                        for hh in range(2):
                            h = 2 * c8 + hh
                            buf = QPbuf[hh]
                            cbk = 4 if hh == 0 else 5

                            def body(sy, s=s, h=h, buf=buf, prev=qpb_VP[hh],
                                     qs0=qp_ST[0], qs1=qp_ST[1]):
                                if prev:
                                    sy.wait_ge(S["VP"], prev)
                                sy.wait_ge(S["QS0"], qs0)
                                sy.wait_ge(S["QS1"], qs1)
                                sy.dma_start(
                                    out=buf[:, :],
                                    in_=qprime_h[(s * NH + h) * 128:(s * NH + h + 1) * 128, :]
                                ).then_inc(S["QL%d" % hh], 16)
                            SY(body)
                            qp_LD = inc("QL%d" % hh, 16)

                            def body(t, h=h, hh=hh, cbk=cbk, buf=buf, qp_LD=qp_LD,
                                     spref_VP=spref_VP, prev=qpb_VP[hh]):
                                t.wait_ge(S["QL%d" % hh], qp_LD)
                                t.wait_ge(S["VP"], max(spref_VP, prev))
                                t.matmul(bank(cbk)[0:64, :],
                                         Spref[:, h * HD:(h + 1) * HD], buf[:, :],
                                         start=True, stop=True)
                                t.fin(S["TP"], 1)
                            TE(body)
                            corr_TP = inc("TP", 1)

                            def body(v, hh=hh, c8=c8, s=s, cbk=cbk, corr_TP=corr_TP):
                                v.wait_ge(S["TP"], corr_TP)
                                dst = O_all[hh * 64:(hh + 1) * 64,
                                            c8 * T + s * SC: c8 * T + (s + 1) * SC]
                                v.tensor_tensor(dst, bank(cbk)[0:64, :], dst, OP.add
                                                ).then_inc(S["VP"], 1)
                            VE(body)
                            qpb_VP[hh] = inc("VP", 1)
                    corr_VP = C["VP"]

                    zg_VPs = []
                    for c8 in range(8):
                        wb = 2 if c8 % 2 == 0 else 3

                        def body(t, c8=c8, wb=wb, corr_VP=corr_VP, wo_GD=wo_GD,
                                 gate_SP=gate_SP, s=s,
                                 prevzg=(zg_VPs[-2] if len(zg_VPs) >= 2 else 0)):
                            t.wait_ge(S["VP"], corr_VP)
                            t.wait_ge(S["SP"], gate_SP)
                            if prevzg:
                                t.wait_ge(S["VP"], prevzg)
                            for k in range(8):
                                t.matmul(bank(wb),
                                         wo_sb[:, k * D + c8 * 128: k * D + (c8 + 1) * 128],
                                         O_all[:, k * T + s * SC: k * T + (s + 1) * SC],
                                         start=(k == 0), stop=(k == 7))
                            t.fin(S["TP"], 1)
                        TE(body)
                        tp = inc("TP", 1)

                        def body(v, c8=c8, wb=wb, tp=tp):
                            v.wait_ge(S["TP"], tp)
                            v.tensor_tensor(zg[:, c8 * SC:(c8 + 1) * SC], bank(wb),
                                            gate[:, c8 * SC:(c8 + 1) * SC], OP.mult
                                            ).then_inc(S["VP"], 1)
                        VE(body)
                        zg_VPs.append(inc("VP", 1))
                    zg_VP = zg_VPs[-1]

                    def body(v, zg_VP=zg_VP):
                        v.wait_ge(S["VP"], zg_VP)
                        for c in range(8):
                            v.tensor_copy(zgbf[:, c * SC:(c + 1) * SC],
                                          zg[:, c * SC:(c + 1) * SC])
                        v.fin(S["VP"], 1)
                    VE(body)
                    zgbf_VP = inc("VP", 1)

                    def body(g, zg_VP=zg_VP):
                        g.wait_ge(S["VP"], zg_VP)
                        for c in range(8):
                            r = g.tensor_tensor(zgsq[:, c * SC:(c + 1) * SC],
                                                zg[:, c * SC:(c + 1) * SC],
                                                zg[:, c * SC:(c + 1) * SC],
                                                OP.mult)
                        r.then_inc(S["PC"], 1)
                    GP(body)
                    zgsq_SP = inc("PC", 1)

                    def body(t, zgbf_VP=zgbf_VP, zgsq_SP=zgsq_SP):
                        t.wait_ge(S["VP"], zgbf_VP)
                        t.wait_ge(S["PC"], zgsq_SP)
                        for c in range(8):
                            t.matmul(bank(0)[0:1, :], onescol_bf[:, :],
                                     zgbf[:, c * SC:(c + 1) * SC],
                                     start=(c == 0), stop=(c == 7))
                        for c in range(8):
                            t.matmul(bank(1)[0:1, :], onescol_bf[:, :],
                                     zgsq[:, c * SC:(c + 1) * SC],
                                     start=(c == 0), stop=(c == 7))
                        t.fin(S["TP"], 1)
                    TE(body)
                    zst_TP = inc("TP", 1)

                    def body(v, zst_TP=zst_TP):
                        v.wait_ge(S["TP"], zst_TP)
                        v.tensor_scalar(rowa[0:1, 0:SC], bank(0)[0:1, :], 1.0 / D,
                                        None, OP.mult)
                        v.tensor_scalar(rowb[0:1, 0:SC], bank(1)[0:1, :], 1.0 / D,
                                        None, OP.mult)
                        v.fin(S["VP"], 1)
                    VE(body)
                    z1 = inc("VP", 1)

                    def body(v, z1=z1):
                        v.wait_ge(S["VP"], z1)
                        v.tensor_tensor(rowc[0:1, 0:SC], rowa[0:1, 0:SC],
                                        rowa[0:1, 0:SC], OP.mult).then_inc(S["VP"], 1)
                    VE(body)
                    z2 = inc("VP", 1)

                    def body(v, z2=z2):
                        v.wait_ge(S["VP"], z2)
                        v.tensor_tensor(rowb[0:1, 0:SC], rowb[0:1, 0:SC],
                                        rowc[0:1, 0:SC], OP.subtract
                                        ).then_inc(S["VP"], 1)
                    VE(body)
                    z3 = inc("VP", 1)

                    def body(sca, z3=z3, z4=C["SP"] + 1):
                        sca.wait_ge(S["VP"], z3)
                        sca.activation(rowc[0:1, 0:SC], rowb[0:1, 0:SC], AF.Ln,
                                       bias=eps_col[0:1, :]).then_inc(S["SP"], 1)
                        sca.wait_ge(S["SP"], z4)
                        sca.activation(rowb[0:1, 0:SC], rowc[0:1, 0:SC], AF.Exp,
                                       scale=-0.5).then_inc(S["SP"], 1)
                    SE(body)
                    z4 = inc("SP", 1)
                    z5 = inc("SP", 1)

                    def body(v, z5=z5):
                        v.wait_ge(S["SP"], z5)
                        v.tensor_copy(rzrow_bf[0:1, :], rowb[0:1, 0:SC])
                        v.scalar_tensor_tensor(rowc[0:1, 0:SC], rowa[0:1, 0:SC],
                                               -1.0, rowb[0:1, 0:SC], OP.mult,
                                               OP.mult)
                        v.fin(S["VP"], 1)
                    VE(body)
                    z6 = inc("VP", 1)

                    def body(v, z6=z6):
                        v.wait_ge(S["VP"], z6)
                        v.tensor_copy(nmzrow_bf[0:1, :], rowc[0:1, 0:SC]
                                      ).then_inc(S["VP"], 1)
                    VE(body)
                    zrows_VP = inc("VP", 1)

                    for c8 in range(8):
                        def body(t, c8=c8, zrows_VP=zrows_VP, prev_a2=prev_a2):
                            t.wait_ge(S["VP"], zrows_VP)
                            if prev_a2:
                                t.wait_ge(S["VP"], prev_a2)
                            t.matmul(bank(4),
                                     gamma_row_bf[0:1, c8 * 128:(c8 + 1) * 128],
                                     rzrow_bf[0:1, :], start=True, stop=True)
                            t.matmul(bank(5),
                                     beta_row_bf[0:1, c8 * 128:(c8 + 1) * 128],
                                     ones512_bf[0:1, :], start=True, stop=False)
                            t.matmul(bank(5),
                                     gamma_row_bf[0:1, c8 * 128:(c8 + 1) * 128],
                                     nmzrow_bf[0:1, :], start=False, stop=True)
                            t.fin(S["TP"], 1)
                        TE(body)
                        mb_TP = inc("TP", 1)

                        def body(v, c8=c8, mb_TP=mb_TP):
                            v.wait_ge(S["TP"], mb_TP)
                            v.tensor_tensor(scr[:, :], zg[:, c8 * SC:(c8 + 1) * SC],
                                            bank(4), OP.mult).then_inc(S["VP"], 1)
                        VE(body)
                        a1 = inc("VP", 1)

                        def body(v, a1=a1):
                            v.wait_ge(S["VP"], a1)
                            v.tensor_tensor(scr2[:, :], scr[:, :], bank(5), OP.add
                                            ).then_inc(S["VP"], 1)
                        VE(body)
                        a2 = inc("VP", 1)

                        def body(v, c8=c8, a2=a2, pxs=x2st_ST, pdbg=C["DBG"],
                                 pdbgg=C["DBGG"]):
                            v.wait_ge(S["VP"], a2)
                            if c8 == 0 and pxs:
                                v.wait_ge(S["X2S"], pxs)
                            if c8 == 0 and DEBUG and pdbg:
                                v.wait_ge(S["DBG"], pdbg)
                            if c8 == 0 and DEBUG and pdbgg:
                                v.wait_ge(S["DBGG"], pdbgg)
                            v.tensor_tensor(x2t[:, c8 * SC:(c8 + 1) * SC],
                                            scr2[:, :],
                                            x_tile[:, c8 * SC:(c8 + 1) * SC], OP.add
                                            ).then_inc(S["VP"], 1)
                        VE(body)
                        inc("VP", 1)
                        prev_a2 = a2
                    x2_VP = C["VP"]
                    prev_x2_VP = x2_VP

                    def body(sy, s=s, x2_VP=x2_VP):
                        sy.wait_ge(S["VP"], x2_VP)
                        sy.dma_start(
                            out=x2_h[:, s * SC:(s + 1) * SC].rearrange(
                                "(c p) n -> p c n", p=128),
                            in_=x2t[:, :].rearrange("p (c n) -> p c n", c=8)
                        ).then_inc(S["X2S"], 16)
                    SY(body)
                    x2st_ST = inc("X2S", 16)
                    if DEBUG and "x2" in DEBUG_SET:
                        def body(sy, s=s, x2_VP=x2_VP):
                            sy.wait_ge(S["VP"], x2_VP)
                            sy.dma_start(
                                out=dbg["x2"][:, s * SC:(s + 1) * SC].rearrange(
                                    "(c p) n -> p c n", p=128),
                                in_=x2t[:, :].rearrange("p (c n) -> p c n", c=8)
                            ).then_inc(S["DBG"], 16)
                        SY(body)
                        inc("DBG", 16)

                def body(g, x2st_ST=x2st_ST):
                    g.wait_ge(S["X2S"], x2st_ST)
                GP(body)

                if DEBUG and ("tail" in DEBUG_SET or "oall" in DEBUG_SET):
                    def body(g, x2st_ST=x2st_ST):
                        g.wait_ge(S["X2S"], x2st_ST)
                        g.dma_start(out=dbg["Oall"], in_=O_all[:, :]
                                    ).then_inc(S["DBGG"], 16)
                        g.dma_start(out=dbg["rows"][0:1, 0:T], in_=m_rowF[0:1, :]
                                    ).then_inc(S["DBGG"], 16)
                        g.dma_start(out=dbg["rows"][0:1, T:2 * T],
                                    in_=rstd_rowF[0:1, :]).then_inc(S["DBGG"], 16)
                    GP(body)
                    inc("DBGG", 48)
            p1_ST = C["X2S"]

        st2 = ExitStack()
        with st2:
            def sb2(name, shape, dt):
                return st2.enter_context(nc.sbuf_tensor("f_" + name, shape, dt))

            tern1 = sb2("tern1", [128, 8 * HID], F8)
            tern2 = sb2("tern2", [128, 32 * D], F8)
            h_y = sb2("h_y", [128, 32 * FTE], F32)
            ca = sb2("ca", [128, 32 * FTE], BF16)
            cc2b = sb2("cc2b", [128, 32 * FTE], BF16)
            ce = sb2("ce", [128, 32 * FTE], BF16)
            x2b = [sb2(f"x2b{i}", [128, 8 * FTE], F32) for i in range(2)]
            xbf2 = sb2("xbf2", [128, 8 * FTE], BF16)
            xsq2 = sb2("xsq2", [128, 2 * FTE], BF16)
            xq = sb2("xq", [128, 8 * FTE], BF16)
            out_sb = sb2("out_sb", [128, 2 * FT], F32)
            tmax = sb2("tmax", [128, FTE], BF16)
            tmin = sb2("tmin", [128, FTE], BF16)
            alb = sb2("alb", [128, FTE], F32)
            beb = sb2("beb", [128, FTE], F32)
            r1b = sb2("r1b", [128, FTE], F32)
            s2b = sb2("s2b", [128, FT], F32)
            r2b = sb2("r2b", [128, FT], F32)
            scq = sb2("scq", [128, FTE], F32)
            scq2 = sb2("scq2", [128, FTE], F32)
            wst2 = sb2("wst2", [128, 2 * 512], F32)
            colred = sb2("colred", [128, 8], F32)

            def body(v, pvp=C["VP"], pst=p1_ST, pdbg=C["DBG"], pdbgg=C["DBGG"]):
                v.wait_ge(S["VP"], pvp)
                v.wait_ge(S["X2S"], pst)
                if pdbg:
                    v.wait_ge(S["DBG"], pdbg)
                if pdbgg:
                    v.wait_ge(S["DBGG"], pdbgg)
                v.memset(x2b[0][:, :], 0.0)
                v.memset(x2b[1][:, :], 0.0)
                v.fin(S["VP"], 1)
            VE(body)
            x2z_VP = inc("VP", 1)

            def emit_tern(wi, wh, tern, sws, first_wait_ST, init_prev=0):
                src = wh.rearrange("(c p) n -> p c n", p=128)
                wcols = wh.shape[1]
                nslab = (wh.shape[0] // 128) * wcols // 512
                prev_q = [init_prev, init_prev]
                for sl in range(nslab):
                    half = sl % 2
                    blk = (sl * 512) // wcols
                    off = (sl * 512) % wcols

                    def body(sy, half=half, blk=blk, off=off, src=src,
                             prev=prev_q[half], fw=(first_wait_ST if sl == 0 else 0),
                             pdbg=(C["DBG"] if sl == 0 else 0),
                             pdbgg=(C["DBGG"] if sl == 0 else 0)):
                        if fw:
                            sy.wait_ge(S["X2S"], fw)
                        if pdbg:
                            sy.wait_ge(S["DBG"], pdbg)
                        if pdbgg:
                            sy.wait_ge(S["DBGG"], pdbgg)
                        if prev:
                            sy.wait_ge(S["VP"], prev)
                        sy.dma_start(out=wst2[:, half * 512:(half + 1) * 512],
                                     in_=src[:, blk, off: off + 512]
                                     ).then_inc(S["WS%d" % half], 16)
                    SY(body)
                    ld = inc("WS%d" % half, 16)

                    def body(g, half=half, sws=sws, ld=ld):
                        g.wait_ge(S["WS%d" % half], ld)
                        g.tensor_scalar(wst2[:, half * 512:(half + 1) * 512],
                                        wst2[:, half * 512:(half + 1) * 512],
                                        sws[:, 0:1], MAGIC, OP.mult, OP.add
                                        ).then_inc(S["PC"], 1)
                    GP(body)
                    sp = inc("PC", 1)

                    def body(v, half=half, sp=sp):
                        v.wait_ge(S["PC"], sp)
                        v.tensor_scalar(wst2[:, half * 512:(half + 1) * 512],
                                        wst2[:, half * 512:(half + 1) * 512],
                                        MAGIC, -1.0, OP.subtract, OP.max
                                        ).then_inc(S["VP"], 1)
                    VE(body)
                    vq = inc("VP", 1)

                    def body(v, half=half, sl=sl, tern=tern, vq=vq):
                        v.wait_ge(S["VP"], vq)
                        v.tensor_scalar(tern[:, sl * 512:(sl + 1) * 512],
                                        wst2[:, half * 512:(half + 1) * 512],
                                        1.0, None, OP.min).then_inc(S["VP"], 1)
                    VE(body)
                    prev_q[half] = inc("VP", 1)
                return C["VP"]

            tern1_VP = emit_tern(0, w1t_h, tern1, sw1s, p1_ST)

            q1_stats_cell = [0]

            def emit_quant1(t_idx, xslot, edge_halo, halo_VP, payload_dst,
                            prev_x2read_VP, prev_stats_TP=0):
                lo = max(0, t_idx * FT - 1)
                hi = min(T, t_idx * FT + FT + 1)
                width = hi - lo
                dof = 1 if t_idx == 0 else 0
                x2tile = x2b[xslot]

                def body(sy, lo=lo, width=width, dof=dof, x2tile=x2tile,
                         pst=p1_ST, pxr=prev_x2read_VP, xslot=xslot,
                         pdbg=C["DBG"], pdbgg=C["DBGG"]):
                    sy.wait_ge(S["X2S"], pst)
                    if pdbg:
                        sy.wait_ge(S["DBG"], pdbg)
                    if pdbgg:
                        sy.wait_ge(S["DBGG"], pdbgg)
                    if pxr:
                        sy.wait_ge(S["VP"], pxr)
                    sy.dma_start(
                        out=x2tile[:, :].rearrange("p (c n) -> p c n", c=8
                                                   )[:, :, dof:dof + width],
                        in_=x2_h[:, lo:lo + width].rearrange("(c p) n -> p c n",
                                                             p=128)
                    ).then_inc(S["XF%d" % xslot], 16)
                SY(body)
                xl_LD = inc("XF%d" % xslot, 16)

                def body(v, x2tile=x2tile, xl_LD=xl_LD, xslot=xslot):
                    v.wait_ge(S["XF%d" % xslot], xl_LD)
                    for c in range(8):
                        v.tensor_copy(xbf2[:, c * FTE:(c + 1) * FTE],
                                      x2tile[:, c * FTE:(c + 1) * FTE])
                    v.fin(S["VP"], 1)
                VE(body)
                xb_VP = inc("VP", 1)

                sq_SPs2 = []
                sq_TPs2 = []
                for c in range(8):
                    def body(g, c=c, x2tile=x2tile, xl_LD=xl_LD, xslot=xslot,
                             ptp=(sq_TPs2[-2] if len(sq_TPs2) >= 2
                                  else prev_stats_TP)):
                        g.wait_ge(S["XF%d" % xslot], xl_LD)
                        if ptp:
                            g.wait_ge(S["TP"], ptp)
                        g.tensor_tensor(xsq2[:, (c % 2) * FTE:(c % 2 + 1) * FTE],
                                        x2tile[:, c * FTE:(c + 1) * FTE],
                                        x2tile[:, c * FTE:(c + 1) * FTE],
                                        OP.mult).then_inc(S["PC"], 1)
                    GP(body)
                    sq_SPs2.append(inc("PC", 1))

                    def body(t, c=c, xb_VP=xb_VP, sp=sq_SPs2[-1]):
                        t.wait_ge(S["VP"], xb_VP)
                        t.wait_ge(S["PC"], sp)
                        t.matmul(bank(5, 0, FTE)[0:1, :], onescol_bf[:, :],
                                 xbf2[:, c * FTE:(c + 1) * FTE],
                                 start=(c == 0), stop=(c == 7))
                        t.matmul(bank(6, 0, FTE)[0:1, :], onescol_bf[:, :],
                                 xsq2[:, (c % 2) * FTE:(c % 2 + 1) * FTE],
                                 start=(c == 0), stop=(c == 7))
                        t.fin(S["TP"], 1)
                    TE(body)
                    sq_TPs2.append(inc("TP", 1))
                st_TP = sq_TPs2[-1]
                q1_stats_cell[0] = st_TP

                def body(v, st_TP=st_TP):
                    v.wait_ge(S["TP"], st_TP)
                    v.tensor_scalar(rowa[0:1, 0:FTE], bank(5, 0, FTE)[0:1, :],
                                    1.0 / D, None, OP.mult)
                    v.tensor_scalar(rowb[0:1, 0:FTE], bank(6, 0, FTE)[0:1, :],
                                    1.0 / D, None, OP.mult)
                    v.fin(S["VP"], 1)
                VE(body)
                q1 = inc("VP", 1)

                def body(v, q1=q1):
                    v.wait_ge(S["VP"], q1)
                    v.tensor_tensor(rowc[0:1, 0:FTE], rowa[0:1, 0:FTE],
                                    rowa[0:1, 0:FTE], OP.mult).then_inc(S["VP"], 1)
                VE(body)
                q2_ = inc("VP", 1)

                def body(v, q2_=q2_):
                    v.wait_ge(S["VP"], q2_)
                    v.tensor_tensor(rowb[0:1, 0:FTE], rowb[0:1, 0:FTE],
                                    rowc[0:1, 0:FTE], OP.subtract
                                    ).then_inc(S["VP"], 1)
                VE(body)
                q3 = inc("VP", 1)

                def body(sca, q3=q3, q4=C["SP"] + 1):
                    sca.wait_ge(S["VP"], q3)
                    sca.activation(rowc[0:1, 0:FTE], rowb[0:1, 0:FTE], AF.Ln,
                                   bias=eps_col[0:1, :]).then_inc(S["SP"], 1)
                    sca.wait_ge(S["SP"], q4)
                    sca.activation(rowb[0:1, 0:FTE], rowc[0:1, 0:FTE], AF.Exp,
                                   scale=-0.5).then_inc(S["SP"], 1)
                SE(body)
                q4 = inc("SP", 1)
                rstd2_SP = inc("SP", 1)

                def body(v, xb_VP=xb_VP):
                    v.wait_ge(S["VP"], xb_VP)
                    for k in range(4):
                        v.tensor_tensor(ca.bitcast(BF16)[:, k * FTE:(k + 1) * FTE],
                                        xbf2[:, 2 * k * FTE:(2 * k + 1) * FTE],
                                        xbf2[:, (2 * k + 1) * FTE:(2 * k + 2) * FTE],
                                        OP.max)
                        v.tensor_tensor(ca.bitcast(BF16)[:, (4 + k) * FTE:(5 + k) * FTE],
                                        xbf2[:, 2 * k * FTE:(2 * k + 1) * FTE],
                                        xbf2[:, (2 * k + 1) * FTE:(2 * k + 2) * FTE],
                                        OP.min)
                    v.fin(S["VP"], 1)
                VE(body)
                tq1 = inc("VP", 1)

                def body(v, tq1=tq1):
                    v.wait_ge(S["VP"], tq1)
                    A = ca.bitcast(BF16)
                    for k in range(2):
                        v.tensor_tensor(A[:, (8 + k) * FTE:(9 + k) * FTE],
                                        A[:, 2 * k * FTE:(2 * k + 1) * FTE],
                                        A[:, (2 * k + 1) * FTE:(2 * k + 2) * FTE],
                                        OP.max)
                        v.tensor_tensor(A[:, (10 + k) * FTE:(11 + k) * FTE],
                                        A[:, (4 + 2 * k) * FTE:(5 + 2 * k) * FTE],
                                        A[:, (5 + 2 * k) * FTE:(6 + 2 * k) * FTE],
                                        OP.min)
                    v.fin(S["VP"], 1)
                VE(body)
                tq2 = inc("VP", 1)

                def body(v, tq2=tq2):
                    v.wait_ge(S["VP"], tq2)
                    A = ca.bitcast(BF16)
                    v.tensor_tensor(tmax[:, :], A[:, 8 * FTE:9 * FTE],
                                    A[:, 9 * FTE:10 * FTE], OP.max)
                    v.tensor_tensor(tmin[:, :], A[:, 10 * FTE:11 * FTE],
                                    A[:, 11 * FTE:12 * FTE], OP.min)
                    v.fin(S["VP"], 1)
                VE(body)
                tree_VP = inc("VP", 1)

                def body(t, tree_VP=tree_VP):
                    t.wait_ge(S["VP"], tree_VP)
                    t.matmul(bank(0, 0, 128), tmax[:, 0:128], ident_bf[:, :],
                             start=True, stop=True)
                    t.matmul(bank(1, 0, 128), tmax[:, 128:256], ident_bf[:, :],
                             start=True, stop=True)
                    t.matmul(bank(2, 0, 128)[0:2, :], tmax[:, 256:258],
                             ident_bf[:, :], start=True, stop=True)
                    t.matmul(bank(0, 128, 256), tmin[:, 0:128], ident_bf[:, :],
                             start=True, stop=True, skip_group_check=True)
                    t.matmul(bank(1, 128, 256), tmin[:, 128:256], ident_bf[:, :],
                             start=True, stop=True, skip_group_check=True)
                    t.matmul(bank(2, 128, 256)[0:2, :], tmin[:, 256:258],
                             ident_bf[:, :], start=True, stop=True,
                             skip_group_check=True)
                    t.fin(S["TP"], 1)
                TE(body)
                tr_TP = inc("TP", 1)

                def body(v, tr_TP=tr_TP):
                    v.wait_ge(S["TP"], tr_TP)
                    v.tensor_reduce(colred[:, 0:1], bank(0, 0, 128), AX.X, OP.max)
                    v.tensor_reduce(colred[:, 1:2], bank(1, 0, 128), AX.X, OP.max)
                    v.tensor_reduce(colred[0:2, 2:3], bank(2, 0, 128)[0:2, :], AX.X,
                                    OP.max)
                    v.tensor_reduce(colred[:, 3:4], bank(0, 128, 256), AX.X, OP.min)
                    v.tensor_reduce(colred[:, 4:5], bank(1, 128, 256), AX.X, OP.min)
                    v.tensor_reduce(colred[0:2, 5:6], bank(2, 128, 256)[0:2, :],
                                    AX.X, OP.min)
                    v.fin(S["VP"], 1)
                VE(body)
                red_VP = inc("VP", 1)

                def body(t, red_VP=red_VP):
                    t.wait_ge(S["VP"], red_VP)
                    t.matmul(bank(3, 0, 128)[0:1, :], colred[:, 0:1], ident_f[:, :],
                             start=True, stop=True)
                    t.matmul(bank(3, 128, 256)[0:1, :], colred[:, 1:2],
                             ident_f[:, :], start=True, stop=True,
                             skip_group_check=True)
                    t.matmul(bank(3, 256, 258)[0:1, :], colred[0:2, 2:3],
                             ident_f[0:2, 0:2], start=True, stop=True,
                             skip_group_check=True)
                    t.matmul(bank(4, 0, 128)[0:1, :], colred[:, 3:4], ident_f[:, :],
                             start=True, stop=True)
                    t.matmul(bank(4, 128, 256)[0:1, :], colred[:, 4:5],
                             ident_f[:, :], start=True, stop=True,
                             skip_group_check=True)
                    t.matmul(bank(4, 256, 258)[0:1, :], colred[0:2, 5:6],
                             ident_f[0:2, 0:2], start=True, stop=True,
                             skip_group_check=True)
                    t.fin(S["TP"], 1)
                TE(body)
                c2r_TP = inc("TP", 1)

                def body(v, c2r_TP=c2r_TP, rstd2_SP=rstd2_SP):
                    v.wait_ge(S["TP"], c2r_TP)
                    v.wait_ge(S["SP"], rstd2_SP)
                    v.tensor_copy(rowc[0:1, 0:FTE], bank(3, 0, FTE)[0:1, :])
                    v.tensor_copy(rowc[0:1, 512:512 + FTE], bank(4, 0, FTE)[0:1, :])
                    v.fin(S["VP"], 1)
                VE(body)
                asm_VP = inc("VP", 1)

                def body(v, asm_VP=asm_VP, rstd2_SP=rstd2_SP):
                    v.wait_ge(S["VP"], asm_VP)
                    v.wait_ge(S["SP"], rstd2_SP)
                    v.tensor_tensor(rowc[0:1, 0:FTE], rowc[0:1, 0:FTE],
                                    rowa[0:1, 0:FTE], OP.subtract)
                    v.fin(S["VP"], 1)
                VE(body)
                m1 = inc("VP", 1)

                def body(v, m1=m1):
                    v.wait_ge(S["VP"], m1)
                    v.tensor_tensor(rowc[0:1, 512:512 + FTE], rowa[0:1, 0:FTE],
                                    rowc[0:1, 512:512 + FTE], OP.subtract
                                    ).then_inc(S["VP"], 1)
                VE(body)
                m2_ = inc("VP", 1)

                def body(v, m2_=m2_):
                    v.wait_ge(S["VP"], m2_)
                    v.tensor_tensor(rowc[0:1, 0:FTE], rowc[0:1, 0:FTE],
                                    rowc[0:1, 512:512 + FTE], OP.max
                                    ).then_inc(S["VP"], 1)
                VE(body)
                m3 = inc("VP", 1)

                def body(v, m3=m3):
                    v.wait_ge(S["VP"], m3)
                    v.tensor_tensor(rowc[0:1, 0:FTE], rowc[0:1, 0:FTE],
                                    rowb[0:1, 0:FTE], OP.mult).then_inc(S["VP"], 1)
                VE(body)
                m4 = inc("VP", 1)

                def body(v, m4=m4):
                    v.wait_ge(S["VP"], m4)
                    v.tensor_scalar(rowc[0:1, 0:FTE], rowc[0:1, 0:FTE], 1e-5, None,
                                    OP.max).then_inc(S["VP"], 1)
                VE(body)
                m5 = inc("VP", 1)

                def body(v, m5=m5):
                    v.wait_ge(S["VP"], m5)
                    v.tensor_scalar(rowc[0:1, 512:512 + FTE], rowc[0:1, 0:FTE],
                                    1.0 / 127.0, None, OP.mult).then_inc(S["VP"], 1)
                VE(body)
                m5b = inc("VP", 1)

                def body(v, m5b=m5b):
                    v.wait_ge(S["VP"], m5b)
                    v.reciprocal(rowc[0:1, 0:FTE], rowc[0:1, 0:FTE]
                                 ).then_inc(S["VP"], 1)
                VE(body)
                m6 = inc("VP", 1)

                def body(v, m6=m6):
                    v.wait_ge(S["VP"], m6)
                    v.tensor_scalar(rowc[0:1, 0:FTE], rowc[0:1, 0:FTE], 127.0, None,
                                    OP.mult).then_inc(S["VP"], 1)
                VE(body)
                m7 = inc("VP", 1)

                def body(v, m7=m7):
                    v.wait_ge(S["VP"], m7)
                    v.tensor_tensor(rowb[0:1, 0:FTE], rowb[0:1, 0:FTE],
                                    rowc[0:1, 0:FTE], OP.mult).then_inc(S["VP"], 1)
                VE(body)
                m8 = inc("VP", 1)

                def body(v, m8=m8):
                    v.wait_ge(S["VP"], m8)
                    v.scalar_tensor_tensor(rowa[0:1, 0:FTE], rowa[0:1, 0:FTE], -1.0,
                                           rowb[0:1, 0:FTE], OP.mult, OP.mult
                                           ).then_inc(S["VP"], 1)
                VE(body)
                rows2_VP = inc("VP", 1)

                if edge_halo == "L":
                    def body(v, rows2_VP=rows2_VP, halo_VP=halo_VP):
                        v.wait_ge(S["VP"], max(rows2_VP, halo_VP))
                        v.tensor_copy(rowc[0:1, 512:513], halo_rcL[0:1, 0:1]
                                      ).then_inc(S["VP"], 1)
                    VE(body)
                    rows2_VP = inc("VP", 1)
                elif edge_halo == "R":
                    def body(v, rows2_VP=rows2_VP, halo_VP=halo_VP):
                        v.wait_ge(S["VP"], max(rows2_VP, halo_VP))
                        v.tensor_copy(rowc[0:1, 512 + 257:512 + 258],
                                      halo_rcR[0:1, 0:1]).then_inc(S["VP"], 1)
                    VE(body)
                    rows2_VP = inc("VP", 1)

                def body(t, rows2_VP=rows2_VP):
                    t.wait_ge(S["VP"], rows2_VP)
                    t.matmul(bank(5, 0, FTE), onesrow_f[:, :], rowb[0:1, 0:FTE],
                             start=True, stop=True)
                    t.matmul(bank(6, 0, FTE), onesrow_f[:, :], rowa[0:1, 0:FTE],
                             start=True, stop=True)
                    t.matmul(bank(7, 0, FTE), onesrow_f[:, :],
                             rowc[0:1, 512:512 + FTE], start=True, stop=True)
                    t.fin(S["TP"], 1)
                TE(body)
                qb_TP = inc("TP", 1)

                def body(v, qb_TP=qb_TP):
                    v.wait_ge(S["TP"], qb_TP)
                    v.tensor_copy(alb[:, :], bank(5, 0, FTE))
                    v.tensor_copy(beb[:, :], bank(6, 0, FTE))
                    v.tensor_copy(r1b[:, :], bank(7, 0, FTE))
                    v.fin(S["VP"], 1)
                VE(body)
                qbc_VP = inc("VP", 1)

                xq_wave = []
                for c in range(8):
                    def body(v, c=c, qbc_VP=qbc_VP, x2tile=x2tile):
                        v.wait_ge(S["VP"], qbc_VP)
                        v.tensor_tensor(scq[:, :], x2tile[:, c * FTE:(c + 1) * FTE],
                                        alb[:, :], OP.mult).then_inc(S["VP"], 1)
                    VE(body)
                    qq1 = inc("VP", 1)

                    def body(v, qq1=qq1):
                        v.wait_ge(S["VP"], qq1)
                        v.tensor_tensor(scq2[:, :], scq[:, :], beb[:, :], OP.add
                                        ).then_inc(S["VP"], 1)
                    VE(body)
                    qq2 = inc("VP", 1)

                    def body(v, c=c, qq2=qq2):
                        v.wait_ge(S["VP"], qq2)
                        v.tensor_scalar(xq[:, c * FTE:(c + 1) * FTE], scq2[:, :],
                                        MAGIC, -MAGIC, OP.add, OP.add
                                        ).then_inc(S["VP"], 1)
                    VE(body)
                    xq_wave.append(inc("VP", 1))
                xq_VP = xq_wave[-1]

                if payload_dst is not None:
                    col, dstc = payload_dst

                    def body(v, xq_VP=xq_VP, col=col, dstc=dstc):
                        v.wait_ge(S["VP"], xq_VP)
                        v.tensor_copy(
                            stage2[:, dstc:dstc + 8],
                            xq[:, :].rearrange("p (c n) -> p c n", c=8
                                               )[:, :, col:col + 1
                                                 ].squeeze(2))
                        v.tensor_copy(stage2[0:1, dstc + 8:dstc + 9],
                                      rowc[0:1, 512 + col:512 + col + 1])
                        v.fin(S["VP"], 1)
                    VE(body)
                    xq_VP = inc("VP", 1)

                if edge_halo == "L":
                    def body(v, xq_VP=xq_VP, halo_VP=halo_VP):
                        v.wait_ge(S["VP"], max(xq_VP, halo_VP))
                        v.tensor_copy(
                            xq[:, :].rearrange("p (c n) -> p c n", c=8)[:, :, 0:1],
                            halo_xqL[:, :].unsqueeze(2)
                        ).then_inc(S["VP"], 1)
                    VE(body)
                    xq_VP = inc("VP", 1)
                elif edge_halo == "R":
                    def body(v, xq_VP=xq_VP, halo_VP=halo_VP):
                        v.wait_ge(S["VP"], max(xq_VP, halo_VP))
                        v.tensor_copy(
                            xq[:, :].rearrange("p (c n) -> p c n", c=8
                                               )[:, :, 257:258],
                            halo_xqR[:, :].unsqueeze(2)
                        ).then_inc(S["VP"], 1)
                    VE(body)
                    xq_VP = inc("VP", 1)

                def body(v, xq_VP=xq_VP):
                    v.wait_ge(S["VP"], xq_VP)
                    for c in range(8):
                        v.tensor_tensor(xq[:, c * FTE:(c + 1) * FTE],
                                        xq[:, c * FTE:(c + 1) * FTE], r1b[:, :],
                                        OP.mult)
                    v.fin(S["VP"], 1)
                VE(body)
                return inc("VP", 1)

            pay0_VP = emit_quant1(0, 0, None, 0, (1, 0), x2z_VP)
            pay7_VP = emit_quant1(7, 1, None, 0, (256, 10), x2z_VP,
                                  prev_stats_TP=q1_stats_cell[0])

            def body(g, pay7_VP=pay7_VP):
                g.wait_ge(S["VP"], pay7_VP)
                g.dma_start(out=cc2_in, in_=stage2[:, :]).then_inc(S["AGD"], 16)
            GP(body)
            ag2_GD = inc("AGD", 16)

            def body(g, ag2_GD=ag2_GD):
                g.wait_ge(S["AGD"], ag2_GD)
                g.collective_compute(
                    "AllGather", OP.bypass, replica_groups=GROUPS,
                    ins=[cc2_in.opt()], outs=[cc2_out.opt()],
                ).then_inc(S["CC"], 1)
            GP(body)
            ag2_CC = inc("CC", 1)

            tern2_VP = emit_tern(1, w2t_h, tern2, sw2s, 0, init_prev=tern1_VP)

            def body(g, ag2_CC=ag2_CC):
                g.wait_ge(S["CC"], ag2_CC)
                g.dma_start(out=stage4[:, :].rearrange("p (r c) -> p r c", r=4),
                            in_=cc2_out.rearrange("(r p) c -> p r c", p=128)
                            ).then_inc(S["AGD"], 16)
            GP(body)
            ag2r_GD = inc("AGD", 16)

            halo_specs = [
                (halo_xqL, 10, 8, hmL), (halo_rcL, 18, 1, hmL),
                (halo_xqR, 0, 8, hmR), (halo_rcR, 8, 1, hmR),
            ]
            last = 0
            for dst, off, w, msk in halo_specs:
                for r in range(4):
                    def body(v, dst=dst, off=off, w=w, msk=msk, r=r, last=last,
                             ag2r_GD=ag2r_GD):
                        if last:
                            v.wait_ge(S["VP"], last)
                        else:
                            v.wait_ge(S["AGD"], ag2r_GD)
                        if r == 0:
                            v.tensor_scalar(dst[:, :],
                                            stage4[:, off:off + w],
                                            msk[:, 0:1], None, OP.mult
                                            ).then_inc(S["VP"], 1)
                        else:
                            v.scalar_tensor_tensor(dst[:, :],
                                                   stage4[:, r * 20 + off:
                                                          r * 20 + off + w],
                                                   msk[:, r:r + 1], dst[:, :],
                                                   OP.mult, OP.add
                                                   ).then_inc(S["VP"], 1)
                    VE(body)
                    last = inc("VP", 1)
            halo_VP = C["VP"]

            prev_x2read_VP = x2z_VP
            prev_out_ST = [0, 0]
            for t_idx in range(NFT):
                eh = "L" if t_idx == 0 else ("R" if t_idx == NFT - 1 else None)
                xqs_VP = emit_quant1(t_idx, t_idx % 2, eh, halo_VP, None,
                                     prev_x2read_VP,
                                     prev_stats_TP=q1_stats_cell[0])

                evac_SPs = []
                for grp in range(8):
                    def body(t, grp=grp, xqs_VP=xqs_VP, tern1_VP=tern1_VP,
                             prevsp=(evac_SPs[-1] if evac_SPs else 0),
                             wsc_VP=wsc_VP):
                        t.wait_ge(S["VP"], max(xqs_VP, tern1_VP))
                        if prevsp:
                            t.wait_ge(S["SP"], prevsp)
                        for j in range(4):
                            hc = grp * 4 + j
                            for k in range(8):
                                t.matmul(
                                    bank(j, 0, FTE),
                                    tern1[:, k * HID + hc * 128: k * HID + (hc + 1) * 128],
                                    xq[:, k * FTE:(k + 1) * FTE],
                                    start=(k == 0), stop=(k == 7))
                        t.fin(S["TP"], 1)
                    TE(body)
                    w1_TP = inc("TP", 1)

                    def body(sca, grp=grp, w1_TP=w1_TP):
                        sca.wait_ge(S["TP"], w1_TP)
                        src = pb[:, 0:2048].rearrange("p (b n) -> p b n", b=4
                                                      )[:, :, 0:FTE]
                        dst = h_y[:, grp * 4 * FTE:(grp + 1) * 4 * FTE
                                  ].rearrange("p (b n) -> p b n", b=4)
                        sca.activation(dst, src, AF.Identity, scale=mean1s[:, :])
                        sca.fin(S["SP"], 1)
                    SE(body)
                    evac_SPs.append(inc("SP", 1))
                h_SP = evac_SPs[-1]

                def body(v, h_SP=h_SP):
                    v.wait_ge(S["SP"], h_SP)
                    for hc in range(32):
                        v.tensor_scalar(ca[:, hc * FTE:(hc + 1) * FTE],
                                        h_y[:, hc * FTE:(hc + 1) * FTE],
                                        cw_pc[0][:, hc:hc + 1], None, OP.mult)
                    v.fin(S["VP"], 1)
                VE(body)
                tapA_VP = inc("VP", 1)

                def body(v, tapA_VP=tapA_VP):
                    v.wait_ge(S["VP"], tapA_VP)
                    for hc in range(32):
                        v.tensor_scalar(cc2b[:, hc * FTE:(hc + 1) * FTE],
                                        h_y[:, hc * FTE:(hc + 1) * FTE],
                                        cw_pc[2][:, hc:hc + 1], None, OP.mult)
                    v.fin(S["VP"], 1)
                VE(body)
                tapC_VP = inc("VP", 1)

                def body(v, tapC_VP=tapC_VP):
                    v.wait_ge(S["VP"], tapC_VP)
                    for hc in range(32):
                        v.tensor_scalar(ce[:, hc * FTE:(hc + 1) * FTE],
                                        h_y[:, hc * FTE:(hc + 1) * FTE],
                                        cw_pc[1][:, hc:hc + 1], None, OP.mult)
                    v.fin(S["VP"], 1)
                VE(body)
                tapE_VP = inc("VP", 1)

                def body(v, tapE_VP=tapE_VP):
                    v.wait_ge(S["VP"], tapE_VP)
                    A3 = ca[:, :].rearrange("p (c n) -> p c n", c=32)
                    C3 = cc2b[:, :].rearrange("p (c n) -> p c n", c=32)
                    v.tensor_tensor(A3[:, :, 0:FT], A3[:, :, 0:FT], C3[:, :, 2:FT + 2],
                                    OP.add).then_inc(S["VP"], 1)
                VE(body)
                t1_VP = inc("VP", 1)

                def body(v, t1_VP=t1_VP):
                    v.wait_ge(S["VP"], t1_VP)
                    A3 = ca[:, :].rearrange("p (c n) -> p c n", c=32)
                    C3 = cc2b[:, :].rearrange("p (c n) -> p c n", c=32)
                    E3 = ce[:, :].rearrange("p (c n) -> p c n", c=32)
                    v.tensor_tensor(C3[:, :, 0:FT], A3[:, :, 0:FT], E3[:, :, 1:FT + 1],
                                    OP.add).then_inc(S["VP"], 1)
                VE(body)
                yp_VP = inc("VP", 1)

                def body(sca, yp_VP=yp_VP):
                    sca.wait_ge(S["VP"], yp_VP)
                    for hc in range(32):
                        sca.activation(h_y[:, hc * FTE + 1: hc * FTE + 1 + FT],
                                       cc2b[:, hc * FTE: hc * FTE + FT], AF.Silu,
                                       bias=cb_pc[:, hc:hc + 1])
                    sca.fin(S["SP"], 1)
                SE(body)
                y_SP = inc("SP", 1)

                if DEBUG and "ffn" in DEBUG_SET and t_idx == 0:
                    def body(sy, y_SP=y_SP):
                        sy.wait_ge(S["SP"], y_SP)
                        sy.dma_start(out=dbg["y0"], in_=h_y[:, :]).then_inc(S["DBG"], 16)
                    SY(body)
                    inc("DBG", 16)

                def body(v, y_SP=y_SP):
                    v.wait_ge(S["SP"], y_SP)
                    A = ca.bitcast(BF16)
                    B = cc2b.bitcast(BF16)
                    for k in range(16):
                        a0 = h_y[:, 2 * k * FTE + 1: 2 * k * FTE + 1 + FT]
                        a1 = h_y[:, (2 * k + 1) * FTE + 1: (2 * k + 1) * FTE + 1 + FT]
                        v.tensor_tensor(A[:, k * FT:(k + 1) * FT], a0, a1, OP.max)
                        v.tensor_tensor(B[:, k * FT:(k + 1) * FT], a0, a1, OP.min)
                    v.fin(S["VP"], 1)
                VE(body)
                ty = inc("VP", 1)
                width = 16
                base = 0
                while width > 1:
                    nw = width // 2
                    def body(v, ty=ty, base=base, nw=nw, width=width):
                        v.wait_ge(S["VP"], ty)
                        A = ca.bitcast(BF16)
                        B = cc2b.bitcast(BF16)
                        nbase = base + width
                        for k in range(nw):
                            v.tensor_tensor(A[:, (nbase + k) * FT:(nbase + k + 1) * FT],
                                            A[:, (base + 2 * k) * FT:(base + 2 * k + 1) * FT],
                                            A[:, (base + 2 * k + 1) * FT:(base + 2 * k + 2) * FT],
                                            OP.max)
                            v.tensor_tensor(B[:, (nbase + k) * FT:(nbase + k + 1) * FT],
                                            B[:, (base + 2 * k) * FT:(base + 2 * k + 1) * FT],
                                            B[:, (base + 2 * k + 1) * FT:(base + 2 * k + 2) * FT],
                                            OP.min)
                        v.fin(S["VP"], 1)
                    VE(body)
                    ty = inc("VP", 1)
                    base += width
                    width = nw

                def body(v, ty=ty, base=base):
                    v.wait_ge(S["VP"], ty)
                    A = ca.bitcast(BF16)
                    B = cc2b.bitcast(BF16)
                    v.tensor_copy(tmax[:, 0:FT], A[:, base * FT:(base + 1) * FT])
                    v.tensor_copy(tmin[:, 0:FT], B[:, base * FT:(base + 1) * FT])
                    v.fin(S["VP"], 1)
                VE(body)
                ytree_VP = inc("VP", 1)

                def body(t, ytree_VP=ytree_VP):
                    t.wait_ge(S["VP"], ytree_VP)
                    t.matmul(bank(0, 0, 128), tmax[:, 0:128], ident_bf[:, :],
                             start=True, stop=True)
                    t.matmul(bank(1, 0, 128), tmax[:, 128:256], ident_bf[:, :],
                             start=True, stop=True)
                    t.matmul(bank(0, 128, 256), tmin[:, 0:128], ident_bf[:, :],
                             start=True, stop=True, skip_group_check=True)
                    t.matmul(bank(1, 128, 256), tmin[:, 128:256], ident_bf[:, :],
                             start=True, stop=True, skip_group_check=True)
                    t.fin(S["TP"], 1)
                TE(body)
                ytr_TP = inc("TP", 1)

                def body(v, ytr_TP=ytr_TP):
                    v.wait_ge(S["TP"], ytr_TP)
                    v.tensor_reduce(colred[:, 0:1], bank(0, 0, 128), AX.X, OP.max)
                    v.tensor_reduce(colred[:, 1:2], bank(1, 0, 128), AX.X, OP.max)
                    v.tensor_reduce(colred[:, 3:4], bank(0, 128, 256), AX.X, OP.min)
                    v.tensor_reduce(colred[:, 4:5], bank(1, 128, 256), AX.X, OP.min)
                    v.fin(S["VP"], 1)
                VE(body)
                yred_VP = inc("VP", 1)

                def body(t, yred_VP=yred_VP):
                    t.wait_ge(S["VP"], yred_VP)
                    t.matmul(bank(3, 0, 128)[0:1, :], colred[:, 0:1], ident_f[:, :],
                             start=True, stop=True)
                    t.matmul(bank(3, 128, 256)[0:1, :], colred[:, 1:2],
                             ident_f[:, :], start=True, stop=True,
                             skip_group_check=True)
                    t.matmul(bank(4, 0, 128)[0:1, :], colred[:, 3:4], ident_f[:, :],
                             start=True, stop=True)
                    t.matmul(bank(4, 128, 256)[0:1, :], colred[:, 4:5],
                             ident_f[:, :], start=True, stop=True,
                             skip_group_check=True)
                    t.fin(S["TP"], 1)
                TE(body)
                yc2r_TP = inc("TP", 1)

                def body(v, yc2r_TP=yc2r_TP):
                    v.wait_ge(S["TP"], yc2r_TP)
                    v.tensor_copy(rowa[0:1, 0:FT], bank(3, 0, FT)[0:1, :])
                    v.tensor_copy(rowb[0:1, 0:FT], bank(4, 0, FT)[0:1, :])
                    v.fin(S["VP"], 1)
                VE(body)
                y1 = inc("VP", 1)

                def body(v, y1=y1):
                    v.wait_ge(S["VP"], y1)
                    v.scalar_tensor_tensor(rowa[0:1, 0:FT], rowb[0:1, 0:FT], -1.0,
                                           rowa[0:1, 0:FT], OP.mult, OP.max
                                           ).then_inc(S["VP"], 1)
                VE(body)
                y2 = inc("VP", 1)

                def body(v, y2=y2):
                    v.wait_ge(S["VP"], y2)
                    v.tensor_scalar(rowa[0:1, 0:FT], rowa[0:1, 0:FT], 1e-5, None,
                                    OP.max).then_inc(S["VP"], 1)
                VE(body)
                y3 = inc("VP", 1)

                def body(v, y3=y3):
                    v.wait_ge(S["VP"], y3)
                    v.tensor_scalar(rowb[0:1, 0:FT], rowa[0:1, 0:FT], 1.0 / 127.0,
                                    None, OP.mult).then_inc(S["VP"], 1)
                VE(body)
                y3b = inc("VP", 1)

                def body(v, y3b=y3b):
                    v.wait_ge(S["VP"], y3b)
                    v.reciprocal(rowa[0:1, 0:FT], rowa[0:1, 0:FT]
                                 ).then_inc(S["VP"], 1)
                VE(body)
                y4 = inc("VP", 1)

                def body(v, y4=y4):
                    v.wait_ge(S["VP"], y4)
                    v.tensor_scalar(rowa[0:1, 0:FT], rowa[0:1, 0:FT], 127.0, None,
                                    OP.mult).then_inc(S["VP"], 1)
                VE(body)
                y5 = inc("VP", 1)

                def body(t, y5=y5):
                    t.wait_ge(S["VP"], y5)
                    t.matmul(bank(5, 0, FT), onesrow_f[:, :], rowa[0:1, 0:FT],
                             start=True, stop=True)
                    t.matmul(bank(6, 0, FT), onesrow_f[:, :], rowb[0:1, 0:FT],
                             start=True, stop=True)
                    t.fin(S["TP"], 1)
                TE(body)
                s2bc_TP = inc("TP", 1)

                def body(v, s2bc_TP=s2bc_TP):
                    v.wait_ge(S["TP"], s2bc_TP)
                    v.tensor_copy(s2b[:, :], bank(5, 0, FT))
                    v.tensor_copy(r2b[:, :], bank(6, 0, FT))
                    v.fin(S["VP"], 1)
                VE(body)
                s2bc_VP = inc("VP", 1)

                xq2_wave = []
                for hc in range(32):
                    def body(v, hc=hc, s2bc_VP=s2bc_VP,
                             prevq=(xq2_wave[-1] if xq2_wave else 0)):
                        v.wait_ge(S["VP"], s2bc_VP)
                        if prevq:
                            v.wait_ge(S["VP"], prevq)
                        v.tensor_tensor(scq[:, 0:FT],
                                        h_y[:, hc * FTE + 1: hc * FTE + 1 + FT],
                                        s2b[:, :], OP.mult).then_inc(S["VP"], 1)
                    VE(body)
                    u1 = inc("VP", 1)

                    def body(v, hc=hc, u1=u1):
                        v.wait_ge(S["VP"], u1)
                        v.tensor_scalar(ce[:, hc * FTE: hc * FTE + FT],
                                        scq[:, 0:FT], MAGIC, -MAGIC,
                                        OP.add, OP.add).then_inc(S["VP"], 1)
                    VE(body)
                    xq2_wave.append(inc("VP", 1))
                xq2_VP = xq2_wave[-1]

                if DEBUG and "ffn" in DEBUG_SET and t_idx == 0:
                    def body(g, xq2_VP=xq2_VP):
                        g.wait_ge(S["VP"], xq2_VP)
                        g.dma_start(out=dbg["h0"], in_=h_y[:, :]).then_inc(S["DBGG"], 16)
                        g.dma_start(out=dbg["xq2_0"], in_=ce[:, :]).then_inc(S["DBGG"], 16)
                        g.dma_start(out=dbg["xq0"], in_=xq[:, :]).then_inc(S["DBGG"], 16)
                    GP(body)
                    inc("DBGG", 48)

                w2ev_VPs = []
                for dc in range(8):
                    wbk = 0 if dc % 2 == 0 else 1

                    def body(t, dc=dc, wbk=wbk, xq2_VP=xq2_VP, tern2_VP=tern2_VP,
                             prevev=(w2ev_VPs[-2] if len(w2ev_VPs) >= 2 else 0)):
                        t.wait_ge(S["VP"], max(xq2_VP, tern2_VP))
                        if prevev:
                            t.wait_ge(S["VP"], prevev)
                        for k in range(32):
                            t.matmul(bank(wbk, 0, FT),
                                     tern2[:, k * D + dc * 128: k * D + (dc + 1) * 128],
                                     ce[:, k * FTE: k * FTE + FT],
                                     start=(k == 0), stop=(k == 31))
                        t.fin(S["TP"], 1)
                    TE(body)
                    w2_TP = inc("TP", 1)

                    def body(v, dc=dc, wbk=wbk, w2_TP=w2_TP, t_idx=t_idx,
                             pwe=(w2ev_VPs[-1] if w2ev_VPs else 0)):
                        v.wait_ge(S["TP"], w2_TP)
                        if pwe:
                            v.wait_ge(S["VP"], pwe)
                        v.scalar_tensor_tensor(scq[:, 0:FT], bank(wbk, 0, FT),
                                               mean2s[:, :], r2b[:, :],
                                               OP.mult, OP.mult).then_inc(S["VP"], 1)
                    VE(body)
                    e1 = inc("VP", 1)

                    oslot = dc % 2

                    def body(v, dc=dc, e1=e1, t_idx=t_idx, oslot=oslot,
                             post=(prev_out_ST if isinstance(prev_out_ST, list)
                                   else [0, 0])):
                        v.wait_ge(S["VP"], e1)
                        if post[oslot]:
                            v.wait_ge(S["OS%d" % oslot], post[oslot])
                        x2tile = x2b[t_idx % 2]
                        v.tensor_tensor(out_sb[:, oslot * FT:(oslot + 1) * FT],
                                        scq[:, 0:FT],
                                        x2tile[:, dc * FTE + 1: dc * FTE + 1 + FT],
                                        OP.add).then_inc(S["VP"], 1)
                    VE(body)
                    ev = inc("VP", 1)
                    w2ev_VPs.append(ev)

                    def body(sy, dc=dc, t_idx=t_idx, oslot=oslot, ev=ev):
                        sy.wait_ge(S["VP"], ev)
                        sy.dma_start(
                            out=out_h[dc * 128:(dc + 1) * 128,
                                      t_idx * FT:(t_idx + 1) * FT],
                            in_=out_sb[:, oslot * FT:(oslot + 1) * FT]
                        ).then_inc(S["OS%d" % oslot], 16)
                    SY(body)
                    stc = inc("OS%d" % oslot, 16)
                    if not isinstance(prev_out_ST, list):
                        prev_out_ST = [0, 0]
                    prev_out_ST[oslot] = stc
                w2_done_VP = w2ev_VPs[-1]
                prev_x2read_VP = w2_done_VP

            fo0, fo1, fdbg, fdbgg = C["OS0"], C["OS1"], C["DBG"], C["DBGG"]

            def body(sy, fo0=fo0, fo1=fo1, fdbg=fdbg, fdbgg=fdbgg):
                sy.wait_ge(S["OS0"], fo0)
                sy.wait_ge(S["OS1"], fo1)
                if fdbg:
                    sy.wait_ge(S["DBG"], fdbg)
                if fdbgg:
                    sy.wait_ge(S["DBGG"], fdbgg)
            SY(body)
    return nc


def _prep_core_inputs(inputs, core):
    b, sc = core // 4, core % 4
    t0 = sc * T
    x = np.asarray(inputs["L0_attn_x"], np.float32)
    xs = np.ascontiguousarray(x[b, t0:t0 + T, :].T)
    d = {
        "x": xs,
        "wq": np.ascontiguousarray(np.asarray(inputs["L0_attn_Wq"], np.float32)),
        "wk": np.ascontiguousarray(np.asarray(inputs["L0_attn_Wk"], np.float32)),
        "wv": np.ascontiguousarray(np.asarray(inputs["L0_attn_Wv"], np.float32)),
        "wo": np.ascontiguousarray(np.asarray(inputs["L0_attn_Wo"], np.float32)),
        "g1w": np.ascontiguousarray(np.asarray(inputs["L0_attn_G1"], np.float32)),
        "g2w": np.ascontiguousarray(np.asarray(inputs["L0_attn_G2"], np.float32)),
        "w1t": np.ascontiguousarray(np.asarray(inputs["L0_ffn_w1"], np.float32).T),
        "w2t": np.ascontiguousarray(np.asarray(inputs["L0_ffn_w2"], np.float32).T),
        "theta_pc": np.ascontiguousarray(
            np.asarray(inputs["L0_attn_theta"], np.float32).reshape(-1)
            .reshape(8, 128).T),
        "gamma_row": np.asarray(inputs["L0_attn_gamma"], np.float32).reshape(1, D),
        "beta_row": np.asarray(inputs["L0_attn_beta"], np.float32).reshape(1, D),
        "cb_pc": np.ascontiguousarray(
            np.asarray(inputs["L0_ffn_conv_b"], np.float32).reshape(32, 128).T),
        "t0col": np.full((128, 1), float(t0), np.float32),
        "trimask": np.triu(np.ones((128, 128), np.float32)),
        "ident": np.eye(128, dtype=np.float32),
        "ones_col": np.ones((128, 1), np.float32),
        "ones_row": np.ones((1, 128), np.float32),
        "ones512": np.ones((1, 512), np.float32),
    }
    cw = np.asarray(inputs["L0_ffn_conv_w"], np.float32)
    for j in range(3):
        d[f"cw{j}_pc"] = np.ascontiguousarray(cw[:, 0, j].reshape(32, 128).T)
    pm = np.zeros((128, 4), np.float32)
    pm[:, :sc] = 1.0
    d["pmask"] = pm
    hl = np.zeros((128, 4), np.float32)
    if sc > 0:
        hl[:, sc - 1] = 1.0
    d["hmaskL"] = hl
    hr = np.zeros((128, 4), np.float32)
    if sc < 3:
        hr[:, sc + 1] = 1.0
    d["hmaskR"] = hr
    return d


_NC_CACHE = {}


def kernel(**inputs):
    key = ("nc", DEBUG, tuple(DEBUG_SET))
    if key not in _NC_CACHE:
        _NC_CACHE[key] = build_nc()
    nc = _NC_CACHE[key]
    in_maps = [_prep_core_inputs(inputs, c) for c in range(8)]
    res = run_bass_kernel_spmd(nc, in_maps, core_ids=list(range(8)))
    out = np.empty((2, 8192, 1024), np.float32)
    for c in range(8):
        b, sc = c // 4, c % 4
        out[b, sc * T:(sc + 1) * T, :] = res.results[c]["out"].T
    if DEBUG:
        kernel._dbg = res.results
    return out


if __name__ == "__main__":
    nc = build_nc()
    print("built ok")
